# version 06
# baseline (speedup 1.0000x reference)
import numpy as np
import ml_dtypes
import concourse.bass as bass
import concourse.mybir as mybir
from concourse.bass_utils import run_bass_kernel_spmd
from concourse.bass import IndirectOffsetOnAxis as IOA

F32 = mybir.dt.float32
BF16 = mybir.dt.bfloat16
I32 = mybir.dt.int32
U8 = mybir.dt.uint8
AF = mybir.ActivationFunctionType
OP = mybir.AluOpType
AX = mybir.AxisListType
BF = ml_dtypes.bfloat16

NCORES = 8
DM = 2048
T = 2048
DV = 128
L = 32
JB = 16
NMACRO = 4
MACRO_T = 512
ALL = ("t", "v", "a", "g", "s", "gd", "sd")


class Prog:

    def __init__(self, name="prog"):
        self.nc = bass.Bass("TRN2", target_bir_lowering=False)
        self.sems = {}
        self.cnt = {}
        for e in ALL:
            self.sems[e] = self.nc.alloc_semaphore("sem_" + e)
            self.cnt[e] = 0
        self._block = None
        self._eng_of = {"t": "tensor", "v": "vector", "a": "scalar",
                        "g": "gpsimd", "s": "sync"}

    def sem(self, name):
        if name not in self.sems:
            self.sems[name] = self.nc.alloc_semaphore("sem_" + name)
            self.cnt[name] = 0
        return self.sems[name]

    def block(self):
        self._block = self.nc.Block()
        return self._block

    def _sec(self, eng_name, fn):
        getattr(self._block, eng_name)(fn)

    def step(self, key, fn, dma=False, n_dma=1, waits="chain", sem_name=None):
        eng_name = self._eng_of[key]
        if waits == "chain":
            prev = getattr(self, "last", None)
            wait_list = [(self.sems[prev[0]], prev[1])] if prev else []
        else:
            seen = {}
            for w in waits:
                if w is None:
                    continue
                sk, cval = w
                seen[sk] = max(seen.get(sk, 0), cval)
            wait_list = [(self.sems[sk], cv) for sk, cv in seen.items()
                         if cv > 0]
        if not dma:
            self.cnt[key] += 1
            sem = self.sems[key]
            tok = (key, self.cnt[key])

            def body(eng):
                for s, c in wait_list:
                    eng.wait_ge(s, c)
                if key in ("v", "a"):
                    eng.drain()
                last = fn(eng)
                assert last is not None, "step fn must return its last instruction"
                last.then_inc(sem, 1)
            self._sec(eng_name, body)
            self.last = tok
            return tok
        else:
            qk = sem_name if sem_name is not None else (
                "gd" if key == "g" else "sd")
            self.sem(qk)
            self.cnt[qk] += 16 * n_dma
            sem = self.sems[qk]
            tok = (qk, self.cnt[qk])

            def body(eng):
                for s, c in wait_list:
                    eng.wait_ge(s, c)
                insts = fn(eng)
                assert isinstance(insts, (list, tuple)) and len(insts) == n_dma, \
                    f"expected {n_dma} dmas, got {insts}"
                for i in insts:
                    i.then_inc(sem, 16)
            self._sec(eng_name, body)
            self.last = tok
            return tok

    def final_wait(self):
        wait_list = [(self.sems[w], self.cnt[w]) for w in self.cnt
                     if self.cnt[w] > 0]

        def body(eng):
            for s, c in wait_list:
                eng.wait_ge(s, c)
        self._sec("gpsimd", body)


def sbuf(nc, name, shape, dt):
    return nc.alloc_sbuf_tensor(name, list(shape), dt)


def psum(nc, name, dt=F32, cols=None):
    if cols is None:
        cols = 512 if dt == F32 else 1024
    return nc.alloc_psum_tensor(name, [128, cols], dt)


def _consts_l0():
    ident = np.eye(128, dtype=np.float32)
    ii, jj = np.arange(128)[:, None], np.arange(128)[None, :]
    same = (ii // JB) == (jj // JB)
    rlt = (ii % JB) < (jj % JB)
    rle = (ii % JB) <= (jj % JB)
    mkd_neg = np.where(same & rlt, -1.0, 0.0).astype(np.float32)
    mkf_neg = np.where(same, -1.0, 0.0).astype(np.float32)
    mkg_pos = np.where(same & rle, 1.0, 0.0).astype(np.float32)
    mkf_pos = np.where(same, 1.0, 0.0).astype(np.float32)
    seqm = np.zeros((128, 8 * 128), np.float32)
    for s in range(8):
        seqm[JB * s:JB * (s + 1), 128 * s:128 * (s + 1)] = 1.0
    ones_col = np.ones((128, 1), np.float32)
    ones_row = np.ones((1, 128), np.float32)
    return dict(idf=ident, mkd_neg=mkd_neg, mkf_neg=mkf_neg, mkg_pos=mkg_pos,
                mkf_pos=mkf_pos, seqm=seqm, ones_col=ones_col,
                ones_row=ones_row)



def build_l0attn():
    P = Prog("l0attn")
    nc = P.nc
    xT = nc.dram_tensor("xT", [DM, T], F32, kind="ExternalInput").ap()
    wT = {nm: nc.dram_tensor(nm, [DM, 1024], F32, kind="ExternalInput").ap()
          for nm in ("wq", "wk", "wv", "wa", "wg")}
    wbT = nc.dram_tensor("wb", [DM, 8], F32, kind="ExternalInput").ap()
    convw = {nm: nc.dram_tensor("cw" + nm, [128, 32], F32,
                                kind="ExternalInput").ap() for nm in "qkv"}
    convb = {nm: nc.dram_tensor("cb" + nm, [128, 8], F32,
                                kind="ExternalInput").ap() for nm in "qkv"}
    bacol = nc.dram_tensor("bacol", [128, 8], F32, kind="ExternalInput").ap()
    bbrow = nc.dram_tensor("bbrow", [1, 8], F32, kind="ExternalInput").ap()
    lnwb = nc.dram_tensor("lnwb", [128, 2], F32, kind="ExternalInput").ap()
    cst = {k: nc.dram_tensor("c_" + k, list(v.shape), F32,
                             kind="ExternalInput").ap()
           for k, v in _consts_l0().items()}
    og = nc.dram_tensor("og", [1024, T], BF16, kind="ExternalOutput").ap()

    scr = {
        "q": nc.dram_tensor("s_q", [1024, T], BF16).ap(),
        "k": nc.dram_tensor("s_k", [1024, T], BF16).ap(),
        "v": nc.dram_tensor("s_v", [1024, T], F32).ap(),
        "a": nc.dram_tensor("s_a", [1024, T], F32).ap(),
        "g": nc.dram_tensor("s_g", [1024, T], BF16).ap(),
        "c": nc.dram_tensor("s_c", [1024, T], F32).ap(),
        "plrow": nc.dram_tensor("s_plrow", [8, 128], F32).ap(),
    }

    from contextlib import ExitStack
    es = ExitStack()
    def sb_a(name, shape, dt):
        return es.enter_context(nc.sbuf_tensor(name, list(shape), dt))
    wsb = {nm: sb_a("w_" + nm, (128, 16 * 1024), BF16)
           for nm in ("wq", "wk", "wv", "wa")}
    wgh = sb_a("wgh", (128, 16 * 128), BF16)
    wb_sb = sb_a("w_wb", (128, 16 * 8), BF16)
    xsb = sb_a("xsb", (128, 16 * MACRO_T), BF16)
    SEG = MACRO_T + 4
    raw = sb_a("raw", (128, 8 * SEG), F32)
    hist = sb_a("hist", (128, 3 * 24), F32)
    stage = sb_a("stage", (128, 8 * MACRO_T), F32)
    stage2 = sb_a("stage2", (128, 8 * MACRO_T), BF16)
    cw_sb = {nm: sb_a("cw_" + nm, (128, 32), F32) for nm in "qkv"}
    cb_sb = {nm: sb_a("cb_" + nm, (128, 8), F32) for nm in "qkv"}
    ba_sb = sb_a("ba_sb", (128, 8), F32)
    bb_sb = sb_a("bb_sb", (1, 8), F32)
    brow = sb_a("brow", (1, MACRO_T), F32)
    ones_row_a = sb_a("ones_row_a", (1, 128), F32)
    psP = es.enter_context(nc.psum_tensor("psP0", [128, 512], F32))
    psP2 = es.enter_context(nc.psum_tensor("psP2", [128, 512], F32))

    with P.block():
        def ld_small(g):
            out = []
            for nm in "qkv":
                out.append(g.dma_start(out=cw_sb[nm][:, :], in_=convw[nm]))
                out.append(g.dma_start(out=cb_sb[nm][:, :], in_=convb[nm]))
            out.append(g.dma_start(out=ba_sb[:, :], in_=bacol))
            out.append(g.dma_start(out=bb_sb[:, :], in_=bbrow))
            return out
        P.step("g", ld_small, dma=True, n_dma=8)

        def init_ones(v):
            return v.memset(ones_row_a[:, :], 1.0)
        P.step("v", init_ones)

        def ld_w(g):
            out = []
            for nm in ("wq", "wk", "wv", "wa"):
                src = wT[nm].rearrange("(ck p) c -> p ck c", p=128)
                out.append(g.dma_start(
                    out=wsb[nm][:, :].rearrange("p (ck c) -> p ck c", ck=16),
                    in_=src))
            out.append(g.dma_start(
                out=wb_sb[:, :].rearrange("p (ck c) -> p ck c", ck=16),
                in_=wbT.rearrange("(ck p) c -> p ck c", p=128)))
            return out
        P.step("g", ld_w, dma=True, n_dma=5)

        for tt in range(NMACRO):
            def ld_x(g, tt=tt):
                src = xT[:, tt * MACRO_T:(tt + 1) * MACRO_T].rearrange(
                    "(ck p) t -> p ck t", p=128)
                return [g.dma_start(out=xsb[:, :].rearrange(
                    "p (ck t) -> p ck t", ck=16), in_=src)]
            P.step("g", ld_x, dma=True, n_dma=1)

            for tau in ("q", "k", "v", "a", "g"):
                wname = {"q": "wq", "k": "wk", "v": "wv", "a": "wa",
                         "g": "wg"}[tau]
                for h in range(8):
                    if tau == "g":
                        def ld_wg(g, h=h):
                            src = wT["wg"][:, h * 128:(h + 1) * 128].rearrange(
                                "(ck p) c -> p ck c", p=128)
                            return [g.dma_start(out=wgh[:, :].rearrange(
                                "p (ck c) -> p ck c", ck=16), in_=src)]
                        P.step("g", ld_wg, dma=True, n_dma=1)

                        def mm_p(t):
                            last = None
                            for ck in range(16):
                                last = t.matmul(
                                    psP[:, 0:MACRO_T],
                                    wgh[:, ck * 128:(ck + 1) * 128],
                                    xsb[:, ck * MACRO_T:(ck + 1) * MACRO_T],
                                    start=(ck == 0), stop=(ck == 15))
                            return last
                        P.step("t", mm_p)
                    else:
                        def mm_p(t, h=h, wname=wname):
                            last = None
                            for ck in range(16):
                                last = t.matmul(
                                    psP[:, 0:MACRO_T],
                                    wsb[wname][:, ck * 1024 + h * 128:
                                               ck * 1024 + (h + 1) * 128],
                                    xsb[:, ck * MACRO_T:(ck + 1) * MACRO_T],
                                    start=(ck == 0), stop=(ck == 15))
                            return last
                        P.step("t", mm_p)

                    if tau in "qkv":
                        def ev_qkv(a, h=h):
                            return a.activation(
                                out=raw[:, h * SEG + 4:(h + 1) * SEG],
                                in_=psP[:, 0:MACRO_T], func=AF.Identity)
                        P.step("a", ev_qkv)
                    elif tau == "a":
                        def ev_a(a, h=h):
                            return a.activation(
                                out=stage[:, h * MACRO_T:(h + 1) * MACRO_T],
                                in_=psP[:, 0:MACRO_T], func=AF.Sigmoid,
                                bias=ba_sb[:, h:h + 1])
                        P.step("a", ev_a)
                    else:
                        def ev_g(a, h=h):
                            return a.activation(
                                out=stage2[:, h * MACRO_T:(h + 1) * MACRO_T],
                                in_=psP[:, 0:MACRO_T], func=AF.Sigmoid)
                        P.step("a", ev_g)

                if tau in "qkv":
                    hseg = {"q": 0, "k": 1, "v": 2}[tau]
                    r3 = raw[:, :].rearrange("p (h c) -> p h c", h=8)
                    if tt == 0:
                        def z_hist(v):
                            return v.memset(r3[:, :, 1:4], 0.0)
                        P.step("v", z_hist)
                    else:
                        def cp_hist(v, hseg=hseg):
                            return v.tensor_copy(
                                r3[:, :, 1:4],
                                hist[:, hseg * 24:(hseg + 1) * 24].rearrange(
                                    "p (h c) -> p h c", h=8))
                        P.step("v", cp_hist)

                    def sv_hist(v, hseg=hseg):
                        return v.tensor_copy(
                            hist[:, hseg * 24:(hseg + 1) * 24].rearrange(
                                "p (h c) -> p h c", h=8),
                            r3[:, :, SEG - 3:SEG])
                    P.step("v", sv_hist)

                    def conv_h(v, tau=tau):
                        last = None
                        for h in range(8):
                            base = h * SEG
                            last = v.tensor_scalar(
                                stage[:, h * MACRO_T:(h + 1) * MACRO_T],
                                raw[:, base + 1:base + 1 + MACRO_T],
                                cw_sb[tau][:, 4 * h:4 * h + 1],
                                cb_sb[tau][:, h:h + 1], OP.mult, OP.add)
                            for j in range(1, 4):
                                v.drain()
                                last = v.scalar_tensor_tensor(
                                    stage[:, h * MACRO_T:(h + 1) * MACRO_T],
                                    raw[:, base + 1 + j:base + 1 + j + MACRO_T],
                                    cw_sb[tau][:, 4 * h + j:4 * h + j + 1],
                                    stage[:, h * MACRO_T:(h + 1) * MACRO_T],
                                    OP.mult, OP.add)
                            v.drain()
                        return last
                    P.step("v", conv_h)

                    if tau in ("q", "k"):
                        def silu_bf(a):
                            return a.activation(out=stage2[:, :],
                                                in_=stage[:, :], func=AF.Silu)
                        P.step("a", silu_bf)
                        if tau == "k":
                            def kscale(v):
                                return v.tensor_scalar(
                                    stage2[:, :], stage2[:, :],
                                    float(DV) ** -0.5, None, OP.mult)
                            P.step("v", kscale)
                    else:
                        def silu32(a):
                            return a.activation(out=stage[:, :],
                                                in_=stage[:, :], func=AF.Silu)
                        P.step("a", silu32)

                def store(s, tau=tau, tt=tt):
                    srcb = stage if tau in ("v", "a") else stage2
                    d = scr[tau][:, tt * MACRO_T:(tt + 1) * MACRO_T].rearrange(
                        "(h p) t -> p h t", p=128)
                    return [s.dma_start(out=d, in_=srcb[:, :].rearrange(
                        "p (h t) -> p h t", h=8))]
                P.step("s", store, dma=True, n_dma=1)

                if tau == "a":
                    def recip_a(v):
                        return v.reciprocal(stage[:, :], stage[:, :])
                    P.step("v", recip_a)
                    for h in range(8):
                        def mm_beta(t, h=h):
                            last = None
                            for ck in range(16):
                                last = t.matmul(
                                    psP2[0:1, 0:MACRO_T],
                                    wb_sb[:, ck * 8 + h:ck * 8 + h + 1],
                                    xsb[:, ck * MACRO_T:(ck + 1) * MACRO_T],
                                    start=(ck == 0), stop=(ck == 15))
                            return last
                        P.step("t", mm_beta)

                        def ev_beta(a, h=h):
                            return a.activation(out=brow[:, :],
                                                in_=psP2[0:1, 0:MACRO_T],
                                                func=AF.Sigmoid,
                                                bias=bb_sb[0:1, h:h + 1])
                        P.step("a", ev_beta)

                        def bc_beta(t):
                            return t.matmul(psP2[:, 0:MACRO_T],
                                            ones_row_a[:, :], brow[:, :],
                                            start=True, stop=True)
                        P.step("t", bc_beta)

                        def cmul(v, h=h):
                            return v.tensor_tensor(
                                raw[:, h * SEG + 4:(h + 1) * SEG],
                                stage[:, h * MACRO_T:(h + 1) * MACRO_T],
                                psP2[:, 0:MACRO_T], OP.mult)
                        P.step("v", cmul)

                    def st_c(s, tt=tt):
                        d = scr["c"][:, tt * MACRO_T:(tt + 1) * MACRO_T].rearrange(
                            "(h p) t -> p h t", p=128)
                        return [s.dma_start(out=d, in_=raw[:, :].rearrange(
                            "p (h c) -> p h c", h=8)[:, :, 4:SEG])]
                    P.step("s", st_c, dma=True, n_dma=1)

        es.close()
        emit_scan(P, scr, cst, lnwb, og)
        P.final_wait()
    return P


def emit_scan(P, scr, cst, lnwb, og):
    nc = P.nc
    qT = sbuf(nc, "qT", (128, 8 * MACRO_T), BF16)
    kT = sbuf(nc, "kT", (128, 8 * MACRO_T), BF16)
    vT = sbuf(nc, "vT", (128, 8 * MACRO_T), F32)
    aT = sbuf(nc, "aT", (128, 8 * MACRO_T), F32)
    gT = sbuf(nc, "gT", (128, 8 * MACRO_T), BF16)
    cT = sbuf(nc, "cT", (128, 8 * MACRO_T), F32)
    ogm = sbuf(nc, "ogm", (128, 8 * MACRO_T), BF16)

    idf = sbuf(nc, "idf_sb", (128, 128), F32)
    idb = sbuf(nc, "idb_sb", (128, 128), BF16)
    mk = sbuf(nc, "mk_sb", (128, 4 * 128), F32)
    seqm = sbuf(nc, "seqm_sb", (128, 8 * 128), F32)
    ones_col = sbuf(nc, "ones_sb", (128, 1), F32)
    ones_row = sbuf(nc, "onesr_sb", (1, 128), F32)
    lnwb_sb = sbuf(nc, "lnwb_sb", (128, 2), F32)

    Smas = sbuf(nc, "Smas", (128, 8 * 128), F32)
    Sbf = sbuf(nc, "Sbf", (128, 8 * 128), BF16)
    Pbuf = sbuf(nc, "Pbuf", (128, 8 * 33), F32)
    Pbc = sbuf(nc, "Pbc", (128, 8 * 128), F32)
    plrow_sb = sbuf(nc, "plrow_sb", (1, 1024), F32)
    plc = sbuf(nc, "plc", (128, 8), F32)
    plc2 = sbuf(nc, "plc2", (8, 128), F32)

    recA = sbuf(nc, "recA", (128, 256), F32)
    vn = sbuf(nc, "vn", (128, 256), F32)
    rp = sbuf(nc, "rp", (128, 256), F32)
    ctg = sbuf(nc, "ctg", (128, 128), F32)
    Kblk = [sbuf(nc, "Kblk0", (128, 128), BF16),
            sbuf(nc, "Kblk1", (128, 128), BF16)]
    Kblk32 = sbuf(nc, "Kblk32", (128, 128), F32)
    Qblk = [sbuf(nc, "Qblk0", (128, 128), BF16),
            sbuf(nc, "Qblk1", (128, 128), BF16)]
    Mm = [sbuf(nc, "Mm0", (128, 128), BF16), sbuf(nc, "Mm1", (128, 128), BF16)]
    Mx = sbuf(nc, "Mx", (128, 128), BF16)
    Gm = [sbuf(nc, "Gm0", (128, 128), BF16), sbuf(nc, "Gm1", (128, 128), BF16)]
    Gx = sbuf(nc, "Gx", (128, 128), BF16)
    Cst = [sbuf(nc, "Cst0", (128, 128), F32), sbuf(nc, "Cst1", (128, 128), F32)]
    Rst = [sbuf(nc, "Rst0", (128, 128), BF16),
           sbuf(nc, "Rst1", (128, 128), BF16)]
    mI = sbuf(nc, "mI", (128, 128), BF16)
    Ub = [sbuf(nc, "Ub0", (128, 128), BF16), sbuf(nc, "Ub1", (128, 128), BF16)]
    KmT = sbuf(nc, "KmT", (128, 8 * 128), BF16)
    KmT1 = sbuf(nc, "KmT1", (128, 8 * 128), BF16)
    GU = sbuf(nc, "GU", (128, 128), F32)
    Otile = sbuf(nc, "Otile", (128, 128), F32)
    sq = sbuf(nc, "sq", (128, 128), F32)
    strow = sbuf(nc, "strow", (1, 6 * 128), F32)

    psG0 = psum(nc, "psG0")
    psG1 = psum(nc, "psG1")
    psJ = psum(nc, "psJ")
    psSQ = psum(nc, "psSQ")
    psSt = psum(nc, "psSt")
    psSt2 = psum(nc, "psSt2")
    psM = psum(nc, "psM")

    def ld_c(g):
        return [
            g.dma_start(out=idf[:, :], in_=cst["idf"]),
            g.dma_start(out=idb[:, :], in_=cst["idf"]),
            g.dma_start(out=mk[:, 0:128], in_=cst["mkd_neg"]),
            g.dma_start(out=mk[:, 128:256], in_=cst["mkf_neg"]),
            g.dma_start(out=mk[:, 256:384], in_=cst["mkg_pos"]),
            g.dma_start(out=mk[:, 384:512], in_=cst["mkf_pos"]),
            g.dma_start(out=seqm[:, :], in_=cst["seqm"]),
            g.dma_start(out=ones_col[:, :], in_=cst["ones_col"]),
            g.dma_start(out=ones_row[:, :], in_=cst["ones_row"]),
            g.dma_start(out=lnwb_sb[:, :], in_=lnwb),
        ]
    P.step("g", ld_c, dma=True, n_dma=10)

    def init_state(v):
        v.memset(Smas[:, :], 0.0)
        v.drain()
        v.memset(Sbf[:, :], 0.0)
        v.drain()
        return v.memset(Pbuf[:, :], 1.0)
    P.step("v", init_state)

    MKD, MKF, MKG, MKGF = (mk[:, 0:128], mk[:, 128:256], mk[:, 256:384],
                           mk[:, 384:512])

    for mm_i in range(NMACRO):
        def ld_m(g, mm_i=mm_i):
            out = []
            for nm, dstt in (("q", qT), ("k", kT), ("v", vT), ("a", aT),
                             ("g", gT), ("c", cT)):
                src = scr[nm][:, mm_i * MACRO_T:(mm_i + 1) * MACRO_T].rearrange(
                    "(h p) t -> p h t", p=128)
                out.append(g.dma_start(
                    out=dstt[:, :].rearrange("p (h t) -> p h t", h=8), in_=src))
            return out
        P.step("g", ld_m, dma=True, n_dma=6)

        for cc in range(MACRO_T // L):
            c0 = cc * L

            def hsl(tens, off, w, c0=c0):
                return tens[:, :].rearrange("p (h t) -> p h t",
                                            h=8)[:, :, c0 + off:c0 + off + w]

            def pscan(v, c0=c0):
                last = None
                for s in range(8):
                    sl_a = aT[:, s * MACRO_T + c0:s * MACRO_T + c0 + L]
                    last = v.tensor_tensor_scan(
                        Pbuf[:, 33 * s + 1:33 * s + 33], sl_a, sl_a,
                        1.0, OP.mult, OP.bypass)
                    v.drain()
                return last
            P.step("v", pscan)

            def cp_pl(v):
                return v.tensor_copy(
                    plc[:, :],
                    Pbuf[:, :].rearrange("p (s c) -> p s c",
                                         s=8)[:, :, 32:33].rearrange(
                        "p s one -> p (s one)"))
            P.step("v", cp_pl)

            def tr_pl(t):
                return t.transpose(psM[0:8, 0:128], plc[:, :], idf[:, :])
            P.step("t", tr_pl)

            def ev_pl(v):
                return v.tensor_copy(plc2[:, :], psM[0:8, 0:128])
            P.step("v", ev_pl)

            def st_pl(s):
                return [s.dma_start(out=scr["plrow"], in_=plc2[:, :])]
            P.step("s", st_pl, dma=True, n_dma=1)

            def ld_pl(g):
                src = scr["plrow"].rearrange("s p -> (s p)")[None, :]
                return [g.dma_start(out=plrow_sb[:, :], in_=src)]
            P.step("g", ld_pl, dma=True, n_dma=1)

            def vn_build(v, c0=c0):
                pprev = Pbuf[:, :].rearrange("p (s c) -> p s c", s=8)[:, :, 0:32]
                r8 = recA[:, :].rearrange("p (s t) -> p s t", s=8)
                v.reciprocal(r8, pprev)
                v.drain()
                return v.tensor_tensor(
                    vn[:, :].rearrange("p (s t) -> p s t", s=8),
                    hsl(vT, 0, L), r8, OP.mult)
            P.step("v", vn_build)

            def blkcopies(v, c0=c0):
                for b in range(2):
                    v.tensor_copy(Kblk[b][:, :].rearrange(
                        "p (s t) -> p s t", t=16), hsl(kT, 16 * b, 16))
                    v.drain()
                    v.tensor_copy(Qblk[b][:, :].rearrange(
                        "p (s t) -> p s t", t=16), hsl(qT, 16 * b, 16))
                    v.drain()
                return v.tensor_copy(Kblk32[:, :].rearrange(
                    "p (s t) -> p s t", t=16), hsl(kT, 0, 16))
            P.step("v", blkcopies)

            def grams(t):
                t.matmul(psG0[:, 0:128], Kblk[0][:, :], Kblk[0][:, :],
                         start=True, stop=True)
                t.matmul(psG1[:, 0:128], Kblk[1][:, :], Kblk[1][:, :],
                         start=True, stop=True)
                t.matmul(psG0[:, 128:256], Kblk[0][:, :], Kblk[1][:, :],
                         start=True, stop=True)
                t.matmul(psG1[:, 128:256], Kblk[0][:, :], Qblk[0][:, :],
                         start=True, stop=True)
                t.matmul(psG0[:, 256:384], Kblk[1][:, :], Qblk[1][:, :],
                         start=True, stop=True)
                return t.matmul(psG1[:, 256:384], Kblk[0][:, :], Qblk[1][:, :],
                                start=True, stop=True)
            P.step("t", grams)

            def maskev(v):
                v.tensor_tensor(Mm[0][:, :], psG0[:, 0:128], MKD, OP.mult)
                v.drain()
                v.tensor_tensor(Mm[1][:, :], psG1[:, 0:128], MKD, OP.mult)
                v.drain()
                v.tensor_tensor(Mx[:, :], psG0[:, 128:256], MKF, OP.mult)
                v.drain()
                v.tensor_tensor(Gm[0][:, :], psG1[:, 128:256], MKG, OP.mult)
                v.drain()
                v.tensor_tensor(Gm[1][:, :], psG0[:, 256:384], MKG, OP.mult)
                v.drain()
                return v.tensor_tensor(Gx[:, :], psG1[:, 256:384], MKGF,
                                       OP.mult)
            P.step("v", maskev)

            def s0k(t, c0=c0):
                last = None
                for s in range(8):
                    last = t.matmul(
                        psSQ[:, 32 * s:32 * s + 32],
                        Sbf[:, 128 * s:128 * (s + 1)],
                        kT[:, s * MACRO_T + c0:s * MACRO_T + c0 + L],
                        start=True, stop=True)
                return last
            P.step("t", s0k)

            def rpbuild(v):
                return v.scalar_tensor_tensor(rp[:, :], psSQ[:, 0:256], -1.0,
                                              vn[:, :], OP.mult, OP.add)
            P.step("v", rpbuild)

            for b in range(2):
                def cp_rp(v, b=b):
                    return v.tensor_copy(
                        ctg[:, :].rearrange("p (s t) -> p s t", t=16),
                        rp[:, :].rearrange("p (s t) -> p s t",
                                           t=32)[:, :, 16 * b:16 * b + 16])
                P.step("v", cp_rp)

                def tr_rp(t):
                    return t.transpose(psM[:, 0:128], ctg[:, :], idf[:, :])
                P.step("t", tr_rp)

                def ev_rp(v, b=b):
                    return v.tensor_copy(Rst[b][:, :], psM[:, 0:128])
                P.step("v", ev_rp)

                def cp_ra(v, b=b, c0=c0):
                    return v.tensor_copy(
                        ctg[:, :].rearrange("p (s t) -> p s t", t=16),
                        hsl(cT, 16 * b, 16))
                P.step("v", cp_ra)

                def tr_ra(t):
                    return t.transpose(psM[:, 0:128], ctg[:, :], idf[:, :])
                P.step("t", tr_ra)

                def ev_ca(v, b=b):
                    return v.tensor_copy(Cst[b][:, :], psM[:, 0:128])
                P.step("v", ev_ca)

            for b in range(2):
                if b == 1:
                    def crossmm(t):
                        t.matmul(psJ[:, 0:128], idb[:, :], Rst[1][:, :],
                                 start=True, stop=False)
                        return t.matmul(psJ[:, 0:128], Mx[:, :], Ub[0][:, :],
                                        start=False, stop=True)
                    P.step("t", crossmm)

                    def evcross(v):
                        return v.tensor_copy(Rst[1][:, :], psJ[:, 0:128])
                    P.step("v", evcross)

                for it in range(JB):
                    def jmm(t, b=b, it=it):
                        last = t.matmul(psJ[:, 0:128], idb[:, :], Rst[b][:, :],
                                        start=True, stop=(it == 0))
                        if it > 0:
                            last = t.matmul(psJ[:, 0:128], Mm[b][:, :],
                                            mI[:, :], start=False, stop=True)
                        return last
                    P.step("t", jmm)

                    out_t = Ub[b] if it == JB - 1 else mI

                    def jdve(v, out_t=out_t, b=b):
                        return v.tensor_tensor(out_t[:, :], psJ[:, 0:128],
                                               Cst[b][:, :], OP.mult)
                    P.step("v", jdve)

            for b in range(2):
                def gu_mm(t, b=b):
                    if b == 0:
                        return t.matmul(psG0[:, 0:128], Gm[0][:, :],
                                        Ub[0][:, :], start=True, stop=True)
                    t.matmul(psG0[:, 0:128], Gx[:, :], Ub[0][:, :],
                             start=True, stop=False)
                    return t.matmul(psG0[:, 0:128], Gm[1][:, :], Ub[1][:, :],
                                    start=False, stop=True)
                P.step("t", gu_mm)

                def gu_ev(v):
                    return v.tensor_copy(GU[:, :], psG0[:, 0:128])
                P.step("v", gu_ev)

                def gu_tr(t, b=b, c0=c0):
                    t.transpose(psG1[:, 0:128], GU[:, :], idf[:, :])
                    last = None
                    for s in range(8):
                        last = t.matmul(
                            psSQ[:, 256 + 16 * s:256 + 16 * s + 16],
                            Sbf[:, 128 * s:128 * (s + 1)],
                            qT[:, s * MACRO_T + c0 + 16 * b:
                               s * MACRO_T + c0 + 16 * b + 16],
                            start=True, stop=True)
                    return last
                P.step("t", gu_tr)

                def ev_gut(v):
                    return v.tensor_copy(GU[:, :], psG1[:, 0:128])
                P.step("v", ev_gut)

                def o_sum(v):
                    return v.scalar_tensor_tensor(
                        Otile[:, :], psSQ[:, 256:384], 1.0, GU[:, :],
                        OP.mult, OP.add)
                P.step("v", o_sum)

                def o_scale(v, b=b):
                    pc = Pbuf[:, :].rearrange(
                        "p (s c) -> p s c", s=8)[:, :, 1 + 16 * b:17 + 16 * b]
                    return v.tensor_tensor(
                        Otile[:, :].rearrange("p (s t) -> p s t", t=16),
                        Otile[:, :].rearrange("p (s t) -> p s t", t=16),
                        pc, OP.mult)
                P.step("v", o_scale)

                def sq_op(a):
                    return a.activation(out=sq[:, :], in_=Otile[:, :],
                                        func=AF.Square)
                P.step("a", sq_op)

                def stat_mm(t):
                    t.matmul(psM[0:1, 0:128], ones_col[:, :], Otile[:, :],
                             start=True, stop=True)
                    return t.matmul(psM[0:1, 128:256], ones_col[:, :],
                                    sq[:, :], start=True, stop=True)
                P.step("t", stat_mm)

                def stats(v):
                    v.tensor_scalar(strow[:, 0:128], psM[0:1, 0:128],
                                    1.0 / 128.0, None, OP.mult)
                    v.drain()
                    v.tensor_scalar(strow[:, 128:256], psM[0:1, 128:256],
                                    1.0 / 128.0, None, OP.mult)
                    v.drain()
                    v.tensor_tensor(strow[:, 256:384], strow[:, 0:128],
                                    strow[:, 0:128], OP.mult)
                    v.drain()
                    v.tensor_tensor(strow[:, 256:384], strow[:, 128:256],
                                    strow[:, 256:384], OP.subtract)
                    v.drain()
                    return v.tensor_scalar(strow[:, 256:384],
                                           strow[:, 256:384], 1e-5, None,
                                           OP.add)
                P.step("v", stats)

                def rstd_op(a):
                    return a.activation(out=strow[:, 384:512],
                                        in_=strow[:, 256:384], func=AF.Sqrt)
                P.step("a", rstd_op)

                def rstd2(v):
                    return v.reciprocal(strow[:, 256:384], strow[:, 384:512])
                P.step("v", rstd2)

                def bc_mm(t):
                    t.matmul(psM[:, 128:256], ones_row[:, :], strow[:, 0:128],
                             start=True, stop=True)
                    return t.matmul(psM[:, 256:384], ones_row[:, :],
                                    strow[:, 256:384], start=True, stop=True)
                P.step("t", bc_mm)

                def lnapply(v, b=b, c0=c0):
                    v.tensor_tensor(Otile[:, :], Otile[:, :], psM[:, 128:256],
                                    OP.subtract)
                    v.drain()
                    v.tensor_tensor(Otile[:, :], Otile[:, :], psM[:, 256:384],
                                    OP.mult)
                    v.drain()
                    v.tensor_scalar(Otile[:, :], Otile[:, :], lnwb_sb[:, 0:1],
                                    lnwb_sb[:, 1:2], OP.mult, OP.add)
                    v.drain()
                    return v.tensor_tensor(
                        ogm[:, :].rearrange("p (s t) -> p s t",
                                            s=8)[:, :, c0 + 16 * b:
                                                 c0 + 16 * b + 16],
                        Otile[:, :].rearrange("p (s t) -> p s t", t=16),
                        hsl(gT, 16 * b, 16), OP.mult)
                P.step("v", lnapply)

            def kst_tr0(t):
                return t.transpose(psSt[:, 0:128], Kblk32[:, :], idf[:, :])
            P.step("t", kst_tr0)

            def kmask0(v):
                last = None
                for s in range(8):
                    last = v.tensor_tensor(KmT[:, 128 * s:128 * (s + 1)],
                                           psSt[:, 0:128],
                                           seqm[:, 128 * s:128 * (s + 1)],
                                           OP.mult)
                    v.drain()
                return last
            P.step("v", kmask0)

            def cp_k32b1(v, c0=c0):
                return v.tensor_copy(
                    Kblk32[:, :].rearrange("p (s t) -> p s t", t=16),
                    hsl(kT, 16, 16))
            P.step("v", cp_k32b1)

            def kst_tr1(t):
                return t.transpose(psM[:, 0:128], Kblk32[:, :], idf[:, :])
            P.step("t", kst_tr1)

            def kmask1(v):
                last = None
                for s in range(8):
                    last = v.tensor_tensor(KmT1[:, 128 * s:128 * (s + 1)],
                                           psM[:, 0:128],
                                           seqm[:, 128 * s:128 * (s + 1)],
                                           OP.mult)
                    v.drain()
                return last
            P.step("v", kmask1)

            def smm_all(t):
                last = None
                for s in range(8):
                    tgt = psSt if s < 4 else psSt2
                    off = 128 * (s % 4)
                    last = t.matmul(tgt[:, off:off + 128],
                                    KmT[:, 128 * s:128 * (s + 1)],
                                    Ub[0][:, :], start=True, stop=False)
                    last = t.matmul(tgt[:, off:off + 128],
                                    KmT1[:, 128 * s:128 * (s + 1)],
                                    Ub[1][:, :], start=False, stop=False)
                    last = t.matmul(tgt[:, off:off + 128], idb[:, :],
                                    Sbf[:, 128 * s:128 * (s + 1)],
                                    start=False, stop=True)
                return last
            P.step("t", smm_all)

            def pbc_mm(t):
                t.matmul(psG0[:, 0:512], ones_row[:, :], plrow_sb[:, 0:512],
                         start=True, stop=True)
                return t.matmul(psG1[:, 0:512], ones_row[:, :],
                                plrow_sb[:, 512:1024], start=True, stop=True)
            P.step("t", pbc_mm)

            def pbc_ev(v):
                v.tensor_copy(Pbc[:, 0:512], psG0[:, 0:512])
                v.drain()
                return v.tensor_copy(Pbc[:, 512:1024], psG1[:, 0:512])
            P.step("v", pbc_ev)

            def st_scale(v):
                v.tensor_tensor(Smas[:, 0:512], psSt[:, 0:512], Pbc[:, 0:512],
                                OP.mult)
                v.drain()
                v.tensor_tensor(Smas[:, 512:1024], psSt2[:, 0:512],
                                Pbc[:, 512:1024], OP.mult)
                v.drain()
                return v.tensor_copy(Sbf[:, :], Smas[:, :])
            P.step("v", st_scale)

        def st_og(s, mm_i=mm_i):
            d = og[:, mm_i * MACRO_T:(mm_i + 1) * MACRO_T].rearrange(
                "(h p) t -> p h t", p=128)
            return [s.dma_start(out=d, in_=ogm[:, :].rearrange(
                "p (h t) -> p h t", h=8))]
        P.step("s", st_og, dma=True, n_dma=1)



def prep_l0attn_inputs(inputs):
    x = np.ascontiguousarray(inputs["L0_attn_x"], dtype=np.float32)
    consts = _consts_l0()
    per_core = []
    for c in range(NCORES):
        b = c // 2
        r = c % 2
        hlo = 8 * r
        rows = slice(hlo * 128, (hlo + 8) * 128)
        d = {
            "xT": np.ascontiguousarray(x[b].T),
            "wq": np.ascontiguousarray(inputs["L0_attn_Wq"][rows].T),
            "wk": np.ascontiguousarray(inputs["L0_attn_Wk"][rows].T),
            "wv": np.ascontiguousarray(inputs["L0_attn_Wv"][rows].T),
            "wa": np.ascontiguousarray(inputs["L0_attn_Wa"][rows].T),
            "wg": np.ascontiguousarray(inputs["L0_attn_Wg"][rows].T),
            "wb": np.ascontiguousarray(inputs["L0_attn_Wb"][hlo:hlo + 8].T),
            "bacol": np.ascontiguousarray(
                inputs["L0_attn_ba"][rows].reshape(8, 128).T),
            "bbrow": np.ascontiguousarray(
                inputs["L0_attn_bb"][hlo:hlo + 8].reshape(1, 8)),
            "lnwb": np.ascontiguousarray(np.stack(
                [inputs["L0_attn_ln_w"], inputs["L0_attn_ln_b"]], axis=1)),
        }
        for nm, key in (("q", "qconv"), ("k", "kconv"), ("v", "vconv")):
            w = inputs[f"L0_attn_{key}_w"][rows]
            bvec = inputs[f"L0_attn_{key}_b"][rows]
            d["cw" + nm] = np.ascontiguousarray(
                w.reshape(8, 128, 4).transpose(1, 0, 2).reshape(128, 32))
            d["cb" + nm] = np.ascontiguousarray(
                bvec.reshape(8, 128).T)
        for k, v in consts.items():
            d["c_" + k] = v
        per_core.append({k: np.ascontiguousarray(v, dtype=np.float32)
                         for k, v in d.items()})
    return per_core





TPC = 1024


def _consts_a2():
    return dict(idf=np.eye(128, dtype=np.float32),
                ones_col=np.ones((128, 1), np.float32),
                ones_row=np.ones((1, 128), np.float32),
                iota8=np.tile(np.arange(8, dtype=np.float32), (128, 1)),
                riota8=np.tile(np.arange(7, -1, -1, dtype=np.float32),
                               (128, 1)))


def build_l0rest():
    P = Prog("l0rest")
    nc = P.nc
    TH = TPC // 2
    og_in = nc.dram_tensor("ogf", [2048, TPC], BF16, kind="ExternalInput").ap()
    xT = nc.dram_tensor("xT", [2048, TPC], F32, kind="ExternalInput").ap()
    woT = nc.dram_tensor("woT", [2048, 2048], F32, kind="ExternalInput").ap()
    gwT = nc.dram_tensor("gwT", [2048, 8], F32, kind="ExternalInput").ap()
    wge = nc.dram_tensor("wge", [8 * 2048, 1024], F32, kind="ExternalInput").ap()
    wue = nc.dram_tensor("wue", [8 * 2048, 1024], F32, kind="ExternalInput").ap()
    wde = nc.dram_tensor("wde", [8 * 1024, 2048], F32, kind="ExternalInput").ap()
    shgT = nc.dram_tensor("shgT", [2048, 2048], F32, kind="ExternalInput").ap()
    shuT = nc.dram_tensor("shuT", [2048, 2048], F32, kind="ExternalInput").ap()
    shdT = nc.dram_tensor("shdT", [2048, 2048], F32, kind="ExternalInput").ap()
    seP = nc.dram_tensor("seP", [128, 16], F32, kind="ExternalInput").ap()
    kvdw = nc.dram_tensor("kvdw", [2048, 512], F32, kind="ExternalInput").ap()
    cst = {k: nc.dram_tensor("c_" + k, list(v.shape), F32,
                             kind="ExternalInput").ap()
           for k, v in _consts_a2().items()}
    h2T_o = nc.dram_tensor("h2T", [2048, TPC], F32, kind="ExternalOutput").ap()
    ckvT_o = nc.dram_tensor("ckvT", [512, TPC], F32, kind="ExternalOutput").ap()
    h1scr = nc.dram_tensor("h1scr", [2048, TPC], F32).ap()
    ckvscr = nc.dram_tensor("ckvscr", [512, TPC], F32).ap()
    combscr = nc.dram_tensor("combscr", [8, TPC], F32).ap()

    h1b = sbuf(nc, "h1b", (128, 16 * TPC), BF16)
    idf = sbuf(nc, "idf_sb", (128, 128), F32)
    ones_col = sbuf(nc, "ones_sb", (128, 1), F32)
    ones_row = sbuf(nc, "onesr_sb", (1, 128), F32)
    wtile = sbuf(nc, "wtile", (128, 16 * 128), BF16)
    wtile2 = sbuf(nc, "wtile2", (128, 16 * 128), BF16)
    wtile3 = sbuf(nc, "wtile3", (128, 16 * 128), BF16)
    xck = sbuf(nc, "xck", (128, TPC), F32)
    xck2 = sbuf(nc, "xck2", (128, TPC), F32)
    tmp1 = sbuf(nc, "tmp1", (128, TPC), F32)
    tmp1b = sbuf(nc, "tmp1b", (128, TPC), F32)
    tmp2 = sbuf(nc, "tmp2", (128, TPC), F32)
    cbsb = sbuf(nc, "cbsb", (128, TPC), F32)
    sgb = sbuf(nc, "sgb", (128, TPC), F32)
    strows = sbuf(nc, "strows", (1, 3 * TPC), F32)
    srow = sbuf(nc, "srow", (1, TPC), F32)
    stok = sbuf(nc, "stok", (128, (TPC // 128) * 8), F32)
    ctok = sbuf(nc, "ctok", (128, (TPC // 128) * 8), F32)
    crows = sbuf(nc, "crows", (8, TPC), F32)
    small = sbuf(nc, "small", (128, 64), F32)

    ps0 = psum(nc, "ps0")
    ps1 = psum(nc, "ps1")
    ps2 = psum(nc, "ps2")
    ps3 = psum(nc, "ps3")
    psS = psum(nc, "psS")
    psB = psum(nc, "psB")

    NT = TPC // 128
    wt3 = [wtile, wtile2, wtile3]
    xb2 = [xck, xck2]
    t1p = [tmp1, tmp1b]

    def ldw_chunkcol(g, srcap, dst, col0, ncol=128, nck=16):
        src = srcap[:, col0:col0 + ncol].rearrange("(ck p) c -> p ck c", p=128)
        return [g.dma_start(out=dst[:, 0:nck * ncol].rearrange(
            "p (ck c) -> p ck c", ck=nck), in_=src)]

    with P.block():
        wstate = {"i": 0}
        wcons = [None, None, None]
        wtok = [None, None, None]

        def wload(mkf, n_dma=1):
            i = wstate["i"] % 3
            q = "g"

            def fn(eng, mkf=mkf, tile=wt3[i]):
                return mkf(eng, tile)
            wtok[i] = P.step(q, fn, dma=True, n_dma=n_dma,
                             waits=(wcons[i],), sem_name="wt%d" % i)
            wstate["i"] += 1
            return i

        xstate = {"i": 0}
        xcons = [None, None]
        xtok = [None, None]

        def xload(mkf, extra=()):
            i = xstate["i"] % 2
            q = ("g", "s")[xstate["i"] % 2]

            def fn(eng, mkf=mkf, tile=xb2[i]):
                return mkf(eng, tile)
            xtok[i] = P.step(q, fn, dma=True, n_dma=1,
                             waits=(xcons[i],) + tuple(extra),
                             sem_name="xk%d" % i)
            xstate["i"] += 1
            return i

        def ld_c(g):
            return [g.dma_start(out=idf[:, :], in_=cst["idf"]),
                    g.dma_start(out=ones_col[:, :], in_=cst["ones_col"]),
                    g.dma_start(out=ones_row[:, :], in_=cst["ones_row"])]
        t_ldc = P.step("g", ld_c, dma=True, n_dma=3, waits=(),
                       sem_name="ldc")

        from contextlib import ExitStack
        es = ExitStack()
        ogsb = es.enter_context(nc.sbuf_tensor("ogsb", [128, 16 * TPC], BF16))

        def ld_og(g):
            src = og_in.rearrange("(ck p) t -> p ck t", p=128)
            return [g.dma_start(out=ogsb[:, :].rearrange(
                "p (ck t) -> p ck t", ck=16), in_=src)]
        t_og = P.step("a", ld_og, dma=True, n_dma=1, waits=(), sem_name="og")

        def z_stats(v):
            return v.memset(strows[:, :], 0.0)
        pv = P.step("v", z_stats, waits=())

        tst = [None, None]
        for ck in range(16):
            wi_ = wload(lambda g, tile, ck=ck:
                        ldw_chunkcol(g, woT, tile, ck * 128))
            xi_ = xload(lambda g, tile, ck=ck:
                        [g.dma_start(out=tile[:, :],
                                     in_=xT[ck * 128:(ck + 1) * 128, :])])
            cp = ck % 2
            for th in range(2):
                def mm_o(t, ck=ck, th=th, wi_=wi_):
                    last = None
                    for hk in range(16):
                        last = t.matmul(
                            ps0[:, 0:TH],
                            wt3[wi_][:, hk * 128:(hk + 1) * 128],
                            ogsb[:, hk * TPC + th * TH:hk * TPC + (th + 1) * TH],
                            start=(hk == 0), stop=(hk == 15))
                    return last
                pv = P.step("t", mm_o, waits=(pv, wtok[wi_], t_og))
                if th == 1:
                    wcons[wi_] = pv

                def addx(v, ck=ck, th=th, xi_=xi_, cp=cp):
                    return v.tensor_tensor(t1p[cp][:, th * TH:(th + 1) * TH],
                                           ps0[:, 0:TH],
                                           xb2[xi_][:, th * TH:(th + 1) * TH],
                                           OP.add)
                pv = P.step("v", addx, waits=(pv, xtok[xi_], tst[cp]))
                if th == 1:
                    xcons[xi_] = pv

            def st_h1(s, ck=ck, cp=cp):
                return [s.dma_start(out=h1scr[ck * 128:(ck + 1) * 128, :],
                                    in_=t1p[cp][:, :])]
            tst[cp] = P.step("s", st_h1, dma=True, n_dma=1, waits=(pv,),
                             sem_name="sh%d" % cp)

            def sq_h1(a, cp=cp):
                return a.activation(out=tmp2[:, :], in_=t1p[cp][:, :],
                                    func=AF.Square)
            pv = P.step("a", sq_h1, waits=(pv,))

            for th in range(2):
                def mm_stats(t, th=th, cp=cp):
                    t.matmul(psS[0:1, 0:TH], ones_col[:, :],
                             t1p[cp][:, th * TH:(th + 1) * TH],
                             start=True, stop=True)
                    return t.matmul(psB[0:1, 0:TH], ones_col[:, :],
                                    tmp2[:, th * TH:(th + 1) * TH],
                                    start=True, stop=True)
                pv = P.step("t", mm_stats, waits=(pv, t_ldc))

                def acc_stats(v, th=th):
                    v.tensor_tensor(strows[:, th * TH:(th + 1) * TH],
                                    strows[:, th * TH:(th + 1) * TH],
                                    psS[0:1, 0:TH], OP.add)
                    v.drain()
                    return v.tensor_tensor(
                        strows[:, TPC + th * TH:TPC + (th + 1) * TH],
                        strows[:, TPC + th * TH:TPC + (th + 1) * TH],
                        psB[0:1, 0:TH], OP.add)
                pv = P.step("v", acc_stats, waits=(pv,))

        def ln_scalars(v):
            v.tensor_scalar(strows[:, 0:TPC], strows[:, 0:TPC],
                            1.0 / 2048.0, None, OP.mult)
            v.drain()
            v.tensor_scalar(strows[:, TPC:2 * TPC], strows[:, TPC:2 * TPC],
                            1.0 / 2048.0, None, OP.mult)
            v.drain()
            v.tensor_tensor(strows[:, 2 * TPC:3 * TPC], strows[:, 0:TPC],
                            strows[:, 0:TPC], OP.mult)
            v.drain()
            v.tensor_tensor(strows[:, TPC:2 * TPC], strows[:, TPC:2 * TPC],
                            strows[:, 2 * TPC:3 * TPC], OP.subtract)
            v.drain()
            return v.tensor_scalar(strows[:, TPC:2 * TPC],
                                   strows[:, TPC:2 * TPC], 1e-5, None, OP.add)

        def rstd_a(a):
            return a.activation(out=strows[:, 2 * TPC:3 * TPC],
                                in_=strows[:, TPC:2 * TPC], func=AF.Sqrt)

        def rstd_v(v):
            return v.reciprocal(strows[:, TPC:2 * TPC],
                                strows[:, 2 * TPC:3 * TPC])

        pv = P.step("v", ln_scalars, waits=(pv,))
        pv = P.step("a", rstd_a, waits=(pv,))
        pv = P.step("v", rstd_v, waits=(pv,))

        tsp = [None, None]

        def emit_ln_apply(pv, scr_in, dst_sb, dst_scr, store_sem, h2mode,
                          pre_toks=()):
            ai = [0]
            for th in range(2):
                def bc_mu(t, th=th):
                    t.matmul(psB[:, 0:TH], ones_row[:, :],
                             strows[:, th * TH:(th + 1) * TH],
                             start=True, stop=True)
                    return t.matmul(ps2[:, 0:TH], ones_row[:, :],
                                    strows[:, TPC + th * TH:TPC + (th + 1) * TH],
                                    start=True, stop=True)
                pv = P.step("t", bc_mu, waits=(pv,))

                def ev_bc(v, th=th):
                    v.tensor_copy(tmp2[:, 0:TH], psB[:, 0:TH])
                    v.drain()
                    return v.tensor_copy(tmp2[:, TH:2 * TH], ps2[:, 0:TH])
                pv = P.step("v", ev_bc, waits=(pv,))

                for ck in range(16):
                    if h2mode:
                        xi_ = None
                    else:
                        xi_ = xload(lambda g, tile, ck=ck, th=th:
                                    [g.dma_start(
                                        out=tile[:, 0:TH],
                                        in_=scr_in[ck * 128:(ck + 1) * 128,
                                                   th * TH:(th + 1) * TH])],
                                    extra=pre_toks)
                    ap = ai[0] % 2

                    def apply(v, ck=ck, th=th, xi_=xi_, ap=ap):
                        if xi_ is None:
                            sl = racc[:, ck * TPC + th * TH:
                                      ck * TPC + (th + 1) * TH]
                        else:
                            sl = xb2[xi_][:, 0:TH]
                        v.tensor_tensor(t1p[ap][:, 0:TH], sl, tmp2[:, 0:TH],
                                        OP.subtract)
                        v.drain()
                        last = v.tensor_tensor(t1p[ap][:, 0:TH],
                                               t1p[ap][:, 0:TH],
                                               tmp2[:, TH:2 * TH], OP.mult)
                        if dst_sb is not None:
                            v.drain()
                            last = v.tensor_copy(
                                dst_sb[:, ck * TPC + th * TH:
                                       ck * TPC + (th + 1) * TH],
                                t1p[ap][:, 0:TH])
                        return last
                    pv = P.step("v", apply,
                                waits=(pv, tsp[ap]) +
                                ((xtok[xi_],) if xi_ is not None else ()))
                    if xi_ is not None:
                        xcons[xi_] = pv

                    if dst_scr is not None:
                        def st_post(s, ck=ck, th=th, ap=ap):
                            return [s.dma_start(
                                out=dst_scr[ck * 128:(ck + 1) * 128,
                                            th * TH:(th + 1) * TH],
                                in_=t1p[ap][:, 0:TH])]
                        tsp[ap] = P.step("s", st_post, dma=True, n_dma=1,
                                         waits=(pv,),
                                         sem_name="%s%d" % (store_sem, ap))
                    ai[0] += 1
            return pv

        tsp[0], tsp[1] = tst[0], tst[1]
        pv = emit_ln_apply(pv, h1scr, h1b, h1scr, "sp", h2mode=False,
                           pre_toks=(tst[0], tst[1]))
        t_h1s_done = (tsp[0], tsp[1])
        es.close()
        actsh = sbuf(nc, "actsh", (128, 16 * TPC), BF16)
        racc = sbuf(nc, "racc", (128, 16 * TPC), F32)
        h1e = sbuf(nc, "h1e", (128, 8 * TPC), BF16)

        wg_i = wload(lambda g, tile: ldw_chunkcol(g, gwT, tile, 0, ncol=8))

        for th in range(2):
            def mm_gate(t, th=th, wg_i=wg_i):
                last = None
                for ck in range(16):
                    last = t.matmul(ps0[0:8, 0:TH],
                                    wt3[wg_i][:, ck * 8:(ck + 1) * 8],
                                    h1b[:, ck * TPC + th * TH:
                                        ck * TPC + (th + 1) * TH],
                                    start=(ck == 0), stop=(ck == 15))
                return last
            pv = P.step("t", mm_gate, waits=(pv, wtok[wg_i]))
            if th == 1:
                wcons[wg_i] = pv

            def ev_gate(v, th=th):
                return v.tensor_copy(crows[:, th * TH:(th + 1) * TH],
                                     ps0[0:8, 0:TH])
            pv = P.step("v", ev_gate, waits=(pv,))

        for ti in range(NT):
            def tr_sc(t, ti=ti):
                return t.transpose(ps1[0:128, 0:8],
                                   crows[:, ti * 128:(ti + 1) * 128],
                                   idf[0:8, 0:8])
            pv = P.step("t", tr_sc, waits=(pv,))

            def ev_sc(v, ti=ti):
                return v.tensor_copy(stok[:, ti * 8:(ti + 1) * 8],
                                     ps1[0:128, 0:8])
            pv = P.step("v", ev_sc, waits=(pv,))

        for ti in range(NT):
            def tk_a(v, ti=ti):
                sl = stok[:, ti * 8:(ti + 1) * 8]
                v.reduce_max(small[:, 0:1], sl, axis=AX.X)
                v.drain()
                v.tensor_scalar(small[:, 8:16], sl, small[:, 0:1], None,
                                OP.is_ge)
                v.drain()
                v.scalar_tensor_tensor(small[:, 16:24], small[:, 8:16], -1e30,
                                       sl, OP.mult, OP.add)
                v.drain()
                v.reduce_max(small[:, 1:2], small[:, 16:24], axis=AX.X)
                v.drain()
                v.tensor_scalar(small[:, 24:32], small[:, 16:24],
                                small[:, 1:2], None, OP.is_ge)
                v.drain()
                return v.tensor_tensor(small[:, 2:3], small[:, 1:2],
                                       small[:, 0:1], OP.subtract)
            pv = P.step("v", tk_a, waits=(pv,))

            def tk_exp(a):
                return a.activation(out=small[:, 3:4], in_=small[:, 2:3],
                                    func=AF.Exp)
            pv = P.step("a", tk_exp, waits=(pv,))

            def tk_w(v, ti=ti):
                v.tensor_scalar(small[:, 4:5], small[:, 3:4], 1.0, None,
                                OP.add)
                v.drain()
                v.reciprocal(small[:, 5:6], small[:, 4:5])
                v.drain()
                v.tensor_tensor(small[:, 6:7], small[:, 3:4], small[:, 5:6],
                                OP.mult)
                v.drain()
                v.tensor_scalar(small[:, 32:40], small[:, 8:16],
                                small[:, 5:6], None, OP.mult)
                v.drain()
                v.tensor_scalar(small[:, 40:48], small[:, 24:32],
                                small[:, 6:7], None, OP.mult)
                v.drain()
                return v.tensor_tensor(ctok[:, ti * 8:(ti + 1) * 8],
                                       small[:, 32:40], small[:, 40:48],
                                       OP.add)
            pv = P.step("v", tk_w, waits=(pv,))

            def tr_cb(t, ti=ti):
                return t.transpose(ps1[0:8, 0:128],
                                   ctok[:, ti * 8:(ti + 1) * 8], idf[:, :])
            pv = P.step("t", tr_cb, waits=(pv,))

            def ev_cb(v, ti=ti):
                return v.tensor_copy(crows[:, ti * 128:(ti + 1) * 128],
                                     ps1[0:8, 0:128])
            pv = P.step("v", ev_cb, waits=(pv,))

        def st_comb(s):
            return [s.dma_start(out=combscr, in_=crows[:, :])]
        t_comb = P.step("s", st_comb, dma=True, n_dma=1, waits=(pv,),
                        sem_name="cmb")

        for hs in range(16):
            wiA = wload(lambda g, tile, hs=hs:
                        ldw_chunkcol(g, shgT, tile, hs * 128))
            wiB = wload(lambda g, tile, hs=hs:
                        ldw_chunkcol(g, shuT, tile, hs * 128))

            for th in range(2):
                def mm_sh(t, th=th, wiA=wiA, wiB=wiB):
                    for ck in range(16):
                        t.matmul(ps0[:, 0:TH],
                                 wt3[wiA][:, ck * 128:(ck + 1) * 128],
                                 h1b[:, ck * TPC + th * TH:
                                     ck * TPC + (th + 1) * TH],
                                 start=(ck == 0), stop=(ck == 15))
                    last = None
                    for ck in range(16):
                        last = t.matmul(ps1[:, 0:TH],
                                        wt3[wiB][:, ck * 128:(ck + 1) * 128],
                                        h1b[:, ck * TPC + th * TH:
                                            ck * TPC + (th + 1) * TH],
                                        start=(ck == 0), stop=(ck == 15))
                    return last
                pv = P.step("t", mm_sh, waits=(pv, wtok[wiA], wtok[wiB]))
                if th == 1:
                    wcons[wiA] = pv
                    wcons[wiB] = pv

                def ev_sh(a, th=th):
                    return a.activation(out=tmp1[:, 0:TH], in_=ps0[:, 0:TH],
                                        func=AF.Silu)
                w_sh = (pv,) + (t_h1s_done if hs == 0 and th == 0 else ())
                pv = P.step("a", ev_sh, waits=w_sh)

                def mul_sh(v, hs=hs, th=th):
                    return v.tensor_tensor(
                        actsh[:, hs * TPC + th * TH:hs * TPC + (th + 1) * TH],
                        tmp1[:, 0:TH], ps1[:, 0:TH], OP.mult)
                pv = P.step("v", mul_sh, waits=(pv,))

        setile = sbuf(nc, "setile", (128, 16), BF16)

        def ld_se(g):
            return [g.dma_start(out=setile[:, :], in_=seP)]
        t_se = P.step("g", ld_se, dma=True, n_dma=1, waits=(), sem_name="se")

        for th in range(2):
            def mm_se(t, th=th):
                last = None
                for ck in range(16):
                    last = t.matmul(ps0[0:1, 0:TH], setile[:, ck:ck + 1],
                                    h1b[:, ck * TPC + th * TH:
                                        ck * TPC + (th + 1) * TH],
                                    start=(ck == 0), stop=(ck == 15))
                return last
            pv = P.step("t", mm_se, waits=(pv, t_se))

            def ev_se(a, th=th):
                return a.activation(out=srow[:, th * TH:(th + 1) * TH],
                                    in_=ps0[0:1, 0:TH], func=AF.Sigmoid)
            pv = P.step("a", ev_se, waits=(pv,))

        for th in range(2):
            def bc_se(t, th=th):
                return t.matmul(psB[:, 0:TH], ones_row[:, :],
                                srow[:, th * TH:(th + 1) * TH],
                                start=True, stop=True)
            pv = P.step("t", bc_se, waits=(pv,))

            def ev_bse(v, th=th):
                return v.tensor_copy(sgb[:, th * TH:(th + 1) * TH],
                                     psB[:, 0:TH])
            pv = P.step("v", ev_bse, waits=(pv,))
        t_srow_free = pv

        for ck in range(16):
            wi_ = wload(lambda g, tile, ck=ck:
                        ldw_chunkcol(g, shdT, tile, ck * 128))

            for th in range(2):
                def mm_shd(t, ck=ck, th=th, wi_=wi_):
                    last = None
                    for hs in range(16):
                        last = t.matmul(ps0[:, 0:TH],
                                        wt3[wi_][:, hs * 128:(hs + 1) * 128],
                                        actsh[:, hs * TPC + th * TH:
                                              hs * TPC + (th + 1) * TH],
                                        start=(hs == 0), stop=(hs == 15))
                    return last
                pv = P.step("t", mm_shd, waits=(pv, wtok[wi_]))
                if th == 1:
                    wcons[wi_] = pv

                def ev_shd(v, ck=ck, th=th):
                    return v.tensor_tensor(
                        racc[:, ck * TPC + th * TH:ck * TPC + (th + 1) * TH],
                        ps0[:, 0:TH], sgb[:, th * TH:(th + 1) * TH], OP.mult)
                pv = P.step("v", ev_shd, waits=(pv,))

        for e in range(8):
            def ld_cb(g, e=e):
                return [g.dma_start(out=srow[:, :], in_=combscr[e:e + 1, :])]
            t_cb = P.step("a", ld_cb, dma=True, n_dma=1,
                          waits=(t_comb, t_srow_free), sem_name="cb")

            for th in range(2):
                def bc_cb(t, th=th):
                    return t.matmul(psB[:, 0:TH], ones_row[:, :],
                                    srow[:, th * TH:(th + 1) * TH],
                                    start=True, stop=True)
                pv = P.step("t", bc_cb, waits=(pv, t_cb))

                def ev_cbc(v, th=th):
                    return v.tensor_copy(cbsb[:, th * TH:(th + 1) * TH],
                                         psB[:, 0:TH])
                pv = P.step("v", ev_cbc, waits=(pv,))
            t_srow_free = pv

            for hc in range(8):
                wiA = wload(lambda g, tile, e=e, hc=hc:
                            [g.dma_start(out=tile[:, :].rearrange(
                                "p (ck c) -> p ck c", ck=16),
                                in_=wge[e * 2048:(e + 1) * 2048,
                                        hc * 128:(hc + 1) * 128].rearrange(
                                    "(ck p) c -> p ck c", p=128))])
                wiB = wload(lambda g, tile, e=e, hc=hc:
                            [g.dma_start(out=tile[:, :].rearrange(
                                "p (ck c) -> p ck c", ck=16),
                                in_=wue[e * 2048:(e + 1) * 2048,
                                        hc * 128:(hc + 1) * 128].rearrange(
                                    "(ck p) c -> p ck c", p=128))])

                for th in range(2):
                    def mm_eg(t, th=th, wiA=wiA, wiB=wiB):
                        for ck in range(16):
                            t.matmul(ps0[:, 0:TH],
                                     wt3[wiA][:, ck * 128:(ck + 1) * 128],
                                     h1b[:, ck * TPC + th * TH:
                                         ck * TPC + (th + 1) * TH],
                                     start=(ck == 0), stop=(ck == 15))
                        last = None
                        for ck in range(16):
                            last = t.matmul(ps1[:, 0:TH],
                                            wt3[wiB][:, ck * 128:(ck + 1) * 128],
                                            h1b[:, ck * TPC + th * TH:
                                                ck * TPC + (th + 1) * TH],
                                            start=(ck == 0), stop=(ck == 15))
                        return last
                    pv = P.step("t", mm_eg, waits=(pv, wtok[wiA], wtok[wiB]))
                    if th == 1:
                        wcons[wiA] = pv
                        wcons[wiB] = pv

                    def ev_eg(a, th=th):
                        return a.activation(out=tmp1[:, 0:TH],
                                            in_=ps0[:, 0:TH], func=AF.Silu)
                    pv = P.step("a", ev_eg, waits=(pv,))

                    def mul_eg(v, hc=hc, th=th):
                        v.tensor_tensor(tmp2[:, 0:TH], tmp1[:, 0:TH],
                                        ps1[:, 0:TH], OP.mult)
                        v.drain()
                        return v.tensor_tensor(
                            h1e[:, hc * TPC + th * TH:hc * TPC + (th + 1) * TH],
                            tmp2[:, 0:TH], cbsb[:, th * TH:(th + 1) * TH],
                            OP.mult)
                    pv = P.step("v", mul_eg, waits=(pv,))

            for ck in range(16):
                wi_ = wload(lambda g, tile, e=e, ck=ck:
                            [g.dma_start(out=tile[:, 0:8 * 128].rearrange(
                                "p (hk c) -> p hk c", hk=8),
                                in_=wde[e * 1024:(e + 1) * 1024,
                                        ck * 128:(ck + 1) * 128].rearrange(
                                    "(hk p) c -> p hk c", p=128))])

                for th in range(2):
                    def mm_ed(t, ck=ck, th=th, wi_=wi_):
                        last = None
                        for hk in range(8):
                            last = t.matmul(
                                ps0[:, 0:TH],
                                wt3[wi_][:, hk * 128:(hk + 1) * 128],
                                h1e[:, hk * TPC + th * TH:
                                    hk * TPC + (th + 1) * TH],
                                start=(hk == 0), stop=(hk == 7))
                        return last
                    pv = P.step("t", mm_ed, waits=(pv, wtok[wi_]))
                    if th == 1:
                        wcons[wi_] = pv

                    def acc_ed(v, ck=ck, th=th):
                        return v.tensor_tensor(
                            racc[:, ck * TPC + th * TH:
                                 ck * TPC + (th + 1) * TH],
                            racc[:, ck * TPC + th * TH:
                                 ck * TPC + (th + 1) * TH],
                            ps0[:, 0:TH], OP.add)
                    pv = P.step("v", acc_ed, waits=(pv,))

        def z_stats2(v):
            return v.memset(strows[:, :], 0.0)
        pv = P.step("v", z_stats2, waits=(pv,))

        for ck in range(16):
            xi_ = xload(lambda g, tile, ck=ck:
                        [g.dma_start(out=tile[:, :],
                                     in_=h1scr[ck * 128:(ck + 1) * 128, :])],
                        extra=t_h1s_done)

            def add_h1(v, ck=ck, xi_=xi_):
                return v.tensor_tensor(racc[:, ck * TPC:(ck + 1) * TPC],
                                       racc[:, ck * TPC:(ck + 1) * TPC],
                                       xb2[xi_][:, :], OP.add)
            pv = P.step("v", add_h1, waits=(pv, xtok[xi_]))
            xcons[xi_] = pv

            def sq_h2(a, ck=ck):
                return a.activation(out=tmp2[:, :],
                                    in_=racc[:, ck * TPC:(ck + 1) * TPC],
                                    func=AF.Square)
            pv = P.step("a", sq_h2, waits=(pv,))

            for th in range(2):
                def mm_st2(t, ck=ck, th=th):
                    t.matmul(psS[0:1, 0:TH], ones_col[:, :],
                             racc[:, ck * TPC + th * TH:
                                  ck * TPC + (th + 1) * TH],
                             start=True, stop=True)
                    return t.matmul(psB[0:1, 0:TH], ones_col[:, :],
                                    tmp2[:, th * TH:(th + 1) * TH],
                                    start=True, stop=True)
                pv = P.step("t", mm_st2, waits=(pv,))

                def acc_st2(v, th=th):
                    v.tensor_tensor(strows[:, th * TH:(th + 1) * TH],
                                    strows[:, th * TH:(th + 1) * TH],
                                    psS[0:1, 0:TH], OP.add)
                    v.drain()
                    return v.tensor_tensor(
                        strows[:, TPC + th * TH:TPC + (th + 1) * TH],
                        strows[:, TPC + th * TH:TPC + (th + 1) * TH],
                        psB[0:1, 0:TH], OP.add)
                pv = P.step("v", acc_st2, waits=(pv,))

        pv = P.step("v", ln_scalars, waits=(pv,))
        pv = P.step("a", rstd_a, waits=(pv,))
        pv = P.step("v", rstd_v, waits=(pv,))
        tsp = [None, None]
        pv = emit_ln_apply(pv, None, None, h2T_o, "hs", h2mode=True)
        t_h2s_done = (tsp[0], tsp[1])

        h2bs = h1b

        def z_stats3(v):
            return v.memset(strows[:, :], 0.0)
        pv = P.step("v", z_stats3, waits=(pv,))

        def ld_h2b(g):
            return [g.dma_start(out=h2bs[:, :].rearrange(
                "p (ck t) -> p ck t", ck=16),
                in_=h2T_o.rearrange("(ck p) t -> p ck t", p=128))]
        t_h2b = P.step("g", ld_h2b, dma=True, n_dma=1,
                       waits=t_h2s_done + (pv,), sem_name="h2b")

        tckv = [tsp[0], tsp[1]]
        for lc in range(4):
            wi_ = wload(lambda g, tile, lc=lc:
                        ldw_chunkcol(g, kvdw, tile, lc * 128))
            cp = lc % 2

            for th in range(2):
                def mm_kv(t, lc=lc, th=th, wi_=wi_):
                    last = None
                    for ck in range(16):
                        last = t.matmul(ps0[:, 0:TH],
                                        wt3[wi_][:, ck * 128:(ck + 1) * 128],
                                        h2bs[:, ck * TPC + th * TH:
                                             ck * TPC + (th + 1) * TH],
                                        start=(ck == 0), stop=(ck == 15))
                    return last
                pv = P.step("t", mm_kv, waits=(pv, wtok[wi_], t_h2b))
                if th == 1:
                    wcons[wi_] = pv

                def ev_kv(v, th=th, cp=cp):
                    return v.tensor_copy(t1p[cp][:, th * TH:(th + 1) * TH],
                                         ps0[:, 0:TH])
                pv = P.step("v", ev_kv, waits=(pv, tckv[cp]))

            def st_ckvp(s, lc=lc, cp=cp):
                return [s.dma_start(out=ckvscr[lc * 128:(lc + 1) * 128, :],
                                    in_=t1p[cp][:, :])]
            tckv[cp] = P.step("s", st_ckvp, dma=True, n_dma=1, waits=(pv,),
                              sem_name="cs%d" % cp)

            def sq_kv(a, lc=lc, cp=cp):
                return a.activation(out=tmp2[:, :], in_=t1p[cp][:, :],
                                    func=AF.Square)
            pv = P.step("a", sq_kv, waits=(pv,))

            for th in range(2):
                def mm_stkv(t, th=th):
                    return t.matmul(psB[0:1, 0:TH], ones_col[:, :],
                                    tmp2[:, th * TH:(th + 1) * TH],
                                    start=True, stop=True)
                pv = P.step("t", mm_stkv, waits=(pv,))

                def acc_stkv(v, th=th):
                    return v.tensor_tensor(
                        strows[:, th * TH:(th + 1) * TH],
                        strows[:, th * TH:(th + 1) * TH],
                        psB[0:1, 0:TH], OP.add)
                pv = P.step("v", acc_stkv, waits=(pv,))

        def rms_kv(v):
            return v.tensor_scalar(strows[:, 0:TPC], strows[:, 0:TPC],
                                   1.0 / 512.0, 1e-6, OP.mult, OP.add)
        pv = P.step("v", rms_kv, waits=(pv,))

        def rms_sqrt(a):
            return a.activation(out=strows[:, TPC:2 * TPC],
                                in_=strows[:, 0:TPC], func=AF.Sqrt)
        pv = P.step("a", rms_sqrt, waits=(pv,))

        def rms_rec(v):
            return v.reciprocal(strows[:, 0:TPC], strows[:, TPC:2 * TPC])
        pv = P.step("v", rms_rec, waits=(pv,))

        for th in range(2):
            def bc_rms(t, th=th):
                return t.matmul(psB[:, 0:TH], ones_row[:, :],
                                strows[:, th * TH:(th + 1) * TH],
                                start=True, stop=True)
            pv = P.step("t", bc_rms, waits=(pv,))

            def ev_brms(v, th=th):
                return v.tensor_copy(tmp2[:, th * TH:(th + 1) * TH],
                                     psB[:, 0:TH])
            pv = P.step("v", ev_brms, waits=(pv,))

        for lc in range(4):
            xi_ = xload(lambda g, tile, lc=lc:
                        [g.dma_start(out=tile[:, :],
                                     in_=ckvscr[lc * 128:(lc + 1) * 128, :])],
                        extra=(tckv[0], tckv[1]))
            cp = lc % 2

            def ap_kv(v, lc=lc, xi_=xi_, cp=cp):
                return v.tensor_tensor(t1p[cp][:, :], xb2[xi_][:, :],
                                       tmp2[:, :], OP.mult)
            pv = P.step("v", ap_kv, waits=(pv, xtok[xi_], tckv[cp]))
            xcons[xi_] = pv

            def st_kv(s, lc=lc, cp=cp):
                return [s.dma_start(out=ckvT_o[lc * 128:(lc + 1) * 128, :],
                                    in_=t1p[cp][:, :])]
            tckv[cp] = P.step("s", st_kv, dma=True, n_dma=1, waits=(pv,),
                              sem_name="co%d" % cp)

        P.final_wait()
    return P


def prep_l0rest_inputs(inputs, og_full):
    consts = _consts_a2()
    woT = np.ascontiguousarray(inputs["L0_attn_Wo"].T, dtype=np.float32)
    gwT = np.ascontiguousarray(inputs["L0_ffn_gate_w"].T, dtype=np.float32)
    wge = np.ascontiguousarray(
        inputs["L0_ffn_w_gate"].transpose(0, 2, 1).reshape(8 * 2048, 1024),
        dtype=np.float32)
    wue = np.ascontiguousarray(
        inputs["L0_ffn_w_up"].transpose(0, 2, 1).reshape(8 * 2048, 1024),
        dtype=np.float32)
    wde = np.ascontiguousarray(
        inputs["L0_ffn_w_down"].transpose(0, 2, 1).reshape(8 * 1024, 2048),
        dtype=np.float32)
    shgT = np.ascontiguousarray(inputs["L0_ffn_sh_gate"].T, dtype=np.float32)
    shuT = np.ascontiguousarray(inputs["L0_ffn_sh_up"].T, dtype=np.float32)
    shdT = np.ascontiguousarray(inputs["L0_ffn_sh_down"].T, dtype=np.float32)
    seP = np.ascontiguousarray(
        inputs["L0_ffn_se_gate_w"].reshape(16, 128).T, dtype=np.float32)
    kvdw = np.ascontiguousarray(inputs["L1_attn_w_kv_down"], dtype=np.float32)
    x = np.asarray(inputs["L0_attn_x"], dtype=np.float32)
    per_core = []
    for c in range(NCORES):
        b, r = c // 2, c % 2
        d = {
            "ogf": np.ascontiguousarray(og_full[b][:, r * TPC:(r + 1) * TPC]),
            "xT": np.ascontiguousarray(x[b].T[:, r * TPC:(r + 1) * TPC]),
            "woT": woT, "gwT": gwT, "wge": wge, "wue": wue, "wde": wde,
            "shgT": shgT, "shuT": shuT, "shdT": shdT, "seP": seP,
            "kvdw": kvdw,
        }
        for k, v in consts.items():
            d["c_" + k] = v
        per_core.append(d)
    return per_core



def _consts_a3():
    c = _consts_a2()
    ROPE = 64
    inv_freq = 10000.0 ** (-np.arange(0, ROPE, 2, dtype=np.float32) / ROPE)
    t = np.arange(T, dtype=np.float32)
    fr = np.outer(t, inv_freq)
    c["cosT"] = np.ascontiguousarray(np.cos(fr).T)
    c["sinT"] = np.ascontiguousarray(np.sin(fr).T)
    KBW = min(512, T)
    for j in range(KBW // 128):
        m = np.zeros((128, KBW), np.float32)
        for i in range(128):
            m[i, 128 * j + i + 1:] = -1e30
        c[f"cmask{j}"] = m
    sel = np.zeros((2, 128), np.float32)
    sel[0, 0:64] = 1.0
    sel[1, 64:128] = 1.0
    c["sel2"] = sel
    g2 = np.zeros((128, 2), np.float32)
    g2[0:64, 0] = 1.0
    g2[64:128, 1] = 1.0
    c["g2"] = g2
    return c


def build_l1attn():
    P = Prog("l1attn")
    nc = P.nc
    PTW = min(1024, T)
    KBW = min(512, T)
    h2Tf = nc.dram_tensor("h2Tf", [2048, T], F32, kind="ExternalInput").ap()
    ckvTf = nc.dram_tensor("ckvTf", [512, T], F32, kind="ExternalInput").ap()
    wqh = nc.dram_tensor("wqh", [2048, 8 * 192], F32, kind="ExternalInput").ap()
    wkvh = nc.dram_tensor("wkvh", [512, 8 * 192], F32, kind="ExternalInput").ap()
    qgain = nc.dram_tensor("qgain", [1, 8], F32, kind="ExternalInput").ap()
    cst = {k: nc.dram_tensor("c_" + k, list(v.shape), F32,
                             kind="ExternalInput").ap()
           for k, v in _consts_a3().items()}
    og2 = nc.dram_tensor("og2", [1024, T], BF16, kind="ExternalOutput").ap()

    qT = sbuf(nc, "qT", (128, 8 * T), BF16)
    kfull = sbuf(nc, "kfull", (128, 8 * T), BF16)
    vtok = sbuf(nc, "vtok", (128, 8 * T), BF16)
    cosT = sbuf(nc, "cosT", (32, T), F32)
    sinT = sbuf(nc, "sinT", (32, T), F32)
    idf = sbuf(nc, "idf_sb", (128, 128), F32)
    idb = sbuf(nc, "idb_sb", (128, 128), BF16)
    ones_col = sbuf(nc, "ones_sb", (128, 1), F32)
    ones_row = sbuf(nc, "onesr_sb", (1, 128), F32)
    sel2 = sbuf(nc, "sel2_sb", (2, 128), F32)
    g2c = sbuf(nc, "g2_sb", (128, 2), F32)
    cmask = sbuf(nc, "cmask_sb", (128, (KBW // 128) * KBW), F32)
    gcol = sbuf(nc, "gcol", (128, 8), F32)
    trow = sbuf(nc, "trow", (2, T), F32)
    stat = sbuf(nc, "stat", (128, 16), F32)
    PTb = sbuf(nc, "PTb", (128, 4 * 128), BF16)
    Obuf = sbuf(nc, "Obuf", (128, 2 * 128), BF16)

    ps0 = psum(nc, "ps0")
    ps1 = psum(nc, "ps1")
    ps2 = psum(nc, "ps2")
    psO = psum(nc, "psO")
    psB = psum(nc, "psB")
    psN = psum(nc, "psN", BF16, cols=128)
    psN2 = psum(nc, "psN2", BF16, cols=128)

    NTT = T // PTW
    SPW = min(512, PTW)
    PTB = min(512, T)
    NTB = T // PTB

    from contextlib import ExitStack
    es1 = ExitStack()
    htile = es1.enter_context(nc.sbuf_tensor("htile", [128, 16 * PTW], BF16))
    ctile = es1.enter_context(nc.sbuf_tensor("ctile", [128, 4 * PTW], BF16))
    wth0 = es1.enter_context(nc.sbuf_tensor("wth0", [128, 16 * 192], BF16))
    wth1 = es1.enter_context(nc.sbuf_tensor("wth1", [128, 16 * 192], BF16))
    wth2 = es1.enter_context(nc.sbuf_tensor("wth2", [128, 16 * 192], BF16))
    wt3 = [wth0, wth1, wth2]

    with P.block():
        wstate = {"i": 0}
        wcons = [None, None, None]
        wtok = [None, None, None]

        def wload(mkf, n_dma=1):
            i = wstate["i"] % 3

            def fn(eng, mkf=mkf, tile=wt3[i]):
                return mkf(eng, tile)
            wtok[i] = P.step("g", fn, dma=True, n_dma=n_dma,
                             waits=(wcons[i],), sem_name="wt%d" % i)
            wstate["i"] += 1
            return i

        def ld_c(g):
            out = [g.dma_start(out=cosT[:, :], in_=cst["cosT"]),
                   g.dma_start(out=sinT[:, :], in_=cst["sinT"]),
                   g.dma_start(out=idf[:, :], in_=cst["idf"]),
                   g.dma_start(out=idb[:, :], in_=cst["idf"]),
                   g.dma_start(out=ones_col[:, :], in_=cst["ones_col"]),
                   g.dma_start(out=ones_row[:, :], in_=cst["ones_row"]),
                   g.dma_start(out=sel2[:, :], in_=cst["sel2"]),
                   g.dma_start(out=g2c[:, :], in_=cst["g2"]),
                   g.dma_start(out=gcol[0:1, 0:8], in_=qgain)]
            for j in range(KBW // 128):
                out.append(g.dma_start(out=cmask[:, j * KBW:(j + 1) * KBW],
                                       in_=cst[f"cmask{j}"]))
            return out
        t_ldc = P.step("g", ld_c, dma=True, n_dma=9 + KBW // 128, waits=(),
                       sem_name="ldc")
        pv = t_ldc

        t_ht = None
        t_ct = None
        for tt in range(NTT):
            def ld_h2(g, tt=tt):
                return [g.dma_start(out=htile[:, 0:16 * PTW].rearrange(
                    "p (ck t) -> p ck t", ck=16),
                    in_=h2Tf[:, tt * PTW:(tt + 1) * PTW].rearrange(
                        "(ck p) t -> p ck t", p=128))]
            t_ht = P.step("g", ld_h2, dma=True, n_dma=1,
                          waits=(t_ht, pv if tt > 0 else None),
                          sem_name="ht")

            for h in range(8):
                wi_ = wload(lambda g, tile, h=h: [g.dma_start(
                    out=tile[:, :].rearrange("p (ck c) -> p ck c", ck=16),
                    in_=wqh[:, h * 192:(h + 1) * 192].rearrange(
                        "(ck p) c -> p ck c", p=128))])

                for sp in range(PTW // SPW):
                    def mm_qh(t, tt=tt, h=h, wi_=wi_, sp=sp):
                        for ck in range(16):
                            t.matmul(ps0[:, 0:SPW],
                                     wt3[wi_][:, ck * 192:ck * 192 + 128],
                                     htile[:, ck * PTW + sp * SPW:
                                           ck * PTW + (sp + 1) * SPW],
                                     start=(ck == 0), stop=(ck == 15))
                        last = None
                        for ck in range(16):
                            last = t.matmul(
                                ps1[0:64, 0:SPW],
                                wt3[wi_][:, ck * 192 + 128:ck * 192 + 192],
                                htile[:, ck * PTW + sp * SPW:
                                      ck * PTW + (sp + 1) * SPW],
                                start=(ck == 0), stop=(ck == 15))
                        return last
                    pv = P.step("t", mm_qh, waits=(pv, wtok[wi_], t_ht))
                    if sp == PTW // SPW - 1:
                        wcons[wi_] = pv

                    def ev_q(v, tt=tt, h=h, sp=sp):
                        c0 = h * T + tt * PTW + sp * SPW
                        v.tensor_copy(qT[:, c0:c0 + SPW], ps0[:, 0:SPW])
                        v.drain()
                        return v.tensor_copy(kfull[64:128, c0:c0 + SPW],
                                             ps1[0:64, 0:SPW])
                    pv = P.step("v", ev_q, waits=(pv,))

        for tt in range(NTT):
            def ld_ckv2(g, tt=tt):
                return [g.dma_start(out=ctile[:, 0:4 * PTW].rearrange(
                    "p (ck t) -> p ck t", ck=4),
                    in_=ckvTf[:, tt * PTW:(tt + 1) * PTW].rearrange(
                        "(ck p) t -> p ck t", p=128))]
            t_ct = P.step("g", ld_ckv2, dma=True, n_dma=1,
                          waits=(t_ct, pv if tt > 0 else None),
                          sem_name="ct")

            for h in range(8):
                wi_ = wload(lambda g, tile, h=h: [g.dma_start(
                    out=tile[:, 0:4 * 192].rearrange(
                        "p (ck c) -> p ck c", ck=4),
                    in_=wkvh[:, h * 192:(h + 1) * 192].rearrange(
                        "(ck p) c -> p ck c", p=128))])

                for sp in range(PTW // SPW):
                    def mm_kn(t, tt=tt, h=h, wi_=wi_, sp=sp):
                        last = None
                        for ck in range(4):
                            last = t.matmul(
                                ps0[0:64, 0:SPW],
                                wt3[wi_][:, ck * 192:ck * 192 + 64],
                                ctile[:, ck * PTW + sp * SPW:
                                      ck * PTW + (sp + 1) * SPW],
                                start=(ck == 0), stop=(ck == 3))
                        return last
                    pv = P.step("t", mm_kn, waits=(pv, wtok[wi_], t_ct))

                    def ev_kn(v, tt=tt, h=h, sp=sp):
                        c0 = h * T + tt * PTW + sp * SPW
                        return v.tensor_copy(kfull[0:64, c0:c0 + SPW],
                                             ps0[0:64, 0:SPW])
                    pv = P.step("v", ev_kn, waits=(pv,))

                for st4 in range(PTW // 128):
                    def mm_v(t, tt=tt, h=h, st4=st4, wi_=wi_):
                        last = None
                        for ck in range(4):
                            last = t.matmul(
                                ps1[:, 0:128],
                                ctile[:, ck * PTW + st4 * 128:
                                      ck * PTW + st4 * 128 + 128],
                                wt3[wi_][:, ck * 192 + 64:ck * 192 + 192],
                                start=(ck == 0), stop=(ck == 3))
                        return last
                    pv = P.step("t", mm_v, waits=(pv,))
                    if st4 == PTW // 128 - 1:
                        wcons[wi_] = pv

                    def ev_v(v, tt=tt, h=h, st4=st4):
                        ti = tt * (PTW // 128) + st4
                        return v.tensor_copy(
                            vtok[:, h * T + ti * 128:h * T + (ti + 1) * 128],
                            ps1[:, 0:128])
                    pv = P.step("v", ev_v, waits=(pv,))

        es1.close()
        es2 = ExitStack()
        Sstr = es2.enter_context(nc.sbuf_tensor("Sstr", [128, T], F32))
        Sstr2 = es2.enter_context(nc.sbuf_tensor("Sstr2", [128, T], F32))
        Pbf = es2.enter_context(nc.sbuf_tensor("Pbf", [128, T], BF16))
        Pbf2 = es2.enter_context(nc.sbuf_tensor("Pbf2", [128, T], BF16))
        tmp64 = es2.enter_context(nc.sbuf_tensor("tmp64", [64, T], F32))
        tmp64b = es2.enter_context(nc.sbuf_tensor("tmp64b", [64, T], F32))
        t32a = es2.enter_context(nc.sbuf_tensor("t32a", [32, T], F32))
        t32b = es2.enter_context(nc.sbuf_tensor("t32b", [32, T], F32))
        S2 = [Sstr, Sstr2]
        P2 = [Pbf, Pbf2]

        for h in range(8):
            hs = lambda tens, h=h: tens[:, h * T:(h + 1) * T]

            def sq_q(a, h=h):
                return a.activation(out=Sstr[:, :], in_=hs(qT), func=AF.Square)
            pv = P.step("a", sq_q, waits=(pv,))

            for tb in range(NTB):
                def st_q2(t, tb=tb):
                    return t.matmul(ps0[0:2, 0:PTB], g2c[:, :],
                                    Sstr[:, tb * PTB:(tb + 1) * PTB],
                                    start=True, stop=True)
                pv = P.step("t", st_q2, waits=(pv,))

                def acc_q(v, tb=tb):
                    return v.tensor_copy(trow[:, tb * PTB:(tb + 1) * PTB],
                                         ps0[0:2, 0:PTB])
                pv = P.step("v", acc_q, waits=(pv,))

            def rstd_q(v):
                return v.tensor_scalar(trow[:, :], trow[:, :], 1.0 / 64.0,
                                       1e-6, OP.mult, OP.add)
            pv = P.step("v", rstd_q, waits=(pv,))

            def rsq_q(a):
                return a.activation(out=Sstr[0:2, 0:T], in_=trow[:, :],
                                    func=AF.Sqrt)
            pv = P.step("a", rsq_q, waits=(pv,))

            def rrec_q(v):
                return v.reciprocal(trow[:, :], Sstr[0:2, 0:T])
            pv = P.step("v", rrec_q, waits=(pv,))

            for tb in range(NTB):
                def bc_q(t, tb=tb):
                    return t.matmul(psB[:, 0:PTB], sel2[:, :],
                                    trow[:, tb * PTB:(tb + 1) * PTB],
                                    start=True, stop=True)
                pv = P.step("t", bc_q, waits=(pv,))

                def ap_q(v, h=h, tb=tb):
                    return v.tensor_tensor(
                        hs(qT)[:, tb * PTB:(tb + 1) * PTB],
                        hs(qT)[:, tb * PTB:(tb + 1) * PTB],
                        psB[:, 0:PTB], OP.mult)
                pv = P.step("v", ap_q, waits=(pv,))

            def cp_qr(v, h=h):
                return v.tensor_copy(tmp64[:, :], hs(qT)[64:128, :])
            pv = P.step("v", cp_qr, waits=(pv,))

            def rope_q(v, h=h):
                v.tensor_copy(t32a[:, :], tmp64[32:64, :])
                v.drain()
                v.tensor_tensor(tmp64b[0:32, :], tmp64[0:32, :], cosT[:, :],
                                OP.mult)
                v.drain()
                v.tensor_tensor(t32b[:, :], t32a[:, :], sinT[:, :], OP.mult)
                v.drain()
                v.tensor_tensor(tmp64b[0:32, :], tmp64b[0:32, :], t32b[:, :],
                                OP.add)
                v.drain()
                v.tensor_tensor(t32b[:, :], t32a[:, :], cosT[:, :], OP.mult)
                v.drain()
                v.tensor_tensor(Sstr[0:32, 0:T], tmp64[0:32, :], sinT[:, :],
                                OP.mult)
                v.drain()
                v.tensor_tensor(t32b[:, :], t32b[:, :], Sstr[0:32, 0:T],
                                OP.subtract)
                v.drain()
                return v.tensor_copy(tmp64b[32:64, :], t32b[:, :])
            pv = P.step("v", rope_q, waits=(pv,))

            def gain_bc(t, h=h):
                return t.matmul(ps2[:, 0:1], ones_row[:, :],
                                gcol[0:1, h:h + 1], start=True, stop=True)
            pv = P.step("t", gain_bc, waits=(pv,))

            def ev_gain(v):
                return v.tensor_copy(stat[:, 0:1], ps2[:, 0:1])
            pv = P.step("v", ev_gain, waits=(pv,))

            def wb_qr(v, h=h):
                v.tensor_scalar(tmp64b[:, :], tmp64b[:, :], stat[0:64, 0:1],
                                None, OP.mult)
                v.drain()
                return v.tensor_copy(hs(qT)[64:128, :], tmp64b[:, :])
            pv = P.step("v", wb_qr, waits=(pv,))

            def gain_nope(v, h=h):
                return v.tensor_scalar(hs(qT)[0:64, :], hs(qT)[0:64, :],
                                       stat[0:64, 0:1], None, OP.mult)
            pv = P.step("v", gain_nope, waits=(pv,))

            def sq_k(a, h=h):
                return a.activation(out=Sstr[:, :], in_=hs(kfull),
                                    func=AF.Square)
            pv = P.step("a", sq_k, waits=(pv,))
            for tb in range(NTB):
                def st_k2(t, tb=tb):
                    return t.matmul(ps0[0:2, 0:PTB], g2c[:, :],
                                    Sstr[:, tb * PTB:(tb + 1) * PTB],
                                    start=True, stop=True)
                pv = P.step("t", st_k2, waits=(pv,))

                def acc_k(v, tb=tb):
                    return v.tensor_copy(trow[:, tb * PTB:(tb + 1) * PTB],
                                         ps0[0:2, 0:PTB])
                pv = P.step("v", acc_k, waits=(pv,))
            pv = P.step("v", rstd_q, waits=(pv,))
            pv = P.step("a", rsq_q, waits=(pv,))
            pv = P.step("v", rrec_q, waits=(pv,))
            for tb in range(NTB):
                def bc_k(t, tb=tb):
                    return t.matmul(psB[:, 0:PTB], sel2[:, :],
                                    trow[:, tb * PTB:(tb + 1) * PTB],
                                    start=True, stop=True)
                pv = P.step("t", bc_k, waits=(pv,))

                def ap_k(v, h=h, tb=tb):
                    return v.tensor_tensor(
                        hs(kfull)[:, tb * PTB:(tb + 1) * PTB],
                        hs(kfull)[:, tb * PTB:(tb + 1) * PTB],
                        psB[:, 0:PTB], OP.mult)
                pv = P.step("v", ap_k, waits=(pv,))

            def cp_kr(v, h=h):
                return v.tensor_copy(tmp64[:, :], hs(kfull)[64:128, :])
            pv = P.step("v", cp_kr, waits=(pv,))
            pv = P.step("v", rope_q, waits=(pv,))

            def wb_kr(v, h=h):
                return v.tensor_copy(hs(kfull)[64:128, :], tmp64b[:, :])
            pv = P.step("v", wb_kr, waits=(pv,))

        t_norms = pv

        SC = 1.0 / float(np.sqrt(128.0))
        NQ = T // 128
        psS = [ps0, ps2]
        psPT = [psN, psN2]
        psOv = [psO, psB]
        jobs = [(h, qt) for h in range(8) for qt in range(NQ)]
        NJ = len(jobs)
        t_evs_last = [t_norms, t_norms]
        t_sexp = [t_norms, t_norms]
        t_trp_last = [t_norms, t_norms]
        t_ssum = [None, None]
        t_evp = {}
        t_evtr = {}
        t_mmpv = {}
        t_evo = [t_norms, t_norms]
        t_sto = [None, None]
        t_smax = [None, None]
        svals = {}

        def emit_S_mm(j):
            h, qt = jobs[j]
            j2 = j % 2
            kend = (qt + 1) * 128
            nkb = (kend + KBW - 1) // KBW
            t_ev = None
            for kb in range(nkb):
                def mm_s(t, h=h, qt=qt, kb=kb, j2=j2):
                    return t.matmul(
                        psS[j2][:, 0:KBW],
                        qT[:, h * T + qt * 128:h * T + (qt + 1) * 128],
                        kfull[:, h * T + kb * KBW:h * T + kb * KBW + KBW],
                        start=True, stop=True)
                w_ = (t_norms, t_ev if t_ev else t_evs_last[j2],
                      t_sexp[j2])
                tok_mm = P.step("t", mm_s, waits=w_)

                last_blk = (kb == nkb - 1)

                def ev_s(v, qt=qt, kb=kb, j2=j2, last_blk=last_blk):
                    if last_blk:
                        jj = qt % (KBW // 128)
                        return v.scalar_tensor_tensor(
                            S2[j2][:, kb * KBW:kb * KBW + KBW],
                            psS[j2][:, 0:KBW], SC,
                            cmask[:, jj * KBW:(jj + 1) * KBW],
                            OP.mult, OP.add)
                    return v.tensor_scalar(
                        S2[j2][:, kb * KBW:kb * KBW + KBW],
                        psS[j2][:, 0:KBW], SC, None, OP.mult)
                t_ev = P.step("v", ev_s, waits=(tok_mm, t_sexp[j2]))
            t_evs_last[j2] = t_ev

        def emit_S_soft(j):
            h, qt = jobs[j]
            j2 = j % 2
            kend = (qt + 1) * 128
            nkb = (kend + KBW - 1) // KBW
            t_ev = t_evs_last[j2]
            kw_all = nkb * KBW

            def smax(v, kw_all=kw_all, j2=j2):
                v.reduce_max(stat[:, j2 * 8:j2 * 8 + 1], S2[j2][:, 0:kw_all],
                             axis=AX.X)
                v.drain()
                return v.tensor_scalar(stat[:, j2 * 8 + 1:j2 * 8 + 2],
                                       stat[:, j2 * 8:j2 * 8 + 1], -1.0,
                                       None, OP.mult)
            t_smax[j2] = P.step("v", smax, waits=(t_ev,))

            def sexp(a, kw_all=kw_all, j2=j2):
                return a.activation(out=P2[j2][:, 0:kw_all],
                                    in_=S2[j2][:, 0:kw_all], func=AF.Exp,
                                    bias=stat[:, j2 * 8 + 1:j2 * 8 + 2])
            t_sexp[j2] = P.step("a", sexp,
                                waits=(t_smax[j2], t_trp_last[j2]))

            def ssum(v, kw_all=kw_all, j2=j2):
                v.reduce_sum(stat[:, j2 * 8 + 2:j2 * 8 + 3],
                             P2[j2][:, 0:kw_all], axis=AX.X)
                v.drain()
                v.reciprocal(stat[:, j2 * 8 + 3:j2 * 8 + 4],
                             stat[:, j2 * 8 + 2:j2 * 8 + 3])
                v.drain()
                return v.tensor_scalar(P2[j2][:, 0:kw_all],
                                       P2[j2][:, 0:kw_all],
                                       stat[:, j2 * 8 + 3:j2 * 8 + 4],
                                       None, OP.mult)
            t_ssum[j2] = P.step("v", ssum, waits=(t_sexp[j2],))

        def emit_PV(j):
            h, qt = jobs[j]
            j2 = j % 2

            def mk_tr(kb):
                kp = kb % 2

                def tr_p(t, kb=kb, kp=kp):
                    return t.transpose(psPT[kp][:, 0:128],
                                       P2[j2][:, kb * 128:(kb + 1) * 128],
                                       idb[:, :])
                return P.step("t", tr_p, waits=(t_ssum[j2], t_evtr.get(kp)))

            t_trs = {0: mk_tr(0)}
            for kb in range(qt + 1):
                kp = kb % 2
                if kb + 1 <= qt:
                    t_trs[kb + 1] = mk_tr(kb + 1)

                def ev_p(v, kb=kb, j2=j2, kp=kp):
                    return v.tensor_copy(PTb[:, (j2 * 2 + kp) * 128:
                                              (j2 * 2 + kp + 1) * 128],
                                         psPT[kp][:, 0:128])
                t_evtr[kp] = t_evp[(j2, kp)] = P.step(
                    "v", ev_p, waits=(t_trs[kb], t_mmpv.get((j2, kp))))

                def mm_pv(t, h=h, qt=qt, kb=kb, j2=j2, kp=kp):
                    return t.matmul(psOv[j2][:, 0:128],
                                    vtok[:, h * T + kb * 128:
                                         h * T + (kb + 1) * 128],
                                    PTb[:, (j2 * 2 + kp) * 128:
                                        (j2 * 2 + kp + 1) * 128],
                                    start=(kb == 0), stop=(kb == qt))
                t_mmpv[(j2, kp)] = P.step(
                    "t", mm_pv,
                    waits=(t_evp[(j2, kp)],) +
                    ((t_evo[j2],) if kb == 0 else ()))
            t_trp_last[j2] = t_trs[qt]

            def ev_o(v, j2=j2):
                return v.tensor_copy(Obuf[:, j2 * 128:(j2 + 1) * 128],
                                     psOv[j2][:, 0:128])
            t_evo[j2] = P.step("v", ev_o,
                               waits=(t_mmpv[(j2, qt % 2)], t_sto[j2]))

            def st_o(s, h=h, qt=qt, j2=j2):
                return [s.dma_start(
                    out=og2[h * 128:(h + 1) * 128,
                            qt * 128:(qt + 1) * 128],
                    in_=Obuf[:, j2 * 128:(j2 + 1) * 128])]
            t_sto[j2] = P.step("s", st_o, dma=True, n_dma=1,
                               waits=(t_evo[j2],), sem_name="so%d" % j2)

        emit_S_mm(0)
        emit_S_soft(0)
        for j in range(1, NJ):
            emit_S_mm(j)
            emit_PV(j - 1)
            emit_S_soft(j)
        emit_PV(NJ - 1)

        P.final_wait()
    return P


def prep_l1attn_inputs(inputs, h2T_full, ckvT_full):
    consts = _consts_a3()
    wqk = np.asarray(inputs["L1_attn_w_q_krope"], dtype=np.float32)
    wkvu = np.asarray(inputs["L1_attn_w_kv_up"], dtype=np.float32)
    qg = np.asarray(inputs["L1_attn_q_gain"], dtype=np.float32)
    per_core = []
    for c in range(NCORES):
        b, r = c // 2, c % 2
        hlo = 8 * r
        wq_cols = []
        wkv_cols = []
        for h in range(8):
            hg = hlo + h
            wq_cols.append(wqk[:, hg * 128:(hg + 1) * 128])
            wq_cols.append(wqk[:, 2048 + hg * 64:2048 + (hg + 1) * 64])
            wkv_cols.append(wkvu[:, hg * 64:(hg + 1) * 64])
            wkv_cols.append(wkvu[:, 1024 + hg * 128:1024 + (hg + 1) * 128])
        d = {
            "h2Tf": h2T_full[b],
            "ckvTf": ckvT_full[b],
            "wqh": np.ascontiguousarray(np.concatenate(wq_cols, axis=1)),
            "wkvh": np.ascontiguousarray(np.concatenate(wkv_cols, axis=1)),
            "qgain": np.ascontiguousarray(qg[hlo:hlo + 8].reshape(1, 8)),
        }
        for k, v in consts.items():
            d["c_" + k] = v
        per_core.append(d)
    return per_core



def _consts_a4():
    c = _consts_a2()
    c["iota64"] = np.tile(np.arange(64, dtype=np.float32), (128, 1))
    c["dumpb"] = np.tile(8192.0 + np.arange(6, dtype=np.float32), (128, 1))
    c["riota64"] = np.tile(np.arange(63, -1, -1, dtype=np.float32), (128, 1))
    triu = np.triu(np.ones((128, 128), np.float32), 1)
    c["triu128"] = triu
    return c


def build_l1mid():
    P = Prog("l1mid")
    nc = P.nc
    TH = TPC // 2
    og2s = nc.dram_tensor("og2s", [2048, TPC], BF16, kind="ExternalInput").ap()
    h2Ts = nc.dram_tensor("h2Ts", [2048, TPC], F32, kind="ExternalInput").ap()
    wproj = nc.dram_tensor("wproj", [2048, 2048], F32, kind="ExternalInput").ap()
    wgate = nc.dram_tensor("wgate", [2048, 64], F32, kind="ExternalInput").ap()
    wshu = nc.dram_tensor("wshu", [2048, 1024], F32, kind="ExternalInput").ap()
    wshd = nc.dram_tensor("wshd", [512, 2048], F32, kind="ExternalInput").ap()
    cst = {k: nc.dram_tensor("c_" + k, list(v.shape), F32,
                             kind="ExternalInput").ap()
           for k, v in _consts_a4().items()}
    h3T_o = nc.dram_tensor("h3T", [2048, TPC], F32, kind="ExternalOutput").ap()
    h3row_o = nc.dram_tensor("h3row", [TPC, 2048], BF16,
                             kind="ExternalOutput").ap()
    shT_o = nc.dram_tensor("shT", [2048, TPC], F32, kind="ExternalOutput").ap()
    e6_o = nc.dram_tensor("e6", [TPC, 6], F32, kind="ExternalOutput").ap()
    w6_o = nc.dram_tensor("w6", [TPC, 6], F32, kind="ExternalOutput").ap()
    p6_o = nc.dram_tensor("p6", [TPC, 6], F32, kind="ExternalOutput").ap()
    cnt_o = nc.dram_tensor("cnt", [1, 64], F32, kind="ExternalOutput").ap()
    h3scr = nc.dram_tensor("h3scr", [2048, TPC], F32).ap()

    h3b = sbuf(nc, "h3b", (128, 16 * TPC), BF16)
    idf = sbuf(nc, "idf_sb", (128, 128), F32)
    ones_col = sbuf(nc, "ones_sb", (128, 1), F32)
    ones_row = sbuf(nc, "onesr_sb", (1, 128), F32)
    iota64 = sbuf(nc, "iota64_sb", (128, 64), F32)
    riota64 = sbuf(nc, "riota64_sb", (128, 64), F32)
    triu128 = sbuf(nc, "triu_sb", (128, 128), F32)
    wtile = sbuf(nc, "wtile", (128, 16 * 128), BF16)
    xck = sbuf(nc, "xck", (128, TPC), F32)
    tmp1 = sbuf(nc, "tmp1", (128, TPC), F32)
    tmp2 = sbuf(nc, "tmp2", (128, TPC), F32)
    strows = sbuf(nc, "strows", (1, 3 * TPC), F32)
    actsu = sbuf(nc, "actsu", (128, 4 * TPC), BF16)
    rowt = sbuf(nc, "rowt", (128, 16 * 128), BF16)
    s64 = sbuf(nc, "s64", (64, TPC), F32)
    stok = sbuf(nc, "stok", (128, 64), F32)
    oh6 = sbuf(nc, "oh6", (128, 6 * 64), F32)
    tokcnt = sbuf(nc, "tokcnt", (128, 64), F32)
    kpre = sbuf(nc, "kpre", (128, 64), F32)
    ptile = sbuf(nc, "ptile", (128, 64), F32)
    e6t = sbuf(nc, "e6t", (128, 6), F32)
    w6t = sbuf(nc, "w6t", (128, 6), F32)
    p6t = sbuf(nc, "p6t", (128, 6), F32)
    offrow = sbuf(nc, "offrow", (1, 64), F32)
    small = sbuf(nc, "small", (128, 16), F32)

    ps0 = psum(nc, "ps0")
    ps1 = psum(nc, "ps1")
    psS = psum(nc, "psS")
    psB = psum(nc, "psB")
    ps2 = psum(nc, "ps2")
    psPT = nc.alloc_psum_tensor("psPT", [128, 1024], BF16)
    idb = sbuf(nc, "idb_sb", (128, 128), BF16)

    NT = TPC // 128

    def ldw_chunkcol(g, srcap, dst, col0, ncol=128, nck=16):
        srcsl = srcap[:, col0:col0 + ncol].rearrange("(ck p) c -> p ck c",
                                                     p=128)
        return [g.dma_start(out=dst[:, 0:nck * ncol].rearrange(
            "p (ck c) -> p ck c", ck=nck), in_=srcsl)]

    with P.block():
        def ld_c(g):
            return [g.dma_start(out=idf[:, :], in_=cst["idf"]),
                    g.dma_start(out=idb[:, :], in_=cst["idf"]),
                    g.dma_start(out=ones_col[:, :], in_=cst["ones_col"]),
                    g.dma_start(out=ones_row[:, :], in_=cst["ones_row"]),
                    g.dma_start(out=iota64[:, :], in_=cst["iota64"]),
                    g.dma_start(out=riota64[:, :], in_=cst["riota64"]),
                    g.dma_start(out=triu128[:, :], in_=cst["triu128"])]
        P.step("g", ld_c, dma=True, n_dma=7)

        from contextlib import ExitStack
        esm = ExitStack()
        og2sb = esm.enter_context(nc.sbuf_tensor("og2sb", [128, 16 * TPC], BF16))

        def ld_og2(g):
            return [g.dma_start(out=og2sb[:, :].rearrange(
                "p (ck t) -> p ck t", ck=16),
                in_=og2s.rearrange("(ck p) t -> p ck t", p=128))]
        P.step("g", ld_og2, dma=True, n_dma=1)

        def z_stats(v):
            return v.memset(strows[:, :], 0.0)
        P.step("v", z_stats)

        for ck in range(16):
            def ld_wp(g, ck=ck):
                return ldw_chunkcol(g, wproj, wtile, ck * 128)
            P.step("g", ld_wp, dma=True, n_dma=1)

            def ld_h2(g, ck=ck):
                return [g.dma_start(out=xck[:, :],
                                    in_=h2Ts[ck * 128:(ck + 1) * 128, :])]
            P.step("g", ld_h2, dma=True, n_dma=1)

            for th in range(2):
                def mm_o(t, ck=ck, th=th):
                    last = None
                    for hk in range(16):
                        last = t.matmul(
                            ps0[:, 0:TH], wtile[:, hk * 128:(hk + 1) * 128],
                            og2sb[:, hk * TPC + th * TH:
                                  hk * TPC + (th + 1) * TH],
                            start=(hk == 0), stop=(hk == 15))
                    return last
                P.step("t", mm_o)

                def addx(v, th=th):
                    return v.tensor_tensor(tmp1[:, th * TH:(th + 1) * TH],
                                           ps0[:, 0:TH],
                                           xck[:, th * TH:(th + 1) * TH],
                                           OP.add)
                P.step("v", addx)

            def st_h3p(s, ck=ck):
                return [s.dma_start(out=h3scr[ck * 128:(ck + 1) * 128, :],
                                    in_=tmp1[:, :])]
            P.step("s", st_h3p, dma=True, n_dma=1)

            def sq_h(a):
                return a.activation(out=tmp2[:, :], in_=tmp1[:, :],
                                    func=AF.Square)
            P.step("a", sq_h)

            for th in range(2):
                def mm_stats(t, th=th):
                    t.matmul(psS[0:1, 0:TH], ones_col[:, :],
                             tmp1[:, th * TH:(th + 1) * TH],
                             start=True, stop=True)
                    return t.matmul(psB[0:1, 0:TH], ones_col[:, :],
                                    tmp2[:, th * TH:(th + 1) * TH],
                                    start=True, stop=True)
                P.step("t", mm_stats)

                def acc_stats(v, th=th):
                    v.tensor_tensor(strows[:, th * TH:(th + 1) * TH],
                                    strows[:, th * TH:(th + 1) * TH],
                                    psS[0:1, 0:TH], OP.add)
                    v.drain()
                    return v.tensor_tensor(
                        strows[:, TPC + th * TH:TPC + (th + 1) * TH],
                        strows[:, TPC + th * TH:TPC + (th + 1) * TH],
                        psB[0:1, 0:TH], OP.add)
                P.step("v", acc_stats)

        def ln_scalars(v):
            v.tensor_scalar(strows[:, 0:TPC], strows[:, 0:TPC],
                            1.0 / 2048.0, None, OP.mult)
            v.drain()
            v.tensor_scalar(strows[:, TPC:2 * TPC], strows[:, TPC:2 * TPC],
                            1.0 / 2048.0, None, OP.mult)
            v.drain()
            v.tensor_tensor(strows[:, 2 * TPC:3 * TPC], strows[:, 0:TPC],
                            strows[:, 0:TPC], OP.mult)
            v.drain()
            v.tensor_tensor(strows[:, TPC:2 * TPC], strows[:, TPC:2 * TPC],
                            strows[:, 2 * TPC:3 * TPC], OP.subtract)
            v.drain()
            return v.tensor_scalar(strows[:, TPC:2 * TPC],
                                   strows[:, TPC:2 * TPC], 1e-5, None, OP.add)
        P.step("v", ln_scalars)

        def rstd_a(a):
            return a.activation(out=strows[:, 2 * TPC:3 * TPC],
                                in_=strows[:, TPC:2 * TPC], func=AF.Sqrt)
        P.step("a", rstd_a)

        def rstd_v(v):
            return v.reciprocal(strows[:, TPC:2 * TPC],
                                strows[:, 2 * TPC:3 * TPC])
        P.step("v", rstd_v)

        for th in range(2):
            def bc_mu(t, th=th):
                t.matmul(psB[:, 0:TH], ones_row[:, :],
                         strows[:, th * TH:(th + 1) * TH],
                         start=True, stop=True)
                return t.matmul(ps2[:, 0:TH], ones_row[:, :],
                                strows[:, TPC + th * TH:TPC + (th + 1) * TH],
                                start=True, stop=True)
            P.step("t", bc_mu)

            def ev_bc(v, th=th):
                v.tensor_copy(tmp2[:, 0:TH], psB[:, 0:TH])
                v.drain()
                return v.tensor_copy(tmp2[:, TH:2 * TH], ps2[:, 0:TH])
            P.step("v", ev_bc)

            for ck in range(16):
                def ld_pre(g, ck=ck, th=th):
                    return [g.dma_start(
                        out=xck[:, 0:TH],
                        in_=h3scr[ck * 128:(ck + 1) * 128,
                                  th * TH:(th + 1) * TH])]
                P.step("g", ld_pre, dma=True, n_dma=1)

                def apply(v, ck=ck, th=th):
                    v.tensor_tensor(tmp1[:, 0:TH], xck[:, 0:TH],
                                    tmp2[:, 0:TH], OP.subtract)
                    v.drain()
                    v.tensor_tensor(tmp1[:, 0:TH], tmp1[:, 0:TH],
                                    tmp2[:, TH:2 * TH], OP.mult)
                    v.drain()
                    return v.tensor_copy(
                        h3b[:, ck * TPC + th * TH:ck * TPC + (th + 1) * TH],
                        tmp1[:, 0:TH])
                P.step("v", apply)

                def st_h3(s, ck=ck, th=th):
                    return [s.dma_start(
                        out=h3T_o[ck * 128:(ck + 1) * 128,
                                  th * TH:(th + 1) * TH], in_=tmp1[:, 0:TH])]
                P.step("s", st_h3, dma=True, n_dma=1)
        esm.close()

        for ti in range(NT):
            for ck in range(16):
                def tr_h3(t, ti=ti, ck=ck):
                    return t.transpose(psPT[:, 0:128],
                                       h3b[:, ck * TPC + ti * 128:
                                           ck * TPC + (ti + 1) * 128],
                                       idb[:, :])
                P.step("t", tr_h3)

                def ev_h3r(v, ck=ck):
                    return v.tensor_copy(rowt[:, ck * 128:(ck + 1) * 128],
                                         psPT[:, 0:128])
                P.step("v", ev_h3r)

            def st_h3r(s, ti=ti):
                return [s.dma_start(
                    out=h3row_o[ti * 128:(ti + 1) * 128, :], in_=rowt[:, :])]
            P.step("s", st_h3r, dma=True, n_dma=1)

        for uc in range(8):
            def ld_shu(g, uc=uc):
                return ldw_chunkcol(g, wshu, wtile, uc * 128)
            P.step("g", ld_shu, dma=True, n_dma=1)

            for th in range(2):
                def mm_su(t, uc=uc, th=th):
                    last = None
                    for ck in range(16):
                        last = t.matmul(
                            ps0[:, 0:TH], wtile[:, ck * 128:(ck + 1) * 128],
                            h3b[:, ck * TPC + th * TH:ck * TPC + (th + 1) * TH],
                            start=(ck == 0), stop=(ck == 15))
                    return last
                P.step("t", mm_su)

                if True:
                    def ev_su(v, uc=uc, th=th):
                        if uc < 4:
                            return v.tensor_copy(
                                actsu[:, uc * TPC + th * TH:
                                      uc * TPC + (th + 1) * TH],
                                ps0[:, 0:TH])
                        return v.tensor_copy(
                            tmp1[:, th * TH:(th + 1) * TH], ps0[:, 0:TH])
                    P.step("v", ev_su)
                if uc >= 4:
                    def silu_gs(a, uc=uc, th=th):
                        return a.activation(out=tmp2[:, th * TH:(th + 1) * TH],
                                            in_=tmp1[:, th * TH:(th + 1) * TH],
                                            func=AF.Silu)
                    P.step("a", silu_gs)

                    def mul_su(v, uc=uc, th=th):
                        return v.tensor_tensor(
                            actsu[:, (uc - 4) * TPC + th * TH:
                                  (uc - 4) * TPC + (th + 1) * TH],
                            actsu[:, (uc - 4) * TPC + th * TH:
                                  (uc - 4) * TPC + (th + 1) * TH],
                            tmp2[:, th * TH:(th + 1) * TH], OP.mult)
                    P.step("v", mul_su)

        for ck in range(16):
            def ld_shd(g, ck=ck):
                return [g.dma_start(out=wtile[:, 0:4 * 128].rearrange(
                    "p (hk c) -> p hk c", hk=4),
                    in_=wshd[:, ck * 128:(ck + 1) * 128].rearrange(
                        "(hk p) c -> p hk c", p=128))]
            P.step("g", ld_shd, dma=True, n_dma=1)

            for th in range(2):
                def mm_shd(t, ck=ck, th=th):
                    last = None
                    for hk in range(4):
                        last = t.matmul(
                            ps0[:, 0:TH], wtile[:, hk * 128:(hk + 1) * 128],
                            actsu[:, hk * TPC + th * TH:
                                  hk * TPC + (th + 1) * TH],
                            start=(hk == 0), stop=(hk == 3))
                    return last
                P.step("t", mm_shd)

                def ev_shd(v, th=th):
                    return v.tensor_copy(tmp1[:, th * TH:(th + 1) * TH],
                                         ps0[:, 0:TH])
                P.step("v", ev_shd)

            def st_sh(s, ck=ck):
                return [s.dma_start(out=shT_o[ck * 128:(ck + 1) * 128, :],
                                    in_=tmp1[:, :])]
            P.step("s", st_sh, dma=True, n_dma=1)

        def ld_wg(g):
            return ldw_chunkcol(g, wgate, wtile, 0, ncol=64)
        P.step("g", ld_wg, dma=True, n_dma=1)

        for th in range(2):
            def mm_g(t, th=th):
                last = None
                for ck in range(16):
                    last = t.matmul(ps0[0:64, 0:TH],
                                    wtile[:, ck * 64:(ck + 1) * 64],
                                    h3b[:, ck * TPC + th * TH:
                                        ck * TPC + (th + 1) * TH],
                                    start=(ck == 0), stop=(ck == 15))
                return last
            P.step("t", mm_g)

            def ev_g(a, th=th):
                return a.activation(out=s64[:, th * TH:(th + 1) * TH],
                                    in_=ps0[0:64, 0:TH], func=AF.Sigmoid)
            P.step("a", ev_g)

        def z_off(v):
            return v.memset(offrow[:, :], 0.0)
        P.step("v", z_off)

        for ti in range(NT):
            def tr_g(t, ti=ti):
                return t.transpose(ps1[0:128, 0:64],
                                   s64[:, ti * 128:(ti + 1) * 128],
                                   idf[0:64, 0:64])
            P.step("t", tr_g)

            def ev_tg(v, ti=ti):
                return v.tensor_copy(stok[:, :], ps1[0:128, 0:64])
            P.step("v", ev_tg)

            def z_cnt(v):
                v.memset(tokcnt[:, :], 0.0)
                v.drain()
                return v.memset(kpre[:, :], 0.0)
            P.step("v", z_cnt)

            for k in range(6):
                def sel_k(v, k=k):
                    v.reduce_max(w6t[:, k:k + 1], stok[:, :], axis=AX.X)
                    v.drain()
                    v.tensor_scalar(oh6[:, k * 64:(k + 1) * 64], stok[:, :],
                                    w6t[:, k:k + 1], None, OP.is_ge)
                    v.drain()
                    v.tensor_tensor(tmp1[:, 0:64],
                                    oh6[:, k * 64:(k + 1) * 64], riota64[:, :],
                                    OP.mult)
                    v.drain()
                    v.reduce_max(small[:, 0:1], tmp1[:, 0:64], axis=AX.X)
                    v.drain()
                    v.tensor_scalar(e6t[:, k:k + 1], small[:, 0:1], -1.0,
                                    63.0, OP.mult, OP.add)
                    v.drain()
                    v.tensor_scalar(oh6[:, k * 64:(k + 1) * 64], iota64[:, :],
                                    e6t[:, k:k + 1], None, OP.is_equal)
                    v.drain()
                    v.scalar_tensor_tensor(stok[:, :],
                                           oh6[:, k * 64:(k + 1) * 64], -2.0,
                                           stok[:, :], OP.mult, OP.add)
                    v.drain()
                    return v.tensor_tensor(tokcnt[:, :], tokcnt[:, :],
                                           oh6[:, k * 64:(k + 1) * 64], OP.add)
                P.step("v", sel_k)

            def norm_w(v):
                v.reduce_sum(small[:, 1:2], w6t[:, :], axis=AX.X)
                v.drain()
                v.reciprocal(small[:, 2:3], small[:, 1:2])
                v.drain()
                return v.tensor_scalar(w6t[:, :], w6t[:, :], small[:, 2:3],
                                       None, OP.mult)
            P.step("v", norm_w)

            def mm_pre(t, ti=ti):
                t.matmul(ps1[:, 0:64], triu128[:, :], tokcnt[:, :],
                         start=True, stop=True)
                return t.matmul(ps2[:, 0:64], ones_row[:, :], offrow[:, :],
                                start=True, stop=True)
            P.step("t", mm_pre)

            def mk_ptile(v):
                v.tensor_copy(ptile[:, :], ps1[:, 0:64])
                v.drain()
                return v.tensor_tensor(ptile[:, :], ptile[:, :], ps2[:, 0:64],
                                       OP.add)
            P.step("v", mk_ptile)

            for k in range(6):
                def pos_k(v, k=k):
                    v.tensor_tensor(tmp1[:, 0:64], ptile[:, :], kpre[:, :],
                                    OP.add)
                    v.drain()
                    v.tensor_tensor(tmp1[:, 0:64], tmp1[:, 0:64],
                                    oh6[:, k * 64:(k + 1) * 64], OP.mult)
                    v.drain()
                    v.reduce_sum(p6t[:, k:k + 1], tmp1[:, 0:64], axis=AX.X)
                    v.drain()
                    return v.tensor_tensor(kpre[:, :], kpre[:, :],
                                           oh6[:, k * 64:(k + 1) * 64], OP.add)
                P.step("v", pos_k)

            def mm_tot(t, ti=ti):
                return t.matmul(ps1[0:1, 0:64], ones_col[:, :], tokcnt[:, :],
                                start=True, stop=True)
            P.step("t", mm_tot)

            def upd_off(v):
                return v.tensor_tensor(offrow[:, :], offrow[:, :],
                                       ps1[0:1, 0:64], OP.add)
            P.step("v", upd_off)

            def st_routing(s, ti=ti):
                return [s.dma_start(out=e6_o[ti * 128:(ti + 1) * 128, :],
                                    in_=e6t[:, :]),
                        s.dma_start(out=w6_o[ti * 128:(ti + 1) * 128, :],
                                    in_=w6t[:, :]),
                        s.dma_start(out=p6_o[ti * 128:(ti + 1) * 128, :],
                                    in_=p6t[:, :])]
            P.step("s", st_routing, dma=True, n_dma=3)

        def st_cnt(s):
            return [s.dma_start(out=cnt_o, in_=offrow[:, :])]
        P.step("s", st_cnt, dma=True, n_dma=1)

        P.final_wait()
    return P


def prep_l1mid_inputs(inputs, og2_full, h2T_full):
    consts = _consts_a4()
    per_core = []
    for c in range(NCORES):
        b, r = c // 2, c % 2
        d = {
            "og2s": np.ascontiguousarray(
                og2_full[b][:, r * TPC:(r + 1) * TPC]),
            "h2Ts": np.ascontiguousarray(
                h2T_full[b][:, r * TPC:(r + 1) * TPC]),
            "wproj": np.asarray(inputs["L1_attn_w_proj"], dtype=np.float32),
            "wgate": np.asarray(inputs["L1_ffn_w_gate"], dtype=np.float32),
            "wshu": np.asarray(inputs["L1_ffn_w_shared_up"], dtype=np.float32),
            "wshd": np.asarray(inputs["L1_ffn_w_shared_down"],
                               dtype=np.float32),
        }
        for k, v in consts.items():
            d["c_" + k] = v
        per_core.append(d)
    return per_core



def build_l1route():
    P = Prog("l1route")
    nc = P.nc
    NTOK = NCORES * TPC
    NTI = TPC // 128
    h3rf = nc.dram_tensor("h3rf", [NTOK, 2048], BF16, kind="ExternalInput").ap()
    e6f = nc.dram_tensor("e6f", [NTOK, 6], F32, kind="ExternalInput").ap()
    p6f = nc.dram_tensor("p6f", [NTOK, 6], F32, kind="ExternalInput").ap()
    cntf = nc.dram_tensor("cntf", [8, 64], F32, kind="ExternalInput").ap()
    wup = nc.dram_tensor("wup", [8 * 2048, 512], F32, kind="ExternalInput").ap()
    wdn = nc.dram_tensor("wdn", [8 * 256, 2048], F32, kind="ExternalInput").ap()
    expoff = nc.dram_tensor("expoff", [128, 64], F32, kind="ExternalInput").ap()
    tria = nc.dram_tensor("tria", [8, 8], F32, kind="ExternalInput").ap()
    tokid = nc.dram_tensor("tokid", [128, 1], F32, kind="ExternalInput").ap()
    cst = {k: nc.dram_tensor("c_" + k, list(v.shape), F32,
                             kind="ExternalInput").ap()
           for k, v in _consts_a4().items()}
    outr_o = nc.dram_tensor("outr", [8 * 1024, 2048], BF16,
                            kind="ExternalOutput").ap()
    tbl = nc.dram_tensor("tbl", [8576, 1], F32).ap()
    offscr = nc.dram_tensor("offscr", [1024, 1], F32).ap()

    iota64 = sbuf(nc, "iota64_sb", (128, 64), F32)
    ones_row = sbuf(nc, "onesr_sb", (1, 128), F32)
    tri8 = sbuf(nc, "tri8_sb", (8, 8), F32)
    cnts = sbuf(nc, "cnts", (8, 64), F32)
    offs = sbuf(nc, "offs", (8, 64), F32)
    offs2 = sbuf(nc, "offs2", (8, 64), F32)
    expob = sbuf(nc, "expob", (128, 64), F32)
    e6t = sbuf(nc, "e6t", (128, 6), F32)
    p6t = sbuf(nc, "p6t", (128, 6), F32)
    e6g = sbuf(nc, "e6g", (128, 6), F32)
    e6i = sbuf(nc, "e6i", (128, 6), I32)
    e6i2 = sbuf(nc, "e6i2", (128, 6), I32)
    dloc6 = sbuf(nc, "dloc6", (128, 6), F32)
    tmp = sbuf(nc, "tmp", (128, 64), F32)
    dumpb = sbuf(nc, "dumpb", (128, 6), F32)
    dcol6 = sbuf(nc, "dcol6", (128, 6), F32)
    dcoli6 = sbuf(nc, "dcoli6", (128, 6), I32)
    tokv6 = sbuf(nc, "tokv6", (128, 1), F32)
    tokc = sbuf(nc, "tokc", (128, 1), F32)
    zrow = sbuf(nc, "zrow", (128, 67), F32)
    idb = sbuf(nc, "idb_sb", (128, 128), BF16)
    idxi = [sbuf(nc, "idxi0", (128, 8), I32), sbuf(nc, "idxi1", (128, 8), I32)]
    idxf = [sbuf(nc, "idxf0", (128, 8), F32), sbuf(nc, "idxf1", (128, 8), F32)]
    xg = [sbuf(nc, "xg0", (128, 2048), BF16),
          sbuf(nc, "xg1", (128, 2048), BF16)]
    xdT = [sbuf(nc, "xdT0", (128, 16 * 1024), BF16),
           sbuf(nc, "xdT1", (128, 16 * 1024), BF16)]
    wub = [sbuf(nc, "wub0", (128, 16 * 512), BF16),
           sbuf(nc, "wub1", (128, 16 * 512), BF16)]
    wdb = [sbuf(nc, "wdb0", (128, 2 * 2048), BF16),
           sbuf(nc, "wdb1", (128, 2 * 2048), BF16)]
    wsu = [sbuf(nc, "wsu0", (128, 4 * 512), F32),
           sbuf(nc, "wsu1", (128, 4 * 512), F32)]
    gbuf = sbuf(nc, "gbuf", (128, 2 * 1024), F32)
    hbuf = sbuf(nc, "hbuf", (128, 2 * 1024), BF16)
    orow = sbuf(nc, "orow", (128, 2048), BF16)

    ps0 = psum(nc, "ps0")
    ps1 = psum(nc, "ps1")
    ps2 = psum(nc, "ps2")
    ps3 = psum(nc, "ps3")
    psT0 = psum(nc, "psT0", BF16, cols=128)
    psT1 = psum(nc, "psT1", BF16, cols=128)
    psT = [psT0, psT1]

    with P.block():
        def ld_c(g):
            return [g.dma_start(out=iota64[:, :], in_=cst["iota64"]),
                    g.dma_start(out=ones_row[:, :], in_=cst["ones_row"]),
                    g.dma_start(out=tri8[:, :], in_=tria),
                    g.dma_start(out=cnts[:, :], in_=cntf),
                    g.dma_start(out=expob[:, :], in_=expoff),
                    g.dma_start(out=tokc[:, :], in_=tokid),
                    g.dma_start(out=dumpb[:, :], in_=cst["dumpb"]),
                    g.dma_start(out=idb[:, :], in_=cst["idf"])]
        t_ldc = P.step("g", ld_c, dma=True, n_dma=8, sem_name="ldc")

        def z_zrow(v):
            return v.memset(zrow[:, :], 0.0)
        t_z = P.step("v", z_zrow, waits=())

        def z_tbl(s):
            return [s.dma_start(out=tbl.rearrange("(a b) one -> a (b one)",
                                                  a=128), in_=zrow[:, 0:67])]
        t_ztbl = P.step("s", z_tbl, dma=True, n_dma=1, waits=(t_z,),
                        sem_name="ztb")

        def mm_off(t):
            return t.matmul(ps0[0:8, 0:64], tri8[:, :], cnts[:, :],
                            start=True, stop=True)
        t_mo = P.step("t", mm_off, waits=(t_ldc,))

        def ev_off(v):
            return v.tensor_copy(offs[:, :], ps0[0:8, 0:64])
        t_eo = P.step("v", ev_off, waits=(t_mo,))

        def add_eb(v):
            return v.tensor_tensor(offs2[:, :], offs[:, :], expob[0:8, :],
                                   OP.add)
        t_eb = P.step("v", add_eb, waits=(t_eo,))

        def st_off(s):
            return [s.dma_start(out=offscr[0:512, :].rearrange(
                "(a b) one -> a (b one)", a=8), in_=offs2[:, :]),
                    s.dma_start(out=offscr[512:1024, :].rearrange(
                "(a b) one -> a (b one)", a=8), in_=offs[:, :])]
        t_so = P.step("s", st_off, dma=True, n_dma=2, waits=(t_eb,),
                      sem_name="sof")

        t_scat_k = [None] * 6
        t_prev_cvt = None
        t_prev_mk = None
        for cb in range(8):
            for ti in range(NTI):
                def ld_rt(g, cb=cb, ti=ti):
                    base = cb * TPC + ti * 128
                    return [g.dma_start(out=e6t[:, :],
                                        in_=e6f[base:base + 128, :]),
                            g.dma_start(out=p6t[:, :],
                                        in_=p6f[base:base + 128, :])]
                t_lrt = P.step("g", ld_rt, dma=True, n_dma=2,
                               waits=(t_prev_cvt, t_prev_mk), sem_name="lrt")

                def cvt(v, cb=cb):
                    v.tensor_scalar(e6g[:, :], e6t[:, :], 1.0,
                                    float(cb * 64), OP.mult, OP.add)
                    v.drain()
                    v.tensor_copy(e6i[:, :], e6g[:, :])
                    v.drain()
                    v.tensor_scalar(e6g[:, :], e6g[:, :], 1.0, 512.0,
                                    OP.mult, OP.add)
                    v.drain()
                    return v.tensor_copy(e6i2[:, :], e6g[:, :])
                t_prev_cvt = P.step("v", cvt, waits=(t_lrt,))

                def gath6(g, cb=cb, ti=ti):
                    out = []
                    for k in range(6):
                        out.append(g.indirect_dma_start(
                            out=dcol6[:, k:k + 1], out_offset=None,
                            in_=offscr, in_offset=IOA(e6i[:, k:k + 1], 0)))
                        out.append(g.indirect_dma_start(
                            out=dloc6[:, k:k + 1], out_offset=None,
                            in_=offscr, in_offset=IOA(e6i2[:, k:k + 1], 0)))
                    return out
                t_g6 = P.step("g", gath6, dma=True, n_dma=12,
                              waits=(t_prev_cvt, t_prev_mk, t_so),
                              sem_name="g6")

                def mk(v, cb=cb, ti=ti):
                    v.tensor_tensor(dcol6[:, :], dcol6[:, :], p6t[:, :],
                                    OP.add)
                    v.tensor_tensor(dloc6[:, :], dloc6[:, :], p6t[:, :],
                                    OP.add)
                    v.drain()
                    v.tensor_scalar(tmp[:, 8:14], dloc6[:, :], 1024.0, None,
                                    OP.is_ge)
                    v.drain()
                    v.scalar_tensor_tensor(dcol6[:, :], tmp[:, 8:14], 1e9,
                                           dcol6[:, :], OP.mult, OP.add)
                    v.drain()
                    v.tensor_scalar(tmp[:, 0:6], dumpb[:, :],
                                    float((cb * NTI + ti) * 6), None,
                                    OP.add)
                    v.drain()
                    v.tensor_tensor(dcol6[:, :], dcol6[:, :], tmp[:, 0:6],
                                    OP.min)
                    v.drain()
                    v.tensor_copy(dcoli6[:, :], dcol6[:, :])
                    v.drain()
                    return v.tensor_scalar(
                        tokv6[:, 0:1], tokc[:, 0:1], 1.0,
                        float(cb * TPC + ti * 128), OP.mult, OP.add)
                t_prev_mk = P.step("v", mk,
                                   waits=(t_g6,) + tuple(t_scat_k))

                for k in range(6):
                    def scat(g, k=k):
                        return [g.indirect_dma_start(
                            out=tbl, out_offset=IOA(dcoli6[:, k:k + 1], 0),
                            in_=tokv6[:, 0:1], in_offset=None)]
                    t_scat_k[k] = P.step("g", scat, dma=True, n_dma=1,
                                         waits=(t_prev_mk, t_ztbl),
                                         sem_name="sc%d" % k)

        tok_cvt = [None, None]
        tok_xrd = [None, None]
        tok_wld = [None, None]
        tok_wcast = [None, None]
        tok_mmup_done = [None, None]
        tok_dn_done = [None, None]
        t_store_prev = None
        prev_tr = {0: None, 1: None}
        t_evT = [None, None]
        t_cast_q = [None, None]

        for e in range(8):
            par = e % 2

            def ld_tbl(g, e=e, par=par):
                return [g.dma_start(
                    out=idxf[par][:, rt:rt + 1],
                    in_=tbl[e * 1024 + rt * 128:e * 1024 + (rt + 1) * 128, :])
                    for rt in range(8)]
            t_ldt = P.step("g", ld_tbl, dma=True, n_dma=8,
                           waits=tuple(t_scat_k) + (tok_cvt[par],),
                           sem_name="ix%d" % par)

            def cvt_idx(v, par=par):
                return v.tensor_copy(idxi[par][:, :], idxf[par][:, :])
            tok_cvt[par] = P.step("v", cvt_idx, waits=(t_ldt,))

            for qd in range(4):
                qp = qd % 2

                def ld_wq(s, e=e, qd=qd, qp=qp):
                    return [s.dma_start(
                        out=wsu[qp][:, :].rearrange("p (ck c) -> p ck c",
                                                    ck=4),
                        in_=wup[e * 2048 + qd * 512:
                                e * 2048 + (qd + 1) * 512, :].rearrange(
                            "(ck p) c -> p ck c", p=128))]
                t_lq = P.step("s", ld_wq, dma=True, n_dma=1,
                              waits=(t_cast_q[qp],), sem_name="wq%d" % qp)

                def cast_wq(v, par=par, qd=qd, qp=qp):
                    return v.tensor_copy(
                        wub[par][:, qd * 2048:(qd + 1) * 2048],
                        wsu[qp][:, 0:2048])
                t_cast_q[qp] = P.step(
                    "v", cast_wq,
                    waits=(t_lq,) +
                    ((tok_mmup_done[par],) if qd == 0 else ()))
            tok_wcast[par] = t_cast_q[(4 - 1) % 2]
            tok_wcast_all = (t_cast_q[0], t_cast_q[1])

            def ld_wd(g, e=e, par=par):
                return [g.dma_start(
                    out=wdb[par][:, :].rearrange("p (hk c) -> p hk c", hk=2),
                    in_=wdn[e * 256:(e + 1) * 256, :].rearrange(
                        "(hk p) c -> p hk c", p=128))]
            tok_wld[par] = P.step("g", ld_wd, dma=True, n_dma=1,
                                  waits=(tok_dn_done[par],),
                                  sem_name="wl%d" % par)

            t_ev_last = None
            for rt in range(8):
                xp = rt % 2

                def gath(g, rt=rt, par=par, xp=xp):
                    return [g.indirect_dma_start(
                        out=xg[xp][:, :], out_offset=None, in_=h3rf,
                        in_offset=IOA(idxi[par][:, rt:rt + 1], 0))]
                t_g = P.step("g", gath, dma=True, n_dma=1,
                             waits=(tok_cvt[par], prev_tr[xp]),
                             sem_name="xg%d" % xp)

                for fb in range(16):
                    tp = fb % 2

                    def tr_x(t, rt=rt, fb=fb, xp=xp, tp=tp):
                        return t.transpose(psT[tp][:, 0:128],
                                           xg[xp][:, fb * 128:(fb + 1) * 128],
                                           idb[:, :])
                    t_tr = P.step("t", tr_x, waits=(t_g, t_evT[tp]))

                    def ev_x(v, rt=rt, fb=fb, par=par, tp=tp):
                        return v.tensor_copy(
                            xdT[par][:, fb * 1024 + rt * 128:
                                     fb * 1024 + (rt + 1) * 128],
                            psT[tp][:, 0:128])
                    t_evT[tp] = P.step(
                        "v", ev_x,
                        waits=(t_tr,) +
                        ((tok_mmup_done[par],) if rt == 0 and fb < 2 else ()))
                prev_tr[xp] = t_tr
            tok_xrd[par] = (t_evT[0], t_evT[1])

            ev_toks = []
            mm_toks = []
            for hc in range(4):
                for th in range(2):
                    i = hc * 2 + th
                    bank = (ps0, ps1)[i % 2]

                    def mm_up(t, hc=hc, th=th, par=par, bank=bank):
                        last = None
                        for ck in range(16):
                            last = t.matmul(
                                bank[:, 0:512],
                                wub[par][:, ck * 512 + hc * 128:
                                         ck * 512 + (hc + 1) * 128],
                                xdT[par][:, ck * 1024 + th * 512:
                                         ck * 1024 + (th + 1) * 512],
                                start=(ck == 0), stop=(ck == 15))
                        return last
                    w_ = [tok_xrd[par][0], tok_xrd[par][1],
                          tok_wcast_all[0], tok_wcast_all[1]]
                    if len(ev_toks) >= 2:
                        w_.append(ev_toks[-2])
                    tok_mm = P.step("t", mm_up, waits=tuple(w_))
                    mm_toks.append(tok_mm)

                    if hc < 2:
                        def ev_up(a, hc=hc, th=th, bank=bank):
                            return a.activation(
                                out=gbuf[:, hc * 1024 + th * 512:
                                         hc * 1024 + (th + 1) * 512],
                                in_=bank[:, 0:512], func=AF.Silu)
                        ev_toks.append(P.step("a", ev_up, waits=(tok_mm,)))
                    else:
                        def ev_up2(v, hc=hc, th=th, bank=bank):
                            return v.tensor_tensor(
                                hbuf[:, (hc - 2) * 1024 + th * 512:
                                     (hc - 2) * 1024 + (th + 1) * 512],
                                gbuf[:, (hc - 2) * 1024 + th * 512:
                                     (hc - 2) * 1024 + (th + 1) * 512],
                                bank[:, 0:512], OP.mult)
                        ev_toks.append(P.step("v", ev_up2,
                                              waits=(tok_mm,
                                                     ev_toks[i - 4])))
            tok_mmup_done[par] = mm_toks[-1]

            dn_ev = []
            for tc in range(8):
                for dm4 in range(4):
                    i = tc * 4 + dm4
                    bank = (ps2, ps3)[i % 2]

                    def mm_dn(t, tc=tc, dm4=dm4, par=par, bank=bank):
                        t.matmul(bank[:, 0:512],
                                 hbuf[:, tc * 128:(tc + 1) * 128],
                                 wdb[par][:, dm4 * 512:(dm4 + 1) * 512],
                                 start=True, stop=False)
                        return t.matmul(
                            bank[:, 0:512],
                            hbuf[:, 1024 + tc * 128:1024 + (tc + 1) * 128],
                            wdb[par][:, 2048 + dm4 * 512:
                                     2048 + (dm4 + 1) * 512],
                            start=False, stop=True)
                    w_ = [ev_toks[-1], ev_toks[-2], tok_wld[par]]
                    if len(dn_ev) >= 2:
                        w_.append(dn_ev[-2])
                    tok_md = P.step("t", mm_dn, waits=tuple(w_))

                    def ev_dn(v, tc=tc, dm4=dm4, bank=bank):
                        return v.tensor_copy(
                            orow[:, dm4 * 512:(dm4 + 1) * 512],
                            bank[:, 0:512])
                    dn_ev.append(P.step("v", ev_dn,
                                        waits=(tok_md, t_store_prev)))

                def st_or(s, e=e, tc=tc):
                    return [s.dma_start(
                        out=outr_o[e * 1024 + tc * 128:
                                   e * 1024 + (tc + 1) * 128, :],
                        in_=orow[:, :])]
                t_store_prev = P.step("s", st_or, dma=True, n_dma=1,
                                      waits=tuple(dn_ev[-4:]),
                                      sem_name="sor")
            tok_dn_done[par] = dn_ev[-1]

        P.final_wait()
    return P


def prep_l1route_inputs(inputs, h3row_full, e6_full, p6_full, cnt_full):
    consts = _consts_a4()
    wu = np.asarray(inputs["L1_ffn_w_up"], dtype=np.float32)
    wd = np.asarray(inputs["L1_ffn_w_down"], dtype=np.float32)
    tria = np.triu(np.ones((8, 8), np.float32), 1)
    per_core = []
    for c in range(NCORES):
        eo = np.full((128, 64), 1e9, np.float32)
        for e in range(8 * c, 8 * c + 8):
            eo[:, e] = (e - 8 * c) * 1024.0
        d = {
            "h3rf": h3row_full,
            "e6f": e6_full, "p6f": p6_full, "cntf": cnt_full,
            "wup": np.ascontiguousarray(
                wu[8 * c:8 * c + 8].reshape(8 * 2048, 512)),
            "wdn": np.ascontiguousarray(
                wd[8 * c:8 * c + 8].reshape(8 * 256, 2048)),
            "expoff": eo,
            "tria": tria,
            "tokid": np.arange(128, dtype=np.float32).reshape(128, 1),
        }
        for k, v in consts.items():
            d["c_" + k] = v
        per_core.append(d)
    return per_core



def build_final():
    P = Prog("final")
    nc = P.nc
    outrf = nc.dram_tensor("outrf", [64 * 1024, 2048], BF16,
                           kind="ExternalInput").ap()
    h3Ts = nc.dram_tensor("h3Ts", [2048, TPC], F32, kind="ExternalInput").ap()
    shTs = nc.dram_tensor("shTs", [2048, TPC], F32, kind="ExternalInput").ap()
    e6s = nc.dram_tensor("e6s", [TPC, 6], F32, kind="ExternalInput").ap()
    p6s = nc.dram_tensor("p6s", [TPC, 6], F32, kind="ExternalInput").ap()
    w6s = nc.dram_tensor("w6s", [TPC, 6], F32, kind="ExternalInput").ap()
    cntf = nc.dram_tensor("cntf", [8, 64], F32, kind="ExternalInput").ap()
    selc = nc.dram_tensor("selc", [8, 1], F32, kind="ExternalInput").ap()
    cst = {k: nc.dram_tensor("c_" + k, list(v.shape), F32,
                             kind="ExternalInput").ap()
           for k, v in _consts_a4().items()}
    out_o = nc.dram_tensor("out", [TPC, 2048], F32, kind="ExternalOutput").ap()

    iota64 = sbuf(nc, "iota64_sb", (128, 64), F32)
    ones_row = sbuf(nc, "onesr_sb", (1, 128), F32)
    idf = sbuf(nc, "idf_sb", (128, 128), F32)
    tri8 = sbuf(nc, "tri8_sb", (8, 8), F32)
    selcs = sbuf(nc, "selcs", (8, 1), F32)
    cnts = sbuf(nc, "cnts", (8, 64), F32)
    offs = sbuf(nc, "offs", (8, 64), F32)
    offb = sbuf(nc, "offb", (128, 64), F32)
    e6t = sbuf(nc, "e6t", (128, 6), F32)
    p6t = sbuf(nc, "p6t", (128, 6), F32)
    w6t = sbuf(nc, "w6t", (128, 6), F32)
    offrow1 = sbuf(nc, "offrow1", (1, 64), F32)
    tmp = sbuf(nc, "tmp", (128, 64), F32)
    oh = sbuf(nc, "oh", (128, 64), F32)
    dcol = sbuf(nc, "dcol", (128, 1), F32)
    dcoli = sbuf(nc, "dcoli", (128, 1), I32)
    vcol = sbuf(nc, "vcol", (128, 1), F32)
    racc = sbuf(nc, "racc", (128, 2048), F32)
    gbuf = sbuf(nc, "gbuf", (128, 2048), BF16)
    fsl = sbuf(nc, "fsl", (128, 128), F32)
    stat = sbuf(nc, "stat", (128, 8), F32)
    sq = sbuf(nc, "sq", (128, 2048), F32)
    tria_in = nc.dram_tensor("tria", [8, 8], F32, kind="ExternalInput").ap()

    ps0 = psum(nc, "ps0")
    ps1 = psum(nc, "ps1")

    NT = TPC // 128

    with P.block():
        def ld_c(g):
            return [g.dma_start(out=iota64[:, :], in_=cst["iota64"]),
                    g.dma_start(out=ones_row[:, :], in_=cst["ones_row"]),
                    g.dma_start(out=idf[:, :], in_=cst["idf"]),
                    g.dma_start(out=tri8[:, :], in_=tria_in),
                    g.dma_start(out=cnts[:, :], in_=cntf),
                    g.dma_start(out=selcs[:, :], in_=selc)]
        P.step("g", ld_c, dma=True, n_dma=6)

        def mm_off(t):
            return t.matmul(ps0[0:8, 0:64], tri8[:, :], cnts[:, :],
                            start=True, stop=True)
        P.step("t", mm_off)

        def ev_off(v):
            return v.tensor_copy(offs[:, :], ps0[0:8, 0:64])
        P.step("v", ev_off)

        def mm_my(t):
            return t.matmul(ps1[0:1, 0:64], selcs[:, :], offs[:, :],
                            start=True, stop=True)
        P.step("t", mm_my)

        def ev_my(v):
            return v.tensor_copy(offrow1[:, :], ps1[0:1, 0:64])
        P.step("v", ev_my)

        for ti in range(NT):
            def ld_rt(g, ti=ti):
                return [g.dma_start(out=e6t[:, :],
                                    in_=e6s[ti * 128:(ti + 1) * 128, :]),
                        g.dma_start(out=p6t[:, :],
                                    in_=p6s[ti * 128:(ti + 1) * 128, :]),
                        g.dma_start(out=w6t[:, :],
                                    in_=w6s[ti * 128:(ti + 1) * 128, :])]
            P.step("g", ld_rt, dma=True, n_dma=3)

            if ti == 0:
                def bc_off(t):
                    return t.matmul(ps1[:, 0:64], ones_row[:, :],
                                    offrow1[:, :], start=True, stop=True)
                P.step("t", bc_off)

                def ev_offb(v):
                    return v.tensor_copy(offb[:, :], ps1[:, 0:64])
                P.step("v", ev_offb)

            def z_racc(v):
                return v.memset(racc[:, :], 0.0)
            P.step("v", z_racc)

            for k in range(6):
                def mkidx(v, k=k):
                    v.tensor_scalar(oh[:, :], iota64[:, :], e6t[:, k:k + 1],
                                    None, OP.is_equal)
                    v.drain()
                    v.tensor_tensor(tmp[:, :], oh[:, :], offb[:, :], OP.mult)
                    v.drain()
                    v.reduce_sum(dcol[:, 0:1], tmp[:, :], axis=AX.X)
                    v.drain()
                    v.tensor_tensor(dcol[:, 0:1], dcol[:, 0:1],
                                    p6t[:, k:k + 1], OP.add)
                    v.drain()
                    v.tensor_scalar(vcol[:, 0:1], dcol[:, 0:1], 1024.0, None,
                                    OP.is_lt)
                    v.drain()
                    v.tensor_tensor(vcol[:, 0:1], vcol[:, 0:1],
                                    w6t[:, k:k + 1], OP.mult)
                    v.drain()
                    v.tensor_scalar(dcol[:, 0:1], dcol[:, 0:1], 1023.0, None,
                                    OP.min)
                    v.drain()
                    v.tensor_scalar(stat[:, 0:1], e6t[:, k:k + 1], 1024.0,
                                    None, OP.mult)
                    v.drain()
                    v.tensor_tensor(dcol[:, 0:1], dcol[:, 0:1], stat[:, 0:1],
                                    OP.add)
                    v.drain()
                    return v.tensor_copy(dcoli[:, :], dcol[:, :])
                P.step("v", mkidx)

                def gath(g):
                    return [g.indirect_dma_start(
                        out=gbuf[:, :], out_offset=None, in_=outrf,
                        in_offset=IOA(dcoli[:, 0:1], 0))]
                P.step("g", gath, dma=True, n_dma=1)

                def acc_r(v, k=k):
                    return v.scalar_tensor_tensor(racc[:, :], gbuf[:, :],
                                                  vcol[:, 0:1], racc[:, :],
                                                  OP.mult, OP.add)
                P.step("v", acc_r)

            for ck in range(16):
                for which, srct in (("sh", shTs), ("h3", h3Ts)):
                    def ld_f(g, ck=ck, ti=ti, srct=srct):
                        return [g.dma_start(
                            out=fsl[:, :],
                            in_=srct[ck * 128:(ck + 1) * 128,
                                     ti * 128:(ti + 1) * 128])]
                    P.step("g", ld_f, dma=True, n_dma=1)

                    def tr_f(t):
                        return t.transpose(ps0[:, 0:128], fsl[:, :],
                                           idf[:, :])
                    P.step("t", tr_f)

                    def add_f(v, ck=ck):
                        return v.tensor_tensor(
                            racc[:, ck * 128:(ck + 1) * 128],
                            racc[:, ck * 128:(ck + 1) * 128],
                            ps0[:, 0:128], OP.add)
                    P.step("v", add_f)

            def ln_f(v):
                v.reduce_sum(stat[:, 1:2], racc[:, :], axis=AX.X)
                v.drain()
                return v.tensor_scalar(stat[:, 1:2], stat[:, 1:2],
                                       1.0 / 2048.0, None, OP.mult)
            P.step("v", ln_f)

            def sq_f(a):
                return a.activation(out=sq[:, :], in_=racc[:, :],
                                    func=AF.Square)
            P.step("a", sq_f)

            def var_f(v):
                v.reduce_sum(stat[:, 2:3], sq[:, :], axis=AX.X)
                v.drain()
                v.tensor_scalar(stat[:, 2:3], stat[:, 2:3], 1.0 / 2048.0,
                                None, OP.mult)
                v.drain()
                v.tensor_tensor(stat[:, 3:4], stat[:, 1:2], stat[:, 1:2],
                                OP.mult)
                v.drain()
                v.tensor_tensor(stat[:, 2:3], stat[:, 2:3], stat[:, 3:4],
                                OP.subtract)
                v.drain()
                return v.tensor_scalar(stat[:, 2:3], stat[:, 2:3], 1e-5,
                                       None, OP.add)
            P.step("v", var_f)

            def sqrt_f(a):
                return a.activation(out=stat[:, 4:5], in_=stat[:, 2:3],
                                    func=AF.Sqrt)
            P.step("a", sqrt_f)

            def fin_f(v):
                v.reciprocal(stat[:, 5:6], stat[:, 4:5])
                v.drain()
                v.tensor_scalar(stat[:, 6:7], stat[:, 1:2], -1.0, None,
                                OP.mult)
                v.drain()
                return v.tensor_scalar(racc[:, :], racc[:, :], stat[:, 6:7],
                                       stat[:, 5:6], OP.add, OP.mult)
            P.step("v", fin_f)

            def st_out(s, ti=ti):
                return [s.dma_start(out=out_o[ti * 128:(ti + 1) * 128, :],
                                    in_=racc[:, :])]
            P.step("s", st_out, dma=True, n_dma=1)

        P.final_wait()
    return P


def prep_final_inputs(inputs, outr_full, h3T_slices, shT_slices, e6_slices,
                      p6_slices, w6_slices, cnt_full):
    consts = _consts_a4()
    tria = np.triu(np.ones((8, 8), np.float32), 1)
    per_core = []
    for c in range(NCORES):
        d = {
            "outrf": outr_full,
            "h3Ts": h3T_slices[c], "shTs": shT_slices[c],
            "e6s": e6_slices[c], "p6s": p6_slices[c], "w6s": w6_slices[c],
            "cntf": cnt_full,
            "selc": np.eye(8, dtype=np.float32)[:, c:c + 1].copy(),
            "tria": tria,
        }
        for k, v in consts.items():
            d["c_" + k] = v
        per_core.append(d)
    return per_core


_BUILD_CACHE = {}


def _run(builder, per_core):
    if builder not in _BUILD_CACHE:
        _BUILD_CACHE[builder] = builder()
    P = _BUILD_CACHE[builder]
    res = run_bass_kernel_spmd(P.nc, per_core, core_ids=list(range(NCORES)))
    return res.results


def kernel(**inputs):
    r1 = _run(build_l0attn, prep_l0attn_inputs(inputs))
    og_full = [np.concatenate([np.asarray(r1[2 * b]["og"]),
                               np.asarray(r1[2 * b + 1]["og"])], axis=0)
               for b in range(4)]
    r2 = _run(build_l0rest, prep_l0rest_inputs(inputs, og_full))
    h2T_full = [np.concatenate([np.asarray(r2[2 * b]["h2T"]),
                                np.asarray(r2[2 * b + 1]["h2T"])], axis=1)
                for b in range(4)]
    ckvT_full = [np.concatenate([np.asarray(r2[2 * b]["ckvT"]),
                                 np.asarray(r2[2 * b + 1]["ckvT"])], axis=1)
                 for b in range(4)]
    r3 = _run(build_l1attn, prep_l1attn_inputs(inputs, h2T_full, ckvT_full))
    og2_full = [np.concatenate([np.asarray(r3[2 * b]["og2"]),
                                np.asarray(r3[2 * b + 1]["og2"])], axis=0)
                for b in range(4)]
    r4 = _run(build_l1mid, prep_l1mid_inputs(inputs, og2_full, h2T_full))
    h3row_full = np.concatenate([np.asarray(r4[c]["h3row"])
                                 for c in range(NCORES)], axis=0)
    e6_full = np.concatenate([np.asarray(r4[c]["e6"]) for c in range(NCORES)],
                             axis=0)
    p6_full = np.concatenate([np.asarray(r4[c]["p6"]) for c in range(NCORES)],
                             axis=0)
    cnt_full = np.concatenate([np.asarray(r4[c]["cnt"])
                               for c in range(NCORES)], axis=0)
    r5 = _run(build_l1route,
              prep_l1route_inputs(inputs, h3row_full, e6_full, p6_full,
                                  cnt_full))
    outr_full = np.concatenate([np.asarray(r5[c]["outr"])
                                for c in range(NCORES)], axis=0)
    h3T_slices = [np.asarray(r4[c]["h3T"]) for c in range(NCORES)]
    shT_slices = [np.asarray(r4[c]["shT"]) for c in range(NCORES)]
    e6_slices = [np.asarray(r4[c]["e6"]) for c in range(NCORES)]
    p6_slices = [np.asarray(r4[c]["p6"]) for c in range(NCORES)]
    w6_slices = [np.asarray(r4[c]["w6"]) for c in range(NCORES)]
    r6 = _run(build_final,
              prep_final_inputs(inputs, outr_full, h3T_slices, shT_slices,
                                e6_slices, p6_slices, w6_slices, cnt_full))
    out = np.zeros((4, T, DM), np.float32)
    for c in range(NCORES):
        b, r = c // 2, c % 2
        out[b, r * TPC:(r + 1) * TPC, :] = np.asarray(r6[c]["out"],
                                                      np.float32)
    return out


# version 07 (final)
# speedup vs baseline: 1.2971x; 1.2971x over previous
import numpy as np
import ml_dtypes
import concourse.bass as bass
import concourse.mybir as mybir
from concourse.bass_utils import run_bass_kernel_spmd
from concourse.bass import IndirectOffsetOnAxis as IOA

F32 = mybir.dt.float32
BF16 = mybir.dt.bfloat16
I32 = mybir.dt.int32
U8 = mybir.dt.uint8
AF = mybir.ActivationFunctionType
OP = mybir.AluOpType
AX = mybir.AxisListType
BF = ml_dtypes.bfloat16

NCORES = 8
DM = 2048
T = 2048
DV = 128
L = 32
JB = 16
NMACRO = 4
MACRO_T = 512
ALL = ("t", "v", "a", "g", "s", "gd", "sd")


class Prog:

    def __init__(self, name="prog"):
        self.nc = bass.Bass("TRN2", target_bir_lowering=False)
        self.sems = {}
        self.cnt = {}
        for e in ALL:
            self.sems[e] = self.nc.alloc_semaphore("sem_" + e)
            self.cnt[e] = 0
        self._block = None
        self._eng_of = {"t": "tensor", "v": "vector", "a": "scalar",
                        "g": "gpsimd", "s": "sync"}

    def sem(self, name):
        if name not in self.sems:
            self.sems[name] = self.nc.alloc_semaphore("sem_" + name)
            self.cnt[name] = 0
        return self.sems[name]

    def block(self):
        self._block = self.nc.Block()
        return self._block

    def _sec(self, eng_name, fn):
        getattr(self._block, eng_name)(fn)

    def step(self, key, fn, dma=False, n_dma=1, waits="chain", sem_name=None):
        eng_name = self._eng_of[key]
        if waits == "chain":
            prev = getattr(self, "last", None)
            wait_list = [(self.sems[prev[0]], prev[1])] if prev else []
        else:
            seen = {}
            for w in waits:
                if w is None:
                    continue
                sk, cval = w
                seen[sk] = max(seen.get(sk, 0), cval)
            wait_list = [(self.sems[sk], cv) for sk, cv in seen.items()
                         if cv > 0]
        if not dma:
            self.cnt[key] += 1
            sem = self.sems[key]
            tok = (key, self.cnt[key])

            def body(eng):
                for s, c in wait_list:
                    eng.wait_ge(s, c)
                if key in ("v", "a"):
                    eng.drain()
                last = fn(eng)
                assert last is not None, "step fn must return its last instruction"
                last.then_inc(sem, 1)
            self._sec(eng_name, body)
            self.last = tok
            return tok
        else:
            qk = sem_name if sem_name is not None else (
                "gd" if key == "g" else "sd")
            self.sem(qk)
            self.cnt[qk] += 16 * n_dma
            sem = self.sems[qk]
            tok = (qk, self.cnt[qk])

            def body(eng):
                for s, c in wait_list:
                    eng.wait_ge(s, c)
                insts = fn(eng)
                assert isinstance(insts, (list, tuple)) and len(insts) == n_dma, \
                    f"expected {n_dma} dmas, got {insts}"
                for i in insts:
                    i.then_inc(sem, 16)
            self._sec(eng_name, body)
            self.last = tok
            return tok

    def final_wait(self):
        wait_list = [(self.sems[w], self.cnt[w]) for w in self.cnt
                     if self.cnt[w] > 0]

        def body(eng):
            for s, c in wait_list:
                eng.wait_ge(s, c)
        self._sec("gpsimd", body)


def sbuf(nc, name, shape, dt):
    return nc.alloc_sbuf_tensor(name, list(shape), dt)


def psum(nc, name, dt=F32, cols=None):
    if cols is None:
        cols = 512 if dt == F32 else 1024
    return nc.alloc_psum_tensor(name, [128, cols], dt)


def _consts_l0():
    ident = np.eye(128, dtype=np.float32)
    ii, jj = np.arange(128)[:, None], np.arange(128)[None, :]
    same = (ii // JB) == (jj // JB)
    rlt = (ii % JB) < (jj % JB)
    rle = (ii % JB) <= (jj % JB)
    mkd_neg = np.where(same & rlt, -1.0, 0.0).astype(np.float32)
    mkf_neg = np.where(same, -1.0, 0.0).astype(np.float32)
    mkg_pos = np.where(same & rle, 1.0, 0.0).astype(np.float32)
    mkf_pos = np.where(same, 1.0, 0.0).astype(np.float32)
    seqm = np.zeros((128, 8 * 128), np.float32)
    for s in range(8):
        seqm[JB * s:JB * (s + 1), 128 * s:128 * (s + 1)] = 1.0
    ones_col = np.ones((128, 1), np.float32)
    ones_row = np.ones((1, 128), np.float32)
    return dict(idf=ident, mkd_neg=mkd_neg, mkf_neg=mkf_neg, mkg_pos=mkg_pos,
                mkf_pos=mkf_pos, seqm=seqm, ones_col=ones_col,
                ones_row=ones_row)



def build_l0attn():
    P = Prog("l0attn")
    nc = P.nc
    xT = nc.dram_tensor("xT", [DM, T], F32, kind="ExternalInput").ap()
    wT = {nm: nc.dram_tensor(nm, [DM, 1024], F32, kind="ExternalInput").ap()
          for nm in ("wq", "wk", "wv", "wa", "wg")}
    wbT = nc.dram_tensor("wb", [DM, 8], F32, kind="ExternalInput").ap()
    convw = {nm: nc.dram_tensor("cw" + nm, [128, 32], F32,
                                kind="ExternalInput").ap() for nm in "qkv"}
    convb = {nm: nc.dram_tensor("cb" + nm, [128, 8], F32,
                                kind="ExternalInput").ap() for nm in "qkv"}
    bacol = nc.dram_tensor("bacol", [128, 8], F32, kind="ExternalInput").ap()
    bbrow = nc.dram_tensor("bbrow", [1, 8], F32, kind="ExternalInput").ap()
    lnwb = nc.dram_tensor("lnwb", [128, 2], F32, kind="ExternalInput").ap()
    cst = {k: nc.dram_tensor("c_" + k, list(v.shape), F32,
                             kind="ExternalInput").ap()
           for k, v in _consts_l0().items()}
    og = nc.dram_tensor("og", [1024, T], BF16, kind="ExternalOutput").ap()

    scr = {
        "q": nc.dram_tensor("s_q", [1024, T], BF16).ap(),
        "k": nc.dram_tensor("s_k", [1024, T], BF16).ap(),
        "v": nc.dram_tensor("s_v", [1024, T], F32).ap(),
        "a": nc.dram_tensor("s_a", [1024, T], F32).ap(),
        "g": nc.dram_tensor("s_g", [1024, T], BF16).ap(),
        "c": nc.dram_tensor("s_c", [1024, T], F32).ap(),
        "plrow": nc.dram_tensor("s_plrow", [8, 128], F32).ap(),
    }

    from contextlib import ExitStack
    es = ExitStack()
    def sb_a(name, shape, dt):
        return es.enter_context(nc.sbuf_tensor(name, list(shape), dt))
    wsb = {nm: sb_a("w_" + nm, (128, 16 * 1024), BF16)
           for nm in ("wq", "wk", "wv", "wa")}
    wgh = sb_a("wgh", (128, 16 * 128), BF16)
    wb_sb = sb_a("w_wb", (128, 16 * 8), BF16)
    xsb = sb_a("xsb", (128, 16 * MACRO_T), BF16)
    SEG = MACRO_T + 4
    raw = sb_a("raw", (128, 8 * SEG), F32)
    hist = sb_a("hist", (128, 3 * 24), F32)
    stage = sb_a("stage", (128, 8 * MACRO_T), F32)
    stage2 = sb_a("stage2", (128, 8 * MACRO_T), BF16)
    cw_sb = {nm: sb_a("cw_" + nm, (128, 32), F32) for nm in "qkv"}
    cb_sb = {nm: sb_a("cb_" + nm, (128, 8), F32) for nm in "qkv"}
    ba_sb = sb_a("ba_sb", (128, 8), F32)
    bb_sb = sb_a("bb_sb", (1, 8), F32)
    brow = sb_a("brow", (1, MACRO_T), F32)
    ones_row_a = sb_a("ones_row_a", (1, 128), F32)
    psP = es.enter_context(nc.psum_tensor("psP0", [128, 512], F32))
    psP2 = es.enter_context(nc.psum_tensor("psP2", [128, 512], F32))

    with P.block():
        def ld_small(g):
            out = []
            for nm in "qkv":
                out.append(g.dma_start(out=cw_sb[nm][:, :], in_=convw[nm]))
                out.append(g.dma_start(out=cb_sb[nm][:, :], in_=convb[nm]))
            out.append(g.dma_start(out=ba_sb[:, :], in_=bacol))
            out.append(g.dma_start(out=bb_sb[:, :], in_=bbrow))
            return out
        P.step("g", ld_small, dma=True, n_dma=8)

        def init_ones(v):
            return v.memset(ones_row_a[:, :], 1.0)
        P.step("v", init_ones)

        def ld_w(g):
            out = []
            for nm in ("wq", "wk", "wv", "wa"):
                src = wT[nm].rearrange("(ck p) c -> p ck c", p=128)
                out.append(g.dma_start(
                    out=wsb[nm][:, :].rearrange("p (ck c) -> p ck c", ck=16),
                    in_=src))
            out.append(g.dma_start(
                out=wb_sb[:, :].rearrange("p (ck c) -> p ck c", ck=16),
                in_=wbT.rearrange("(ck p) c -> p ck c", p=128)))
            return out
        P.step("g", ld_w, dma=True, n_dma=5)

        for tt in range(NMACRO):
            def ld_x(g, tt=tt):
                src = xT[:, tt * MACRO_T:(tt + 1) * MACRO_T].rearrange(
                    "(ck p) t -> p ck t", p=128)
                return [g.dma_start(out=xsb[:, :].rearrange(
                    "p (ck t) -> p ck t", ck=16), in_=src)]
            P.step("g", ld_x, dma=True, n_dma=1)

            for tau in ("q", "k", "v", "a", "g"):
                wname = {"q": "wq", "k": "wk", "v": "wv", "a": "wa",
                         "g": "wg"}[tau]
                for h in range(8):
                    if tau == "g":
                        def ld_wg(g, h=h):
                            src = wT["wg"][:, h * 128:(h + 1) * 128].rearrange(
                                "(ck p) c -> p ck c", p=128)
                            return [g.dma_start(out=wgh[:, :].rearrange(
                                "p (ck c) -> p ck c", ck=16), in_=src)]
                        P.step("g", ld_wg, dma=True, n_dma=1)

                        def mm_p(t):
                            last = None
                            for ck in range(16):
                                last = t.matmul(
                                    psP[:, 0:MACRO_T],
                                    wgh[:, ck * 128:(ck + 1) * 128],
                                    xsb[:, ck * MACRO_T:(ck + 1) * MACRO_T],
                                    start=(ck == 0), stop=(ck == 15))
                            return last
                        P.step("t", mm_p)
                    else:
                        def mm_p(t, h=h, wname=wname):
                            last = None
                            for ck in range(16):
                                last = t.matmul(
                                    psP[:, 0:MACRO_T],
                                    wsb[wname][:, ck * 1024 + h * 128:
                                               ck * 1024 + (h + 1) * 128],
                                    xsb[:, ck * MACRO_T:(ck + 1) * MACRO_T],
                                    start=(ck == 0), stop=(ck == 15))
                            return last
                        P.step("t", mm_p)

                    if tau in "qkv":
                        def ev_qkv(a, h=h):
                            return a.activation(
                                out=raw[:, h * SEG + 4:(h + 1) * SEG],
                                in_=psP[:, 0:MACRO_T], func=AF.Identity)
                        P.step("a", ev_qkv)
                    elif tau == "a":
                        def ev_a(a, h=h):
                            return a.activation(
                                out=stage[:, h * MACRO_T:(h + 1) * MACRO_T],
                                in_=psP[:, 0:MACRO_T], func=AF.Sigmoid,
                                bias=ba_sb[:, h:h + 1])
                        P.step("a", ev_a)
                    else:
                        def ev_g(a, h=h):
                            return a.activation(
                                out=stage2[:, h * MACRO_T:(h + 1) * MACRO_T],
                                in_=psP[:, 0:MACRO_T], func=AF.Sigmoid)
                        P.step("a", ev_g)

                if tau in "qkv":
                    hseg = {"q": 0, "k": 1, "v": 2}[tau]
                    r3 = raw[:, :].rearrange("p (h c) -> p h c", h=8)
                    if tt == 0:
                        def z_hist(v):
                            return v.memset(r3[:, :, 1:4], 0.0)
                        P.step("v", z_hist)
                    else:
                        def cp_hist(v, hseg=hseg):
                            return v.tensor_copy(
                                r3[:, :, 1:4],
                                hist[:, hseg * 24:(hseg + 1) * 24].rearrange(
                                    "p (h c) -> p h c", h=8))
                        P.step("v", cp_hist)

                    def sv_hist(v, hseg=hseg):
                        return v.tensor_copy(
                            hist[:, hseg * 24:(hseg + 1) * 24].rearrange(
                                "p (h c) -> p h c", h=8),
                            r3[:, :, SEG - 3:SEG])
                    P.step("v", sv_hist)

                    def conv_h(v, tau=tau):
                        last = None
                        for h in range(8):
                            base = h * SEG
                            last = v.tensor_scalar(
                                stage[:, h * MACRO_T:(h + 1) * MACRO_T],
                                raw[:, base + 1:base + 1 + MACRO_T],
                                cw_sb[tau][:, 4 * h:4 * h + 1],
                                cb_sb[tau][:, h:h + 1], OP.mult, OP.add)
                            for j in range(1, 4):
                                v.drain()
                                last = v.scalar_tensor_tensor(
                                    stage[:, h * MACRO_T:(h + 1) * MACRO_T],
                                    raw[:, base + 1 + j:base + 1 + j + MACRO_T],
                                    cw_sb[tau][:, 4 * h + j:4 * h + j + 1],
                                    stage[:, h * MACRO_T:(h + 1) * MACRO_T],
                                    OP.mult, OP.add)
                            v.drain()
                        return last
                    P.step("v", conv_h)

                    if tau in ("q", "k"):
                        def silu_bf(a):
                            return a.activation(out=stage2[:, :],
                                                in_=stage[:, :], func=AF.Silu)
                        P.step("a", silu_bf)
                        if tau == "k":
                            def kscale(v):
                                return v.tensor_scalar(
                                    stage2[:, :], stage2[:, :],
                                    float(DV) ** -0.5, None, OP.mult)
                            P.step("v", kscale)
                    else:
                        def silu32(a):
                            return a.activation(out=stage[:, :],
                                                in_=stage[:, :], func=AF.Silu)
                        P.step("a", silu32)

                def store(s, tau=tau, tt=tt):
                    srcb = stage if tau in ("v", "a") else stage2
                    d = scr[tau][:, tt * MACRO_T:(tt + 1) * MACRO_T].rearrange(
                        "(h p) t -> p h t", p=128)
                    return [s.dma_start(out=d, in_=srcb[:, :].rearrange(
                        "p (h t) -> p h t", h=8))]
                P.step("s", store, dma=True, n_dma=1)

                if tau == "a":
                    def recip_a(v):
                        return v.reciprocal(stage[:, :], stage[:, :])
                    P.step("v", recip_a)
                    for h in range(8):
                        def mm_beta(t, h=h):
                            last = None
                            for ck in range(16):
                                last = t.matmul(
                                    psP2[0:1, 0:MACRO_T],
                                    wb_sb[:, ck * 8 + h:ck * 8 + h + 1],
                                    xsb[:, ck * MACRO_T:(ck + 1) * MACRO_T],
                                    start=(ck == 0), stop=(ck == 15))
                            return last
                        P.step("t", mm_beta)

                        def ev_beta(a, h=h):
                            return a.activation(out=brow[:, :],
                                                in_=psP2[0:1, 0:MACRO_T],
                                                func=AF.Sigmoid,
                                                bias=bb_sb[0:1, h:h + 1])
                        P.step("a", ev_beta)

                        def bc_beta(t):
                            return t.matmul(psP2[:, 0:MACRO_T],
                                            ones_row_a[:, :], brow[:, :],
                                            start=True, stop=True)
                        P.step("t", bc_beta)

                        def cmul(v, h=h):
                            return v.tensor_tensor(
                                raw[:, h * SEG + 4:(h + 1) * SEG],
                                stage[:, h * MACRO_T:(h + 1) * MACRO_T],
                                psP2[:, 0:MACRO_T], OP.mult)
                        P.step("v", cmul)

                    def st_c(s, tt=tt):
                        d = scr["c"][:, tt * MACRO_T:(tt + 1) * MACRO_T].rearrange(
                            "(h p) t -> p h t", p=128)
                        return [s.dma_start(out=d, in_=raw[:, :].rearrange(
                            "p (h c) -> p h c", h=8)[:, :, 4:SEG])]
                    P.step("s", st_c, dma=True, n_dma=1)

        es.close()
        emit_scan(P, scr, cst, lnwb, og)
        P.final_wait()
    return P


def emit_scan(P, scr, cst, lnwb, og):
    nc = P.nc
    qT = sbuf(nc, "qT", (128, 8 * MACRO_T), BF16)
    kT = sbuf(nc, "kT", (128, 8 * MACRO_T), BF16)
    vT = sbuf(nc, "vT", (128, 8 * MACRO_T), F32)
    aT = sbuf(nc, "aT", (128, 8 * MACRO_T), F32)
    gT = sbuf(nc, "gT", (128, 8 * MACRO_T), BF16)
    cT = sbuf(nc, "cT", (128, 8 * MACRO_T), F32)
    ogm = sbuf(nc, "ogm", (128, 8 * MACRO_T), BF16)

    idf = sbuf(nc, "idf_sb", (128, 128), F32)
    idb = sbuf(nc, "idb_sb", (128, 128), BF16)
    mk = sbuf(nc, "mk_sb", (128, 4 * 128), F32)
    seqm = sbuf(nc, "seqm_sb", (128, 8 * 128), F32)
    ones_col = sbuf(nc, "ones_sb", (128, 1), F32)
    ones_row = sbuf(nc, "onesr_sb", (1, 128), F32)
    lnwb_sb = sbuf(nc, "lnwb_sb", (128, 2), F32)

    Smas = sbuf(nc, "Smas", (128, 8 * 128), F32)
    Sbf = sbuf(nc, "Sbf", (128, 8 * 128), BF16)
    Pbuf = sbuf(nc, "Pbuf", (128, 8 * 33), F32)
    Pbc = sbuf(nc, "Pbc", (128, 8 * 128), F32)
    plrow_sb = sbuf(nc, "plrow_sb", (1, 1024), F32)
    plc = sbuf(nc, "plc", (128, 8), F32)
    plc2 = sbuf(nc, "plc2", (8, 128), F32)

    recA = sbuf(nc, "recA", (128, 256), F32)
    vn = sbuf(nc, "vn", (128, 256), F32)
    rp = sbuf(nc, "rp", (128, 256), F32)
    ctg = sbuf(nc, "ctg", (128, 128), F32)
    Kblk = [sbuf(nc, "Kblk0", (128, 128), BF16),
            sbuf(nc, "Kblk1", (128, 128), BF16)]
    Kblk32 = sbuf(nc, "Kblk32", (128, 128), F32)
    Qblk = [sbuf(nc, "Qblk0", (128, 128), BF16),
            sbuf(nc, "Qblk1", (128, 128), BF16)]
    Mm = [sbuf(nc, "Mm0", (128, 128), BF16), sbuf(nc, "Mm1", (128, 128), BF16)]
    Mx = sbuf(nc, "Mx", (128, 128), BF16)
    Gm = [sbuf(nc, "Gm0", (128, 128), BF16), sbuf(nc, "Gm1", (128, 128), BF16)]
    Gx = sbuf(nc, "Gx", (128, 128), BF16)
    Cst = [sbuf(nc, "Cst0", (128, 128), F32), sbuf(nc, "Cst1", (128, 128), F32)]
    Rst = [sbuf(nc, "Rst0", (128, 128), BF16),
           sbuf(nc, "Rst1", (128, 128), BF16)]
    mI = sbuf(nc, "mI", (128, 128), BF16)
    Ub = [sbuf(nc, "Ub0", (128, 128), BF16), sbuf(nc, "Ub1", (128, 128), BF16)]
    KmT = sbuf(nc, "KmT", (128, 8 * 128), BF16)
    KmT1 = sbuf(nc, "KmT1", (128, 8 * 128), BF16)
    GU = sbuf(nc, "GU", (128, 128), F32)
    Otile = sbuf(nc, "Otile", (128, 128), F32)
    sq = sbuf(nc, "sq", (128, 128), F32)
    strow = sbuf(nc, "strow", (1, 6 * 128), F32)

    psG0 = psum(nc, "psG0")
    psG1 = psum(nc, "psG1")
    psJ = psum(nc, "psJ")
    psSQ = psum(nc, "psSQ")
    psSt = psum(nc, "psSt")
    psSt2 = psum(nc, "psSt2")
    psM = psum(nc, "psM")

    def ld_c(g):
        return [
            g.dma_start(out=idf[:, :], in_=cst["idf"]),
            g.dma_start(out=idb[:, :], in_=cst["idf"]),
            g.dma_start(out=mk[:, 0:128], in_=cst["mkd_neg"]),
            g.dma_start(out=mk[:, 128:256], in_=cst["mkf_neg"]),
            g.dma_start(out=mk[:, 256:384], in_=cst["mkg_pos"]),
            g.dma_start(out=mk[:, 384:512], in_=cst["mkf_pos"]),
            g.dma_start(out=seqm[:, :], in_=cst["seqm"]),
            g.dma_start(out=ones_col[:, :], in_=cst["ones_col"]),
            g.dma_start(out=ones_row[:, :], in_=cst["ones_row"]),
            g.dma_start(out=lnwb_sb[:, :], in_=lnwb),
        ]
    P.step("g", ld_c, dma=True, n_dma=10)

    def init_state(v):
        v.memset(Smas[:, :], 0.0)
        v.drain()
        v.memset(Sbf[:, :], 0.0)
        v.drain()
        return v.memset(Pbuf[:, :], 1.0)
    P.step("v", init_state)

    MKD, MKF, MKG, MKGF = (mk[:, 0:128], mk[:, 128:256], mk[:, 256:384],
                           mk[:, 384:512])

    for mm_i in range(NMACRO):
        def ld_m(g, mm_i=mm_i):
            out = []
            for nm, dstt in (("q", qT), ("k", kT), ("v", vT), ("a", aT),
                             ("g", gT), ("c", cT)):
                src = scr[nm][:, mm_i * MACRO_T:(mm_i + 1) * MACRO_T].rearrange(
                    "(h p) t -> p h t", p=128)
                out.append(g.dma_start(
                    out=dstt[:, :].rearrange("p (h t) -> p h t", h=8), in_=src))
            return out
        P.step("g", ld_m, dma=True, n_dma=6)

        for cc in range(MACRO_T // L):
            c0 = cc * L

            def hsl(tens, off, w, c0=c0):
                return tens[:, :].rearrange("p (h t) -> p h t",
                                            h=8)[:, :, c0 + off:c0 + off + w]

            def pscan(v, c0=c0):
                last = None
                for s in range(8):
                    sl_a = aT[:, s * MACRO_T + c0:s * MACRO_T + c0 + L]
                    last = v.tensor_tensor_scan(
                        Pbuf[:, 33 * s + 1:33 * s + 33], sl_a, sl_a,
                        1.0, OP.mult, OP.bypass)
                    v.drain()
                return last
            P.step("v", pscan)

            def cp_pl(v):
                return v.tensor_copy(
                    plc[:, :],
                    Pbuf[:, :].rearrange("p (s c) -> p s c",
                                         s=8)[:, :, 32:33].rearrange(
                        "p s one -> p (s one)"))
            P.step("v", cp_pl)

            def tr_pl(t):
                return t.transpose(psM[0:8, 0:128], plc[:, :], idf[:, :])
            P.step("t", tr_pl)

            def ev_pl(v):
                return v.tensor_copy(plc2[:, :], psM[0:8, 0:128])
            P.step("v", ev_pl)

            def st_pl(s):
                return [s.dma_start(out=scr["plrow"], in_=plc2[:, :])]
            P.step("s", st_pl, dma=True, n_dma=1)

            def ld_pl(g):
                src = scr["plrow"].rearrange("s p -> (s p)")[None, :]
                return [g.dma_start(out=plrow_sb[:, :], in_=src)]
            P.step("g", ld_pl, dma=True, n_dma=1)

            def vn_build(v, c0=c0):
                pprev = Pbuf[:, :].rearrange("p (s c) -> p s c", s=8)[:, :, 0:32]
                r8 = recA[:, :].rearrange("p (s t) -> p s t", s=8)
                v.reciprocal(r8, pprev)
                v.drain()
                return v.tensor_tensor(
                    vn[:, :].rearrange("p (s t) -> p s t", s=8),
                    hsl(vT, 0, L), r8, OP.mult)
            P.step("v", vn_build)

            def blkcopies(v, c0=c0):
                for b in range(2):
                    v.tensor_copy(Kblk[b][:, :].rearrange(
                        "p (s t) -> p s t", t=16), hsl(kT, 16 * b, 16))
                    v.drain()
                    v.tensor_copy(Qblk[b][:, :].rearrange(
                        "p (s t) -> p s t", t=16), hsl(qT, 16 * b, 16))
                    v.drain()
                return v.tensor_copy(Kblk32[:, :].rearrange(
                    "p (s t) -> p s t", t=16), hsl(kT, 0, 16))
            P.step("v", blkcopies)

            def grams(t):
                t.matmul(psG0[:, 0:128], Kblk[0][:, :], Kblk[0][:, :],
                         start=True, stop=True)
                t.matmul(psG1[:, 0:128], Kblk[1][:, :], Kblk[1][:, :],
                         start=True, stop=True)
                t.matmul(psG0[:, 128:256], Kblk[0][:, :], Kblk[1][:, :],
                         start=True, stop=True)
                t.matmul(psG1[:, 128:256], Kblk[0][:, :], Qblk[0][:, :],
                         start=True, stop=True)
                t.matmul(psG0[:, 256:384], Kblk[1][:, :], Qblk[1][:, :],
                         start=True, stop=True)
                return t.matmul(psG1[:, 256:384], Kblk[0][:, :], Qblk[1][:, :],
                                start=True, stop=True)
            P.step("t", grams)

            def maskev(v):
                v.tensor_tensor(Mm[0][:, :], psG0[:, 0:128], MKD, OP.mult)
                v.drain()
                v.tensor_tensor(Mm[1][:, :], psG1[:, 0:128], MKD, OP.mult)
                v.drain()
                v.tensor_tensor(Mx[:, :], psG0[:, 128:256], MKF, OP.mult)
                v.drain()
                v.tensor_tensor(Gm[0][:, :], psG1[:, 128:256], MKG, OP.mult)
                v.drain()
                v.tensor_tensor(Gm[1][:, :], psG0[:, 256:384], MKG, OP.mult)
                v.drain()
                return v.tensor_tensor(Gx[:, :], psG1[:, 256:384], MKGF,
                                       OP.mult)
            P.step("v", maskev)

            def s0k(t, c0=c0):
                last = None
                for s in range(8):
                    last = t.matmul(
                        psSQ[:, 32 * s:32 * s + 32],
                        Sbf[:, 128 * s:128 * (s + 1)],
                        kT[:, s * MACRO_T + c0:s * MACRO_T + c0 + L],
                        start=True, stop=True)
                return last
            P.step("t", s0k)

            def rpbuild(v):
                return v.scalar_tensor_tensor(rp[:, :], psSQ[:, 0:256], -1.0,
                                              vn[:, :], OP.mult, OP.add)
            P.step("v", rpbuild)

            for b in range(2):
                def cp_rp(v, b=b):
                    return v.tensor_copy(
                        ctg[:, :].rearrange("p (s t) -> p s t", t=16),
                        rp[:, :].rearrange("p (s t) -> p s t",
                                           t=32)[:, :, 16 * b:16 * b + 16])
                P.step("v", cp_rp)

                def tr_rp(t):
                    return t.transpose(psM[:, 0:128], ctg[:, :], idf[:, :])
                P.step("t", tr_rp)

                def ev_rp(v, b=b):
                    return v.tensor_copy(Rst[b][:, :], psM[:, 0:128])
                P.step("v", ev_rp)

                def cp_ra(v, b=b, c0=c0):
                    return v.tensor_copy(
                        ctg[:, :].rearrange("p (s t) -> p s t", t=16),
                        hsl(cT, 16 * b, 16))
                P.step("v", cp_ra)

                def tr_ra(t):
                    return t.transpose(psM[:, 0:128], ctg[:, :], idf[:, :])
                P.step("t", tr_ra)

                def ev_ca(v, b=b):
                    return v.tensor_copy(Cst[b][:, :], psM[:, 0:128])
                P.step("v", ev_ca)

            for b in range(2):
                if b == 1:
                    def crossmm(t):
                        t.matmul(psJ[:, 0:128], idb[:, :], Rst[1][:, :],
                                 start=True, stop=False)
                        return t.matmul(psJ[:, 0:128], Mx[:, :], Ub[0][:, :],
                                        start=False, stop=True)
                    P.step("t", crossmm)

                    def evcross(v):
                        return v.tensor_copy(Rst[1][:, :], psJ[:, 0:128])
                    P.step("v", evcross)

                for it in range(JB):
                    def jmm(t, b=b, it=it):
                        last = t.matmul(psJ[:, 0:128], idb[:, :], Rst[b][:, :],
                                        start=True, stop=(it == 0))
                        if it > 0:
                            last = t.matmul(psJ[:, 0:128], Mm[b][:, :],
                                            mI[:, :], start=False, stop=True)
                        return last
                    P.step("t", jmm)

                    out_t = Ub[b] if it == JB - 1 else mI

                    def jdve(v, out_t=out_t, b=b):
                        return v.tensor_tensor(out_t[:, :], psJ[:, 0:128],
                                               Cst[b][:, :], OP.mult)
                    P.step("v", jdve)

            for b in range(2):
                def gu_mm(t, b=b):
                    if b == 0:
                        return t.matmul(psG0[:, 0:128], Gm[0][:, :],
                                        Ub[0][:, :], start=True, stop=True)
                    t.matmul(psG0[:, 0:128], Gx[:, :], Ub[0][:, :],
                             start=True, stop=False)
                    return t.matmul(psG0[:, 0:128], Gm[1][:, :], Ub[1][:, :],
                                    start=False, stop=True)
                P.step("t", gu_mm)

                def gu_ev(v):
                    return v.tensor_copy(GU[:, :], psG0[:, 0:128])
                P.step("v", gu_ev)

                def gu_tr(t, b=b, c0=c0):
                    t.transpose(psG1[:, 0:128], GU[:, :], idf[:, :])
                    last = None
                    for s in range(8):
                        last = t.matmul(
                            psSQ[:, 256 + 16 * s:256 + 16 * s + 16],
                            Sbf[:, 128 * s:128 * (s + 1)],
                            qT[:, s * MACRO_T + c0 + 16 * b:
                               s * MACRO_T + c0 + 16 * b + 16],
                            start=True, stop=True)
                    return last
                P.step("t", gu_tr)

                def ev_gut(v):
                    return v.tensor_copy(GU[:, :], psG1[:, 0:128])
                P.step("v", ev_gut)

                def o_sum(v):
                    return v.scalar_tensor_tensor(
                        Otile[:, :], psSQ[:, 256:384], 1.0, GU[:, :],
                        OP.mult, OP.add)
                P.step("v", o_sum)

                def o_scale(v, b=b):
                    pc = Pbuf[:, :].rearrange(
                        "p (s c) -> p s c", s=8)[:, :, 1 + 16 * b:17 + 16 * b]
                    return v.tensor_tensor(
                        Otile[:, :].rearrange("p (s t) -> p s t", t=16),
                        Otile[:, :].rearrange("p (s t) -> p s t", t=16),
                        pc, OP.mult)
                P.step("v", o_scale)

                def sq_op(a):
                    return a.activation(out=sq[:, :], in_=Otile[:, :],
                                        func=AF.Square)
                P.step("a", sq_op)

                def stat_mm(t):
                    t.matmul(psM[0:1, 0:128], ones_col[:, :], Otile[:, :],
                             start=True, stop=True)
                    return t.matmul(psM[0:1, 128:256], ones_col[:, :],
                                    sq[:, :], start=True, stop=True)
                P.step("t", stat_mm)

                def stats(v):
                    v.tensor_scalar(strow[:, 0:128], psM[0:1, 0:128],
                                    1.0 / 128.0, None, OP.mult)
                    v.drain()
                    v.tensor_scalar(strow[:, 128:256], psM[0:1, 128:256],
                                    1.0 / 128.0, None, OP.mult)
                    v.drain()
                    v.tensor_tensor(strow[:, 256:384], strow[:, 0:128],
                                    strow[:, 0:128], OP.mult)
                    v.drain()
                    v.tensor_tensor(strow[:, 256:384], strow[:, 128:256],
                                    strow[:, 256:384], OP.subtract)
                    v.drain()
                    return v.tensor_scalar(strow[:, 256:384],
                                           strow[:, 256:384], 1e-5, None,
                                           OP.add)
                P.step("v", stats)

                def rstd_op(a):
                    return a.activation(out=strow[:, 384:512],
                                        in_=strow[:, 256:384], func=AF.Sqrt)
                P.step("a", rstd_op)

                def rstd2(v):
                    return v.reciprocal(strow[:, 256:384], strow[:, 384:512])
                P.step("v", rstd2)

                def bc_mm(t):
                    t.matmul(psM[:, 128:256], ones_row[:, :], strow[:, 0:128],
                             start=True, stop=True)
                    return t.matmul(psM[:, 256:384], ones_row[:, :],
                                    strow[:, 256:384], start=True, stop=True)
                P.step("t", bc_mm)

                def lnapply(v, b=b, c0=c0):
                    v.tensor_tensor(Otile[:, :], Otile[:, :], psM[:, 128:256],
                                    OP.subtract)
                    v.drain()
                    v.tensor_tensor(Otile[:, :], Otile[:, :], psM[:, 256:384],
                                    OP.mult)
                    v.drain()
                    v.tensor_scalar(Otile[:, :], Otile[:, :], lnwb_sb[:, 0:1],
                                    lnwb_sb[:, 1:2], OP.mult, OP.add)
                    v.drain()
                    return v.tensor_tensor(
                        ogm[:, :].rearrange("p (s t) -> p s t",
                                            s=8)[:, :, c0 + 16 * b:
                                                 c0 + 16 * b + 16],
                        Otile[:, :].rearrange("p (s t) -> p s t", t=16),
                        hsl(gT, 16 * b, 16), OP.mult)
                P.step("v", lnapply)

            def kst_tr0(t):
                return t.transpose(psSt[:, 0:128], Kblk32[:, :], idf[:, :])
            P.step("t", kst_tr0)

            def kmask0(v):
                last = None
                for s in range(8):
                    last = v.tensor_tensor(KmT[:, 128 * s:128 * (s + 1)],
                                           psSt[:, 0:128],
                                           seqm[:, 128 * s:128 * (s + 1)],
                                           OP.mult)
                    v.drain()
                return last
            P.step("v", kmask0)

            def cp_k32b1(v, c0=c0):
                return v.tensor_copy(
                    Kblk32[:, :].rearrange("p (s t) -> p s t", t=16),
                    hsl(kT, 16, 16))
            P.step("v", cp_k32b1)

            def kst_tr1(t):
                return t.transpose(psM[:, 0:128], Kblk32[:, :], idf[:, :])
            P.step("t", kst_tr1)

            def kmask1(v):
                last = None
                for s in range(8):
                    last = v.tensor_tensor(KmT1[:, 128 * s:128 * (s + 1)],
                                           psM[:, 0:128],
                                           seqm[:, 128 * s:128 * (s + 1)],
                                           OP.mult)
                    v.drain()
                return last
            P.step("v", kmask1)

            def smm_all(t):
                last = None
                for s in range(8):
                    tgt = psSt if s < 4 else psSt2
                    off = 128 * (s % 4)
                    last = t.matmul(tgt[:, off:off + 128],
                                    KmT[:, 128 * s:128 * (s + 1)],
                                    Ub[0][:, :], start=True, stop=False)
                    last = t.matmul(tgt[:, off:off + 128],
                                    KmT1[:, 128 * s:128 * (s + 1)],
                                    Ub[1][:, :], start=False, stop=False)
                    last = t.matmul(tgt[:, off:off + 128], idb[:, :],
                                    Sbf[:, 128 * s:128 * (s + 1)],
                                    start=False, stop=True)
                return last
            P.step("t", smm_all)

            def pbc_mm(t):
                t.matmul(psG0[:, 0:512], ones_row[:, :], plrow_sb[:, 0:512],
                         start=True, stop=True)
                return t.matmul(psG1[:, 0:512], ones_row[:, :],
                                plrow_sb[:, 512:1024], start=True, stop=True)
            P.step("t", pbc_mm)

            def pbc_ev(v):
                v.tensor_copy(Pbc[:, 0:512], psG0[:, 0:512])
                v.drain()
                return v.tensor_copy(Pbc[:, 512:1024], psG1[:, 0:512])
            P.step("v", pbc_ev)

            def st_scale(v):
                v.tensor_tensor(Smas[:, 0:512], psSt[:, 0:512], Pbc[:, 0:512],
                                OP.mult)
                v.drain()
                v.tensor_tensor(Smas[:, 512:1024], psSt2[:, 0:512],
                                Pbc[:, 512:1024], OP.mult)
                v.drain()
                return v.tensor_copy(Sbf[:, :], Smas[:, :])
            P.step("v", st_scale)

        def st_og(s, mm_i=mm_i):
            d = og[:, mm_i * MACRO_T:(mm_i + 1) * MACRO_T].rearrange(
                "(h p) t -> p h t", p=128)
            return [s.dma_start(out=d, in_=ogm[:, :].rearrange(
                "p (h t) -> p h t", h=8))]
        P.step("s", st_og, dma=True, n_dma=1)



def prep_l0attn_inputs(inputs):
    x = np.ascontiguousarray(inputs["L0_attn_x"], dtype=np.float32)
    consts = _consts_l0()
    per_core = []
    for c in range(NCORES):
        b = c // 2
        r = c % 2
        hlo = 8 * r
        rows = slice(hlo * 128, (hlo + 8) * 128)
        d = {
            "xT": np.ascontiguousarray(x[b].T),
            "wq": np.ascontiguousarray(inputs["L0_attn_Wq"][rows].T),
            "wk": np.ascontiguousarray(inputs["L0_attn_Wk"][rows].T),
            "wv": np.ascontiguousarray(inputs["L0_attn_Wv"][rows].T),
            "wa": np.ascontiguousarray(inputs["L0_attn_Wa"][rows].T),
            "wg": np.ascontiguousarray(inputs["L0_attn_Wg"][rows].T),
            "wb": np.ascontiguousarray(inputs["L0_attn_Wb"][hlo:hlo + 8].T),
            "bacol": np.ascontiguousarray(
                inputs["L0_attn_ba"][rows].reshape(8, 128).T),
            "bbrow": np.ascontiguousarray(
                inputs["L0_attn_bb"][hlo:hlo + 8].reshape(1, 8)),
            "lnwb": np.ascontiguousarray(np.stack(
                [inputs["L0_attn_ln_w"], inputs["L0_attn_ln_b"]], axis=1)),
        }
        for nm, key in (("q", "qconv"), ("k", "kconv"), ("v", "vconv")):
            w = inputs[f"L0_attn_{key}_w"][rows]
            bvec = inputs[f"L0_attn_{key}_b"][rows]
            d["cw" + nm] = np.ascontiguousarray(
                w.reshape(8, 128, 4).transpose(1, 0, 2).reshape(128, 32))
            d["cb" + nm] = np.ascontiguousarray(
                bvec.reshape(8, 128).T)
        for k, v in consts.items():
            d["c_" + k] = v
        per_core.append({k: np.ascontiguousarray(v, dtype=np.float32)
                         for k, v in d.items()})
    return per_core





TPC = 1024


def _consts_a2():
    return dict(idf=np.eye(128, dtype=np.float32),
                ones_col=np.ones((128, 1), np.float32),
                ones_row=np.ones((1, 128), np.float32),
                iota8=np.tile(np.arange(8, dtype=np.float32), (128, 1)),
                riota8=np.tile(np.arange(7, -1, -1, dtype=np.float32),
                               (128, 1)))


def build_l0rest():
    P = Prog("l0rest")
    nc = P.nc
    TH = TPC // 2
    og_in = nc.dram_tensor("ogf", [2048, TPC], BF16, kind="ExternalInput").ap()
    xT = nc.dram_tensor("xT", [2048, TPC], F32, kind="ExternalInput").ap()
    woT = nc.dram_tensor("woT", [2048, 2048], F32, kind="ExternalInput").ap()
    gwT = nc.dram_tensor("gwT", [2048, 8], F32, kind="ExternalInput").ap()
    wge = nc.dram_tensor("wge", [8 * 2048, 1024], F32, kind="ExternalInput").ap()
    wue = nc.dram_tensor("wue", [8 * 2048, 1024], F32, kind="ExternalInput").ap()
    wde = nc.dram_tensor("wde", [8 * 1024, 2048], F32, kind="ExternalInput").ap()
    shgT = nc.dram_tensor("shgT", [2048, 2048], F32, kind="ExternalInput").ap()
    shuT = nc.dram_tensor("shuT", [2048, 2048], F32, kind="ExternalInput").ap()
    shdT = nc.dram_tensor("shdT", [2048, 2048], F32, kind="ExternalInput").ap()
    seP = nc.dram_tensor("seP", [128, 16], F32, kind="ExternalInput").ap()
    kvdw = nc.dram_tensor("kvdw", [2048, 512], F32, kind="ExternalInput").ap()
    cst = {k: nc.dram_tensor("c_" + k, list(v.shape), F32,
                             kind="ExternalInput").ap()
           for k, v in _consts_a2().items()}
    h2T_o = nc.dram_tensor("h2T", [2048, TPC], F32, kind="ExternalOutput").ap()
    ckvT_o = nc.dram_tensor("ckvT", [512, TPC], F32, kind="ExternalOutput").ap()
    h1scr = nc.dram_tensor("h1scr", [2048, TPC], F32).ap()
    ckvscr = nc.dram_tensor("ckvscr", [512, TPC], F32).ap()
    combscr = nc.dram_tensor("combscr", [8, TPC], F32).ap()

    h1b = sbuf(nc, "h1b", (128, 16 * TPC), BF16)
    idf = sbuf(nc, "idf_sb", (128, 128), F32)
    ones_col = sbuf(nc, "ones_sb", (128, 1), F32)
    ones_row = sbuf(nc, "onesr_sb", (1, 128), F32)
    wtile = sbuf(nc, "wtile", (128, 16 * 128), BF16)
    wtile2 = sbuf(nc, "wtile2", (128, 16 * 128), BF16)
    wtile3 = sbuf(nc, "wtile3", (128, 16 * 128), BF16)
    xck = sbuf(nc, "xck", (128, TPC), F32)
    xck2 = sbuf(nc, "xck2", (128, TPC), F32)
    tmp1 = sbuf(nc, "tmp1", (128, TPC), F32)
    tmp1b = sbuf(nc, "tmp1b", (128, TPC), F32)
    tmp2 = sbuf(nc, "tmp2", (128, TPC), F32)
    cbsb = sbuf(nc, "cbsb", (128, TPC), F32)
    sgb = sbuf(nc, "sgb", (128, TPC), F32)
    strows = sbuf(nc, "strows", (1, 3 * TPC), F32)
    srow = sbuf(nc, "srow", (1, TPC), F32)
    stok = sbuf(nc, "stok", (128, (TPC // 128) * 8), F32)
    ctok = sbuf(nc, "ctok", (128, (TPC // 128) * 8), F32)
    crows = sbuf(nc, "crows", (8, TPC), F32)
    small = sbuf(nc, "small", (128, 64), F32)

    ps0 = psum(nc, "ps0")
    ps1 = psum(nc, "ps1")
    ps2 = psum(nc, "ps2")
    ps3 = psum(nc, "ps3")
    psS = psum(nc, "psS")
    psB = psum(nc, "psB")

    NT = TPC // 128
    wt3 = [wtile, wtile2, wtile3]
    xb2 = [xck, xck2]
    t1p = [tmp1, tmp1b]

    def ldw_chunkcol(g, srcap, dst, col0, ncol=128, nck=16):
        src = srcap[:, col0:col0 + ncol].rearrange("(ck p) c -> p ck c", p=128)
        return [g.dma_start(out=dst[:, 0:nck * ncol].rearrange(
            "p (ck c) -> p ck c", ck=nck), in_=src)]

    with P.block():
        wstate = {"i": 0}
        wcons = [None, None, None]
        wtok = [None, None, None]

        def wload(mkf, n_dma=1):
            i = wstate["i"] % 3
            q = "g"

            def fn(eng, mkf=mkf, tile=wt3[i]):
                return mkf(eng, tile)
            wtok[i] = P.step(q, fn, dma=True, n_dma=n_dma,
                             waits=(wcons[i],), sem_name="wt%d" % i)
            wstate["i"] += 1
            return i

        xstate = {"i": 0}
        xcons = [None, None]
        xtok = [None, None]

        def xload(mkf, extra=()):
            i = xstate["i"] % 2
            q = ("g", "s")[xstate["i"] % 2]

            def fn(eng, mkf=mkf, tile=xb2[i]):
                return mkf(eng, tile)
            xtok[i] = P.step(q, fn, dma=True, n_dma=1,
                             waits=(xcons[i],) + tuple(extra),
                             sem_name="xk%d" % i)
            xstate["i"] += 1
            return i

        def ld_c(g):
            return [g.dma_start(out=idf[:, :], in_=cst["idf"]),
                    g.dma_start(out=ones_col[:, :], in_=cst["ones_col"]),
                    g.dma_start(out=ones_row[:, :], in_=cst["ones_row"])]
        t_ldc = P.step("g", ld_c, dma=True, n_dma=3, waits=(),
                       sem_name="ldc")

        from contextlib import ExitStack
        es = ExitStack()
        ogsb = es.enter_context(nc.sbuf_tensor("ogsb", [128, 16 * TPC], BF16))

        def ld_og(g):
            src = og_in.rearrange("(ck p) t -> p ck t", p=128)
            return [g.dma_start(out=ogsb[:, :].rearrange(
                "p (ck t) -> p ck t", ck=16), in_=src)]
        t_og = P.step("a", ld_og, dma=True, n_dma=1, waits=(), sem_name="og")

        def z_stats(v):
            return v.memset(strows[:, :], 0.0)
        pv = P.step("v", z_stats, waits=())

        tst = [None, None]
        for ck in range(16):
            wi_ = wload(lambda g, tile, ck=ck:
                        ldw_chunkcol(g, woT, tile, ck * 128))
            xi_ = xload(lambda g, tile, ck=ck:
                        [g.dma_start(out=tile[:, :],
                                     in_=xT[ck * 128:(ck + 1) * 128, :])])
            cp = ck % 2
            for th in range(2):
                def mm_o(t, ck=ck, th=th, wi_=wi_):
                    last = None
                    for hk in range(16):
                        last = t.matmul(
                            ps0[:, 0:TH],
                            wt3[wi_][:, hk * 128:(hk + 1) * 128],
                            ogsb[:, hk * TPC + th * TH:hk * TPC + (th + 1) * TH],
                            start=(hk == 0), stop=(hk == 15))
                    return last
                pv = P.step("t", mm_o, waits=(pv, wtok[wi_], t_og))
                if th == 1:
                    wcons[wi_] = pv

                def addx(v, ck=ck, th=th, xi_=xi_, cp=cp):
                    return v.tensor_tensor(t1p[cp][:, th * TH:(th + 1) * TH],
                                           ps0[:, 0:TH],
                                           xb2[xi_][:, th * TH:(th + 1) * TH],
                                           OP.add)
                pv = P.step("v", addx, waits=(pv, xtok[xi_], tst[cp]))
                if th == 1:
                    xcons[xi_] = pv

            def st_h1(s, ck=ck, cp=cp):
                return [s.dma_start(out=h1scr[ck * 128:(ck + 1) * 128, :],
                                    in_=t1p[cp][:, :])]
            tst[cp] = P.step("s", st_h1, dma=True, n_dma=1, waits=(pv,),
                             sem_name="sh%d" % cp)

            def sq_h1(a, cp=cp):
                return a.activation(out=tmp2[:, :], in_=t1p[cp][:, :],
                                    func=AF.Square)
            pv = P.step("a", sq_h1, waits=(pv,))

            for th in range(2):
                def mm_stats(t, th=th, cp=cp):
                    t.matmul(psS[0:1, 0:TH], ones_col[:, :],
                             t1p[cp][:, th * TH:(th + 1) * TH],
                             start=True, stop=True)
                    return t.matmul(psB[0:1, 0:TH], ones_col[:, :],
                                    tmp2[:, th * TH:(th + 1) * TH],
                                    start=True, stop=True)
                pv = P.step("t", mm_stats, waits=(pv, t_ldc))

                def acc_stats(v, th=th):
                    v.tensor_tensor(strows[:, th * TH:(th + 1) * TH],
                                    strows[:, th * TH:(th + 1) * TH],
                                    psS[0:1, 0:TH], OP.add)
                    v.drain()
                    return v.tensor_tensor(
                        strows[:, TPC + th * TH:TPC + (th + 1) * TH],
                        strows[:, TPC + th * TH:TPC + (th + 1) * TH],
                        psB[0:1, 0:TH], OP.add)
                pv = P.step("v", acc_stats, waits=(pv,))

        def ln_scalars(v):
            v.tensor_scalar(strows[:, 0:TPC], strows[:, 0:TPC],
                            1.0 / 2048.0, None, OP.mult)
            v.drain()
            v.tensor_scalar(strows[:, TPC:2 * TPC], strows[:, TPC:2 * TPC],
                            1.0 / 2048.0, None, OP.mult)
            v.drain()
            v.tensor_tensor(strows[:, 2 * TPC:3 * TPC], strows[:, 0:TPC],
                            strows[:, 0:TPC], OP.mult)
            v.drain()
            v.tensor_tensor(strows[:, TPC:2 * TPC], strows[:, TPC:2 * TPC],
                            strows[:, 2 * TPC:3 * TPC], OP.subtract)
            v.drain()
            return v.tensor_scalar(strows[:, TPC:2 * TPC],
                                   strows[:, TPC:2 * TPC], 1e-5, None, OP.add)

        def rstd_a(a):
            return a.activation(out=strows[:, 2 * TPC:3 * TPC],
                                in_=strows[:, TPC:2 * TPC], func=AF.Sqrt)

        def rstd_v(v):
            return v.reciprocal(strows[:, TPC:2 * TPC],
                                strows[:, 2 * TPC:3 * TPC])

        pv = P.step("v", ln_scalars, waits=(pv,))
        pv = P.step("a", rstd_a, waits=(pv,))
        pv = P.step("v", rstd_v, waits=(pv,))

        tsp = [None, None]

        def emit_ln_apply(pv, scr_in, dst_sb, dst_scr, store_sem, h2mode,
                          pre_toks=()):
            ai = [0]
            for th in range(2):
                def bc_mu(t, th=th):
                    t.matmul(psB[:, 0:TH], ones_row[:, :],
                             strows[:, th * TH:(th + 1) * TH],
                             start=True, stop=True)
                    return t.matmul(ps2[:, 0:TH], ones_row[:, :],
                                    strows[:, TPC + th * TH:TPC + (th + 1) * TH],
                                    start=True, stop=True)
                pv = P.step("t", bc_mu, waits=(pv,))

                def ev_bc(v, th=th):
                    v.tensor_copy(tmp2[:, 0:TH], psB[:, 0:TH])
                    v.drain()
                    return v.tensor_copy(tmp2[:, TH:2 * TH], ps2[:, 0:TH])
                pv = P.step("v", ev_bc, waits=(pv,))

                for ck in range(16):
                    if h2mode:
                        xi_ = None
                    else:
                        xi_ = xload(lambda g, tile, ck=ck, th=th:
                                    [g.dma_start(
                                        out=tile[:, 0:TH],
                                        in_=scr_in[ck * 128:(ck + 1) * 128,
                                                   th * TH:(th + 1) * TH])],
                                    extra=pre_toks)
                    ap = ai[0] % 2

                    def apply(v, ck=ck, th=th, xi_=xi_, ap=ap):
                        if xi_ is None:
                            sl = racc[:, ck * TPC + th * TH:
                                      ck * TPC + (th + 1) * TH]
                        else:
                            sl = xb2[xi_][:, 0:TH]
                        v.tensor_tensor(t1p[ap][:, 0:TH], sl, tmp2[:, 0:TH],
                                        OP.subtract)
                        v.drain()
                        last = v.tensor_tensor(t1p[ap][:, 0:TH],
                                               t1p[ap][:, 0:TH],
                                               tmp2[:, TH:2 * TH], OP.mult)
                        if dst_sb is not None:
                            v.drain()
                            last = v.tensor_copy(
                                dst_sb[:, ck * TPC + th * TH:
                                       ck * TPC + (th + 1) * TH],
                                t1p[ap][:, 0:TH])
                        return last
                    pv = P.step("v", apply,
                                waits=(pv, tsp[ap]) +
                                ((xtok[xi_],) if xi_ is not None else ()))
                    if xi_ is not None:
                        xcons[xi_] = pv

                    if dst_scr is not None:
                        def st_post(s, ck=ck, th=th, ap=ap):
                            return [s.dma_start(
                                out=dst_scr[ck * 128:(ck + 1) * 128,
                                            th * TH:(th + 1) * TH],
                                in_=t1p[ap][:, 0:TH])]
                        tsp[ap] = P.step("s", st_post, dma=True, n_dma=1,
                                         waits=(pv,),
                                         sem_name="%s%d" % (store_sem, ap))
                    ai[0] += 1
            return pv

        tsp[0], tsp[1] = tst[0], tst[1]
        pv = emit_ln_apply(pv, h1scr, h1b, h1scr, "sp", h2mode=False,
                           pre_toks=(tst[0], tst[1]))
        t_h1s_done = (tsp[0], tsp[1])
        es.close()
        actsh = sbuf(nc, "actsh", (128, 16 * TPC), BF16)
        racc = sbuf(nc, "racc", (128, 16 * TPC), F32)
        h1e = sbuf(nc, "h1e", (128, 8 * TPC), BF16)

        wg_i = wload(lambda g, tile: ldw_chunkcol(g, gwT, tile, 0, ncol=8))

        for th in range(2):
            def mm_gate(t, th=th, wg_i=wg_i):
                last = None
                for ck in range(16):
                    last = t.matmul(ps0[0:8, 0:TH],
                                    wt3[wg_i][:, ck * 8:(ck + 1) * 8],
                                    h1b[:, ck * TPC + th * TH:
                                        ck * TPC + (th + 1) * TH],
                                    start=(ck == 0), stop=(ck == 15))
                return last
            pv = P.step("t", mm_gate, waits=(pv, wtok[wg_i]))
            if th == 1:
                wcons[wg_i] = pv

            def ev_gate(v, th=th):
                return v.tensor_copy(crows[:, th * TH:(th + 1) * TH],
                                     ps0[0:8, 0:TH])
            pv = P.step("v", ev_gate, waits=(pv,))

        for ti in range(NT):
            def tr_sc(t, ti=ti):
                return t.transpose(ps1[0:128, 0:8],
                                   crows[:, ti * 128:(ti + 1) * 128],
                                   idf[0:8, 0:8])
            pv = P.step("t", tr_sc, waits=(pv,))

            def ev_sc(v, ti=ti):
                return v.tensor_copy(stok[:, ti * 8:(ti + 1) * 8],
                                     ps1[0:128, 0:8])
            pv = P.step("v", ev_sc, waits=(pv,))

        for ti in range(NT):
            def tk_a(v, ti=ti):
                sl = stok[:, ti * 8:(ti + 1) * 8]
                v.reduce_max(small[:, 0:1], sl, axis=AX.X)
                v.drain()
                v.tensor_scalar(small[:, 8:16], sl, small[:, 0:1], None,
                                OP.is_ge)
                v.drain()
                v.scalar_tensor_tensor(small[:, 16:24], small[:, 8:16], -1e30,
                                       sl, OP.mult, OP.add)
                v.drain()
                v.reduce_max(small[:, 1:2], small[:, 16:24], axis=AX.X)
                v.drain()
                v.tensor_scalar(small[:, 24:32], small[:, 16:24],
                                small[:, 1:2], None, OP.is_ge)
                v.drain()
                return v.tensor_tensor(small[:, 2:3], small[:, 1:2],
                                       small[:, 0:1], OP.subtract)
            pv = P.step("v", tk_a, waits=(pv,))

            def tk_exp(a):
                return a.activation(out=small[:, 3:4], in_=small[:, 2:3],
                                    func=AF.Exp)
            pv = P.step("a", tk_exp, waits=(pv,))

            def tk_w(v, ti=ti):
                v.tensor_scalar(small[:, 4:5], small[:, 3:4], 1.0, None,
                                OP.add)
                v.drain()
                v.reciprocal(small[:, 5:6], small[:, 4:5])
                v.drain()
                v.tensor_tensor(small[:, 6:7], small[:, 3:4], small[:, 5:6],
                                OP.mult)
                v.drain()
                v.tensor_scalar(small[:, 32:40], small[:, 8:16],
                                small[:, 5:6], None, OP.mult)
                v.drain()
                v.tensor_scalar(small[:, 40:48], small[:, 24:32],
                                small[:, 6:7], None, OP.mult)
                v.drain()
                return v.tensor_tensor(ctok[:, ti * 8:(ti + 1) * 8],
                                       small[:, 32:40], small[:, 40:48],
                                       OP.add)
            pv = P.step("v", tk_w, waits=(pv,))

            def tr_cb(t, ti=ti):
                return t.transpose(ps1[0:8, 0:128],
                                   ctok[:, ti * 8:(ti + 1) * 8], idf[:, :])
            pv = P.step("t", tr_cb, waits=(pv,))

            def ev_cb(v, ti=ti):
                return v.tensor_copy(crows[:, ti * 128:(ti + 1) * 128],
                                     ps1[0:8, 0:128])
            pv = P.step("v", ev_cb, waits=(pv,))

        def st_comb(s):
            return [s.dma_start(out=combscr, in_=crows[:, :])]
        t_comb = P.step("s", st_comb, dma=True, n_dma=1, waits=(pv,),
                        sem_name="cmb")

        for hs in range(16):
            wiA = wload(lambda g, tile, hs=hs:
                        ldw_chunkcol(g, shgT, tile, hs * 128))
            wiB = wload(lambda g, tile, hs=hs:
                        ldw_chunkcol(g, shuT, tile, hs * 128))

            for th in range(2):
                def mm_sh(t, th=th, wiA=wiA, wiB=wiB):
                    for ck in range(16):
                        t.matmul(ps0[:, 0:TH],
                                 wt3[wiA][:, ck * 128:(ck + 1) * 128],
                                 h1b[:, ck * TPC + th * TH:
                                     ck * TPC + (th + 1) * TH],
                                 start=(ck == 0), stop=(ck == 15))
                    last = None
                    for ck in range(16):
                        last = t.matmul(ps1[:, 0:TH],
                                        wt3[wiB][:, ck * 128:(ck + 1) * 128],
                                        h1b[:, ck * TPC + th * TH:
                                            ck * TPC + (th + 1) * TH],
                                        start=(ck == 0), stop=(ck == 15))
                    return last
                pv = P.step("t", mm_sh, waits=(pv, wtok[wiA], wtok[wiB]))
                if th == 1:
                    wcons[wiA] = pv
                    wcons[wiB] = pv

                def ev_sh(a, th=th):
                    return a.activation(out=tmp1[:, 0:TH], in_=ps0[:, 0:TH],
                                        func=AF.Silu)
                w_sh = (pv,) + (t_h1s_done if hs == 0 and th == 0 else ())
                pv = P.step("a", ev_sh, waits=w_sh)

                def mul_sh(v, hs=hs, th=th):
                    return v.tensor_tensor(
                        actsh[:, hs * TPC + th * TH:hs * TPC + (th + 1) * TH],
                        tmp1[:, 0:TH], ps1[:, 0:TH], OP.mult)
                pv = P.step("v", mul_sh, waits=(pv,))

        setile = sbuf(nc, "setile", (128, 16), BF16)

        def ld_se(g):
            return [g.dma_start(out=setile[:, :], in_=seP)]
        t_se = P.step("g", ld_se, dma=True, n_dma=1, waits=(), sem_name="se")

        for th in range(2):
            def mm_se(t, th=th):
                last = None
                for ck in range(16):
                    last = t.matmul(ps0[0:1, 0:TH], setile[:, ck:ck + 1],
                                    h1b[:, ck * TPC + th * TH:
                                        ck * TPC + (th + 1) * TH],
                                    start=(ck == 0), stop=(ck == 15))
                return last
            pv = P.step("t", mm_se, waits=(pv, t_se))

            def ev_se(a, th=th):
                return a.activation(out=srow[:, th * TH:(th + 1) * TH],
                                    in_=ps0[0:1, 0:TH], func=AF.Sigmoid)
            pv = P.step("a", ev_se, waits=(pv,))

        for th in range(2):
            def bc_se(t, th=th):
                return t.matmul(psB[:, 0:TH], ones_row[:, :],
                                srow[:, th * TH:(th + 1) * TH],
                                start=True, stop=True)
            pv = P.step("t", bc_se, waits=(pv,))

            def ev_bse(v, th=th):
                return v.tensor_copy(sgb[:, th * TH:(th + 1) * TH],
                                     psB[:, 0:TH])
            pv = P.step("v", ev_bse, waits=(pv,))
        t_srow_free = pv

        for ck in range(16):
            wi_ = wload(lambda g, tile, ck=ck:
                        ldw_chunkcol(g, shdT, tile, ck * 128))

            for th in range(2):
                def mm_shd(t, ck=ck, th=th, wi_=wi_):
                    last = None
                    for hs in range(16):
                        last = t.matmul(ps0[:, 0:TH],
                                        wt3[wi_][:, hs * 128:(hs + 1) * 128],
                                        actsh[:, hs * TPC + th * TH:
                                              hs * TPC + (th + 1) * TH],
                                        start=(hs == 0), stop=(hs == 15))
                    return last
                pv = P.step("t", mm_shd, waits=(pv, wtok[wi_]))
                if th == 1:
                    wcons[wi_] = pv

                def ev_shd(v, ck=ck, th=th):
                    return v.tensor_tensor(
                        racc[:, ck * TPC + th * TH:ck * TPC + (th + 1) * TH],
                        ps0[:, 0:TH], sgb[:, th * TH:(th + 1) * TH], OP.mult)
                pv = P.step("v", ev_shd, waits=(pv,))

        for e in range(8):
            def ld_cb(g, e=e):
                return [g.dma_start(out=srow[:, :], in_=combscr[e:e + 1, :])]
            t_cb = P.step("a", ld_cb, dma=True, n_dma=1,
                          waits=(t_comb, t_srow_free), sem_name="cb")

            for th in range(2):
                def bc_cb(t, th=th):
                    return t.matmul(psB[:, 0:TH], ones_row[:, :],
                                    srow[:, th * TH:(th + 1) * TH],
                                    start=True, stop=True)
                pv = P.step("t", bc_cb, waits=(pv, t_cb))

                def ev_cbc(v, th=th):
                    return v.tensor_copy(cbsb[:, th * TH:(th + 1) * TH],
                                         psB[:, 0:TH])
                pv = P.step("v", ev_cbc, waits=(pv,))
            t_srow_free = pv

            for hc in range(8):
                wiA = wload(lambda g, tile, e=e, hc=hc:
                            [g.dma_start(out=tile[:, :].rearrange(
                                "p (ck c) -> p ck c", ck=16),
                                in_=wge[e * 2048:(e + 1) * 2048,
                                        hc * 128:(hc + 1) * 128].rearrange(
                                    "(ck p) c -> p ck c", p=128))])
                wiB = wload(lambda g, tile, e=e, hc=hc:
                            [g.dma_start(out=tile[:, :].rearrange(
                                "p (ck c) -> p ck c", ck=16),
                                in_=wue[e * 2048:(e + 1) * 2048,
                                        hc * 128:(hc + 1) * 128].rearrange(
                                    "(ck p) c -> p ck c", p=128))])

                for th in range(2):
                    def mm_eg(t, th=th, wiA=wiA, wiB=wiB):
                        for ck in range(16):
                            t.matmul(ps0[:, 0:TH],
                                     wt3[wiA][:, ck * 128:(ck + 1) * 128],
                                     h1b[:, ck * TPC + th * TH:
                                         ck * TPC + (th + 1) * TH],
                                     start=(ck == 0), stop=(ck == 15))
                        last = None
                        for ck in range(16):
                            last = t.matmul(ps1[:, 0:TH],
                                            wt3[wiB][:, ck * 128:(ck + 1) * 128],
                                            h1b[:, ck * TPC + th * TH:
                                                ck * TPC + (th + 1) * TH],
                                            start=(ck == 0), stop=(ck == 15))
                        return last
                    pv = P.step("t", mm_eg, waits=(pv, wtok[wiA], wtok[wiB]))
                    if th == 1:
                        wcons[wiA] = pv
                        wcons[wiB] = pv

                    def ev_eg(a, th=th):
                        return a.activation(out=tmp1[:, 0:TH],
                                            in_=ps0[:, 0:TH], func=AF.Silu)
                    pv = P.step("a", ev_eg, waits=(pv,))

                    def mul_eg(v, hc=hc, th=th):
                        v.tensor_tensor(tmp2[:, 0:TH], tmp1[:, 0:TH],
                                        ps1[:, 0:TH], OP.mult)
                        v.drain()
                        return v.tensor_tensor(
                            h1e[:, hc * TPC + th * TH:hc * TPC + (th + 1) * TH],
                            tmp2[:, 0:TH], cbsb[:, th * TH:(th + 1) * TH],
                            OP.mult)
                    pv = P.step("v", mul_eg, waits=(pv,))

            for ck in range(16):
                wi_ = wload(lambda g, tile, e=e, ck=ck:
                            [g.dma_start(out=tile[:, 0:8 * 128].rearrange(
                                "p (hk c) -> p hk c", hk=8),
                                in_=wde[e * 1024:(e + 1) * 1024,
                                        ck * 128:(ck + 1) * 128].rearrange(
                                    "(hk p) c -> p hk c", p=128))])

                for th in range(2):
                    def mm_ed(t, ck=ck, th=th, wi_=wi_):
                        last = None
                        for hk in range(8):
                            last = t.matmul(
                                ps0[:, 0:TH],
                                wt3[wi_][:, hk * 128:(hk + 1) * 128],
                                h1e[:, hk * TPC + th * TH:
                                    hk * TPC + (th + 1) * TH],
                                start=(hk == 0), stop=(hk == 7))
                        return last
                    pv = P.step("t", mm_ed, waits=(pv, wtok[wi_]))
                    if th == 1:
                        wcons[wi_] = pv

                    def acc_ed(v, ck=ck, th=th):
                        return v.tensor_tensor(
                            racc[:, ck * TPC + th * TH:
                                 ck * TPC + (th + 1) * TH],
                            racc[:, ck * TPC + th * TH:
                                 ck * TPC + (th + 1) * TH],
                            ps0[:, 0:TH], OP.add)
                    pv = P.step("v", acc_ed, waits=(pv,))

        def z_stats2(v):
            return v.memset(strows[:, :], 0.0)
        pv = P.step("v", z_stats2, waits=(pv,))

        for ck in range(16):
            xi_ = xload(lambda g, tile, ck=ck:
                        [g.dma_start(out=tile[:, :],
                                     in_=h1scr[ck * 128:(ck + 1) * 128, :])],
                        extra=t_h1s_done)

            def add_h1(v, ck=ck, xi_=xi_):
                return v.tensor_tensor(racc[:, ck * TPC:(ck + 1) * TPC],
                                       racc[:, ck * TPC:(ck + 1) * TPC],
                                       xb2[xi_][:, :], OP.add)
            pv = P.step("v", add_h1, waits=(pv, xtok[xi_]))
            xcons[xi_] = pv

            def sq_h2(a, ck=ck):
                return a.activation(out=tmp2[:, :],
                                    in_=racc[:, ck * TPC:(ck + 1) * TPC],
                                    func=AF.Square)
            pv = P.step("a", sq_h2, waits=(pv,))

            for th in range(2):
                def mm_st2(t, ck=ck, th=th):
                    t.matmul(psS[0:1, 0:TH], ones_col[:, :],
                             racc[:, ck * TPC + th * TH:
                                  ck * TPC + (th + 1) * TH],
                             start=True, stop=True)
                    return t.matmul(psB[0:1, 0:TH], ones_col[:, :],
                                    tmp2[:, th * TH:(th + 1) * TH],
                                    start=True, stop=True)
                pv = P.step("t", mm_st2, waits=(pv,))

                def acc_st2(v, th=th):
                    v.tensor_tensor(strows[:, th * TH:(th + 1) * TH],
                                    strows[:, th * TH:(th + 1) * TH],
                                    psS[0:1, 0:TH], OP.add)
                    v.drain()
                    return v.tensor_tensor(
                        strows[:, TPC + th * TH:TPC + (th + 1) * TH],
                        strows[:, TPC + th * TH:TPC + (th + 1) * TH],
                        psB[0:1, 0:TH], OP.add)
                pv = P.step("v", acc_st2, waits=(pv,))

        pv = P.step("v", ln_scalars, waits=(pv,))
        pv = P.step("a", rstd_a, waits=(pv,))
        pv = P.step("v", rstd_v, waits=(pv,))
        tsp = [None, None]
        pv = emit_ln_apply(pv, None, None, h2T_o, "hs", h2mode=True)
        t_h2s_done = (tsp[0], tsp[1])

        h2bs = h1b

        def z_stats3(v):
            return v.memset(strows[:, :], 0.0)
        pv = P.step("v", z_stats3, waits=(pv,))

        def ld_h2b(g):
            return [g.dma_start(out=h2bs[:, :].rearrange(
                "p (ck t) -> p ck t", ck=16),
                in_=h2T_o.rearrange("(ck p) t -> p ck t", p=128))]
        t_h2b = P.step("g", ld_h2b, dma=True, n_dma=1,
                       waits=t_h2s_done + (pv,), sem_name="h2b")

        tckv = [tsp[0], tsp[1]]
        for lc in range(4):
            wi_ = wload(lambda g, tile, lc=lc:
                        ldw_chunkcol(g, kvdw, tile, lc * 128))
            cp = lc % 2

            for th in range(2):
                def mm_kv(t, lc=lc, th=th, wi_=wi_):
                    last = None
                    for ck in range(16):
                        last = t.matmul(ps0[:, 0:TH],
                                        wt3[wi_][:, ck * 128:(ck + 1) * 128],
                                        h2bs[:, ck * TPC + th * TH:
                                             ck * TPC + (th + 1) * TH],
                                        start=(ck == 0), stop=(ck == 15))
                    return last
                pv = P.step("t", mm_kv, waits=(pv, wtok[wi_], t_h2b))
                if th == 1:
                    wcons[wi_] = pv

                def ev_kv(v, th=th, cp=cp):
                    return v.tensor_copy(t1p[cp][:, th * TH:(th + 1) * TH],
                                         ps0[:, 0:TH])
                pv = P.step("v", ev_kv, waits=(pv, tckv[cp]))

            def st_ckvp(s, lc=lc, cp=cp):
                return [s.dma_start(out=ckvscr[lc * 128:(lc + 1) * 128, :],
                                    in_=t1p[cp][:, :])]
            tckv[cp] = P.step("s", st_ckvp, dma=True, n_dma=1, waits=(pv,),
                              sem_name="cs%d" % cp)

            def sq_kv(a, lc=lc, cp=cp):
                return a.activation(out=tmp2[:, :], in_=t1p[cp][:, :],
                                    func=AF.Square)
            pv = P.step("a", sq_kv, waits=(pv,))

            for th in range(2):
                def mm_stkv(t, th=th):
                    return t.matmul(psB[0:1, 0:TH], ones_col[:, :],
                                    tmp2[:, th * TH:(th + 1) * TH],
                                    start=True, stop=True)
                pv = P.step("t", mm_stkv, waits=(pv,))

                def acc_stkv(v, th=th):
                    return v.tensor_tensor(
                        strows[:, th * TH:(th + 1) * TH],
                        strows[:, th * TH:(th + 1) * TH],
                        psB[0:1, 0:TH], OP.add)
                pv = P.step("v", acc_stkv, waits=(pv,))

        def rms_kv(v):
            return v.tensor_scalar(strows[:, 0:TPC], strows[:, 0:TPC],
                                   1.0 / 512.0, 1e-6, OP.mult, OP.add)
        pv = P.step("v", rms_kv, waits=(pv,))

        def rms_sqrt(a):
            return a.activation(out=strows[:, TPC:2 * TPC],
                                in_=strows[:, 0:TPC], func=AF.Sqrt)
        pv = P.step("a", rms_sqrt, waits=(pv,))

        def rms_rec(v):
            return v.reciprocal(strows[:, 0:TPC], strows[:, TPC:2 * TPC])
        pv = P.step("v", rms_rec, waits=(pv,))

        for th in range(2):
            def bc_rms(t, th=th):
                return t.matmul(psB[:, 0:TH], ones_row[:, :],
                                strows[:, th * TH:(th + 1) * TH],
                                start=True, stop=True)
            pv = P.step("t", bc_rms, waits=(pv,))

            def ev_brms(v, th=th):
                return v.tensor_copy(tmp2[:, th * TH:(th + 1) * TH],
                                     psB[:, 0:TH])
            pv = P.step("v", ev_brms, waits=(pv,))

        for lc in range(4):
            xi_ = xload(lambda g, tile, lc=lc:
                        [g.dma_start(out=tile[:, :],
                                     in_=ckvscr[lc * 128:(lc + 1) * 128, :])],
                        extra=(tckv[0], tckv[1]))
            cp = lc % 2

            def ap_kv(v, lc=lc, xi_=xi_, cp=cp):
                return v.tensor_tensor(t1p[cp][:, :], xb2[xi_][:, :],
                                       tmp2[:, :], OP.mult)
            pv = P.step("v", ap_kv, waits=(pv, xtok[xi_], tckv[cp]))
            xcons[xi_] = pv

            def st_kv(s, lc=lc, cp=cp):
                return [s.dma_start(out=ckvT_o[lc * 128:(lc + 1) * 128, :],
                                    in_=t1p[cp][:, :])]
            tckv[cp] = P.step("s", st_kv, dma=True, n_dma=1, waits=(pv,),
                              sem_name="co%d" % cp)

        P.final_wait()
    return P


def prep_l0rest_inputs(inputs, og_full):
    consts = _consts_a2()
    woT = np.ascontiguousarray(inputs["L0_attn_Wo"].T, dtype=np.float32)
    gwT = np.ascontiguousarray(inputs["L0_ffn_gate_w"].T, dtype=np.float32)
    wge = np.ascontiguousarray(
        inputs["L0_ffn_w_gate"].transpose(0, 2, 1).reshape(8 * 2048, 1024),
        dtype=np.float32)
    wue = np.ascontiguousarray(
        inputs["L0_ffn_w_up"].transpose(0, 2, 1).reshape(8 * 2048, 1024),
        dtype=np.float32)
    wde = np.ascontiguousarray(
        inputs["L0_ffn_w_down"].transpose(0, 2, 1).reshape(8 * 1024, 2048),
        dtype=np.float32)
    shgT = np.ascontiguousarray(inputs["L0_ffn_sh_gate"].T, dtype=np.float32)
    shuT = np.ascontiguousarray(inputs["L0_ffn_sh_up"].T, dtype=np.float32)
    shdT = np.ascontiguousarray(inputs["L0_ffn_sh_down"].T, dtype=np.float32)
    seP = np.ascontiguousarray(
        inputs["L0_ffn_se_gate_w"].reshape(16, 128).T, dtype=np.float32)
    kvdw = np.ascontiguousarray(inputs["L1_attn_w_kv_down"], dtype=np.float32)
    x = np.asarray(inputs["L0_attn_x"], dtype=np.float32)
    per_core = []
    for c in range(NCORES):
        b, r = c // 2, c % 2
        d = {
            "ogf": np.ascontiguousarray(og_full[b][:, r * TPC:(r + 1) * TPC]),
            "xT": np.ascontiguousarray(x[b].T[:, r * TPC:(r + 1) * TPC]),
            "woT": woT, "gwT": gwT, "wge": wge, "wue": wue, "wde": wde,
            "shgT": shgT, "shuT": shuT, "shdT": shdT, "seP": seP,
            "kvdw": kvdw,
        }
        for k, v in consts.items():
            d["c_" + k] = v
        per_core.append(d)
    return per_core



def _consts_a3():
    c = _consts_a2()
    ROPE = 64
    inv_freq = 10000.0 ** (-np.arange(0, ROPE, 2, dtype=np.float32) / ROPE)
    t = np.arange(T, dtype=np.float32)
    fr = np.outer(t, inv_freq)
    c["cosT"] = np.ascontiguousarray(np.cos(fr).T)
    c["sinT"] = np.ascontiguousarray(np.sin(fr).T)
    KBW = min(512, T)
    for j in range(KBW // 128):
        m = np.zeros((128, KBW), np.float32)
        for i in range(128):
            m[i, 128 * j + i + 1:] = -1e30
        c[f"cmask{j}"] = m
    sel = np.zeros((2, 128), np.float32)
    sel[0, 0:64] = 1.0
    sel[1, 64:128] = 1.0
    c["sel2"] = sel
    g2 = np.zeros((128, 2), np.float32)
    g2[0:64, 0] = 1.0
    g2[64:128, 1] = 1.0
    c["g2"] = g2
    return c


def build_l1attn():
    P = Prog("l1attn")
    nc = P.nc
    PTW = min(1024, T)
    KBW = min(512, T)
    h2Tf = nc.dram_tensor("h2Tf", [2048, T], F32, kind="ExternalInput").ap()
    ckvTf = nc.dram_tensor("ckvTf", [512, T], F32, kind="ExternalInput").ap()
    wqh = nc.dram_tensor("wqh", [2048, 8 * 192], F32, kind="ExternalInput").ap()
    wkvh = nc.dram_tensor("wkvh", [512, 8 * 192], F32, kind="ExternalInput").ap()
    qgain = nc.dram_tensor("qgain", [1, 8], F32, kind="ExternalInput").ap()
    cst = {k: nc.dram_tensor("c_" + k, list(v.shape), F32,
                             kind="ExternalInput").ap()
           for k, v in _consts_a3().items()}
    og2 = nc.dram_tensor("og2", [1024, T], BF16, kind="ExternalOutput").ap()

    qT = sbuf(nc, "qT", (128, 8 * T), BF16)
    kfull = sbuf(nc, "kfull", (128, 8 * T), BF16)
    vtok = sbuf(nc, "vtok", (128, 8 * T), BF16)
    cosT = sbuf(nc, "cosT", (32, T), F32)
    sinT = sbuf(nc, "sinT", (32, T), F32)
    idf = sbuf(nc, "idf_sb", (128, 128), F32)
    idb = sbuf(nc, "idb_sb", (128, 128), BF16)
    ones_col = sbuf(nc, "ones_sb", (128, 1), F32)
    ones_row = sbuf(nc, "onesr_sb", (1, 128), F32)
    sel2 = sbuf(nc, "sel2_sb", (2, 128), F32)
    g2c = sbuf(nc, "g2_sb", (128, 2), F32)
    cmask = sbuf(nc, "cmask_sb", (128, (KBW // 128) * KBW), F32)
    gcol = sbuf(nc, "gcol", (128, 8), F32)
    trow = sbuf(nc, "trow", (2, T), F32)
    stat = sbuf(nc, "stat", (128, 16), F32)
    PTb = sbuf(nc, "PTb", (128, 4 * 128), BF16)
    Obuf = sbuf(nc, "Obuf", (128, 2 * 128), BF16)

    ps0 = psum(nc, "ps0")
    ps1 = psum(nc, "ps1")
    ps2 = psum(nc, "ps2")
    psO = psum(nc, "psO")
    psB = psum(nc, "psB")
    psN = psum(nc, "psN", BF16, cols=128)
    psN2 = psum(nc, "psN2", BF16, cols=128)

    NTT = T // PTW
    SPW = min(512, PTW)
    PTB = min(512, T)
    NTB = T // PTB

    from contextlib import ExitStack
    es1 = ExitStack()
    htile = es1.enter_context(nc.sbuf_tensor("htile", [128, 16 * PTW], BF16))
    ctile = es1.enter_context(nc.sbuf_tensor("ctile", [128, 4 * PTW], BF16))
    wth0 = es1.enter_context(nc.sbuf_tensor("wth0", [128, 16 * 192], BF16))
    wth1 = es1.enter_context(nc.sbuf_tensor("wth1", [128, 16 * 192], BF16))
    wth2 = es1.enter_context(nc.sbuf_tensor("wth2", [128, 16 * 192], BF16))
    wt3 = [wth0, wth1, wth2]

    with P.block():
        wstate = {"i": 0}
        wcons = [None, None, None]
        wtok = [None, None, None]

        def wload(mkf, n_dma=1):
            i = wstate["i"] % 3

            def fn(eng, mkf=mkf, tile=wt3[i]):
                return mkf(eng, tile)
            wtok[i] = P.step("g", fn, dma=True, n_dma=n_dma,
                             waits=(wcons[i],), sem_name="wt%d" % i)
            wstate["i"] += 1
            return i

        def ld_c(g):
            out = [g.dma_start(out=cosT[:, :], in_=cst["cosT"]),
                   g.dma_start(out=sinT[:, :], in_=cst["sinT"]),
                   g.dma_start(out=idf[:, :], in_=cst["idf"]),
                   g.dma_start(out=idb[:, :], in_=cst["idf"]),
                   g.dma_start(out=ones_col[:, :], in_=cst["ones_col"]),
                   g.dma_start(out=ones_row[:, :], in_=cst["ones_row"]),
                   g.dma_start(out=sel2[:, :], in_=cst["sel2"]),
                   g.dma_start(out=g2c[:, :], in_=cst["g2"]),
                   g.dma_start(out=gcol[0:1, 0:8], in_=qgain)]
            for j in range(KBW // 128):
                out.append(g.dma_start(out=cmask[:, j * KBW:(j + 1) * KBW],
                                       in_=cst[f"cmask{j}"]))
            return out
        t_ldc = P.step("g", ld_c, dma=True, n_dma=9 + KBW // 128, waits=(),
                       sem_name="ldc")
        pv = t_ldc

        t_ht = None
        t_ct = None
        for tt in range(NTT):
            def ld_h2(g, tt=tt):
                return [g.dma_start(out=htile[:, 0:16 * PTW].rearrange(
                    "p (ck t) -> p ck t", ck=16),
                    in_=h2Tf[:, tt * PTW:(tt + 1) * PTW].rearrange(
                        "(ck p) t -> p ck t", p=128))]
            t_ht = P.step("g", ld_h2, dma=True, n_dma=1,
                          waits=(t_ht, pv if tt > 0 else None),
                          sem_name="ht")

            for h in range(8):
                wi_ = wload(lambda g, tile, h=h: [g.dma_start(
                    out=tile[:, :].rearrange("p (ck c) -> p ck c", ck=16),
                    in_=wqh[:, h * 192:(h + 1) * 192].rearrange(
                        "(ck p) c -> p ck c", p=128))])

                for sp in range(PTW // SPW):
                    def mm_qh(t, tt=tt, h=h, wi_=wi_, sp=sp):
                        for ck in range(16):
                            t.matmul(ps0[:, 0:SPW],
                                     wt3[wi_][:, ck * 192:ck * 192 + 128],
                                     htile[:, ck * PTW + sp * SPW:
                                           ck * PTW + (sp + 1) * SPW],
                                     start=(ck == 0), stop=(ck == 15))
                        last = None
                        for ck in range(16):
                            last = t.matmul(
                                ps1[0:64, 0:SPW],
                                wt3[wi_][:, ck * 192 + 128:ck * 192 + 192],
                                htile[:, ck * PTW + sp * SPW:
                                      ck * PTW + (sp + 1) * SPW],
                                start=(ck == 0), stop=(ck == 15))
                        return last
                    pv = P.step("t", mm_qh, waits=(pv, wtok[wi_], t_ht))
                    if sp == PTW // SPW - 1:
                        wcons[wi_] = pv

                    def ev_q(v, tt=tt, h=h, sp=sp):
                        c0 = h * T + tt * PTW + sp * SPW
                        v.tensor_copy(qT[:, c0:c0 + SPW], ps0[:, 0:SPW])
                        v.drain()
                        return v.tensor_copy(kfull[64:128, c0:c0 + SPW],
                                             ps1[0:64, 0:SPW])
                    pv = P.step("v", ev_q, waits=(pv,))

        for tt in range(NTT):
            def ld_ckv2(g, tt=tt):
                return [g.dma_start(out=ctile[:, 0:4 * PTW].rearrange(
                    "p (ck t) -> p ck t", ck=4),
                    in_=ckvTf[:, tt * PTW:(tt + 1) * PTW].rearrange(
                        "(ck p) t -> p ck t", p=128))]
            t_ct = P.step("g", ld_ckv2, dma=True, n_dma=1,
                          waits=(t_ct, pv if tt > 0 else None),
                          sem_name="ct")

            for h in range(8):
                wi_ = wload(lambda g, tile, h=h: [g.dma_start(
                    out=tile[:, 0:4 * 192].rearrange(
                        "p (ck c) -> p ck c", ck=4),
                    in_=wkvh[:, h * 192:(h + 1) * 192].rearrange(
                        "(ck p) c -> p ck c", p=128))])

                for sp in range(PTW // SPW):
                    def mm_kn(t, tt=tt, h=h, wi_=wi_, sp=sp):
                        last = None
                        for ck in range(4):
                            last = t.matmul(
                                ps0[0:64, 0:SPW],
                                wt3[wi_][:, ck * 192:ck * 192 + 64],
                                ctile[:, ck * PTW + sp * SPW:
                                      ck * PTW + (sp + 1) * SPW],
                                start=(ck == 0), stop=(ck == 3))
                        return last
                    pv = P.step("t", mm_kn, waits=(pv, wtok[wi_], t_ct))

                    def ev_kn(v, tt=tt, h=h, sp=sp):
                        c0 = h * T + tt * PTW + sp * SPW
                        return v.tensor_copy(kfull[0:64, c0:c0 + SPW],
                                             ps0[0:64, 0:SPW])
                    pv = P.step("v", ev_kn, waits=(pv,))

                for st4 in range(PTW // 128):
                    def mm_v(t, tt=tt, h=h, st4=st4, wi_=wi_):
                        last = None
                        for ck in range(4):
                            last = t.matmul(
                                ps1[:, 0:128],
                                ctile[:, ck * PTW + st4 * 128:
                                      ck * PTW + st4 * 128 + 128],
                                wt3[wi_][:, ck * 192 + 64:ck * 192 + 192],
                                start=(ck == 0), stop=(ck == 3))
                        return last
                    pv = P.step("t", mm_v, waits=(pv,))
                    if st4 == PTW // 128 - 1:
                        wcons[wi_] = pv

                    def ev_v(v, tt=tt, h=h, st4=st4):
                        ti = tt * (PTW // 128) + st4
                        return v.tensor_copy(
                            vtok[:, h * T + ti * 128:h * T + (ti + 1) * 128],
                            ps1[:, 0:128])
                    pv = P.step("v", ev_v, waits=(pv,))

        es1.close()
        es2 = ExitStack()
        Sstr = es2.enter_context(nc.sbuf_tensor("Sstr", [128, T], F32))
        Sstr2 = es2.enter_context(nc.sbuf_tensor("Sstr2", [128, T], F32))
        Pbf = es2.enter_context(nc.sbuf_tensor("Pbf", [128, T], BF16))
        Pbf2 = es2.enter_context(nc.sbuf_tensor("Pbf2", [128, T], BF16))
        tmp64 = es2.enter_context(nc.sbuf_tensor("tmp64", [64, T], F32))
        tmp64b = es2.enter_context(nc.sbuf_tensor("tmp64b", [64, T], F32))
        t32a = es2.enter_context(nc.sbuf_tensor("t32a", [32, T], F32))
        t32b = es2.enter_context(nc.sbuf_tensor("t32b", [32, T], F32))
        S2 = [Sstr, Sstr2]
        P2 = [Pbf, Pbf2]

        for h in range(8):
            hs = lambda tens, h=h: tens[:, h * T:(h + 1) * T]

            def sq_q(a, h=h):
                return a.activation(out=Sstr[:, :], in_=hs(qT), func=AF.Square)
            pv = P.step("a", sq_q, waits=(pv,))

            for tb in range(NTB):
                def st_q2(t, tb=tb):
                    return t.matmul(ps0[0:2, 0:PTB], g2c[:, :],
                                    Sstr[:, tb * PTB:(tb + 1) * PTB],
                                    start=True, stop=True)
                pv = P.step("t", st_q2, waits=(pv,))

                def acc_q(v, tb=tb):
                    return v.tensor_copy(trow[:, tb * PTB:(tb + 1) * PTB],
                                         ps0[0:2, 0:PTB])
                pv = P.step("v", acc_q, waits=(pv,))

            def rstd_q(v):
                return v.tensor_scalar(trow[:, :], trow[:, :], 1.0 / 64.0,
                                       1e-6, OP.mult, OP.add)
            pv = P.step("v", rstd_q, waits=(pv,))

            def rsq_q(a):
                return a.activation(out=Sstr[0:2, 0:T], in_=trow[:, :],
                                    func=AF.Sqrt)
            pv = P.step("a", rsq_q, waits=(pv,))

            def rrec_q(v):
                return v.reciprocal(trow[:, :], Sstr[0:2, 0:T])
            pv = P.step("v", rrec_q, waits=(pv,))

            for tb in range(NTB):
                def bc_q(t, tb=tb):
                    return t.matmul(psB[:, 0:PTB], sel2[:, :],
                                    trow[:, tb * PTB:(tb + 1) * PTB],
                                    start=True, stop=True)
                pv = P.step("t", bc_q, waits=(pv,))

                def ap_q(v, h=h, tb=tb):
                    return v.tensor_tensor(
                        hs(qT)[:, tb * PTB:(tb + 1) * PTB],
                        hs(qT)[:, tb * PTB:(tb + 1) * PTB],
                        psB[:, 0:PTB], OP.mult)
                pv = P.step("v", ap_q, waits=(pv,))

            def cp_qr(v, h=h):
                return v.tensor_copy(tmp64[:, :], hs(qT)[64:128, :])
            pv = P.step("v", cp_qr, waits=(pv,))

            def rope_q(v, h=h):
                v.tensor_copy(t32a[:, :], tmp64[32:64, :])
                v.drain()
                v.tensor_tensor(tmp64b[0:32, :], tmp64[0:32, :], cosT[:, :],
                                OP.mult)
                v.drain()
                v.tensor_tensor(t32b[:, :], t32a[:, :], sinT[:, :], OP.mult)
                v.drain()
                v.tensor_tensor(tmp64b[0:32, :], tmp64b[0:32, :], t32b[:, :],
                                OP.add)
                v.drain()
                v.tensor_tensor(t32b[:, :], t32a[:, :], cosT[:, :], OP.mult)
                v.drain()
                v.tensor_tensor(Sstr[0:32, 0:T], tmp64[0:32, :], sinT[:, :],
                                OP.mult)
                v.drain()
                v.tensor_tensor(t32b[:, :], t32b[:, :], Sstr[0:32, 0:T],
                                OP.subtract)
                v.drain()
                return v.tensor_copy(tmp64b[32:64, :], t32b[:, :])
            pv = P.step("v", rope_q, waits=(pv,))

            def gain_bc(t, h=h):
                return t.matmul(ps2[:, 0:1], ones_row[:, :],
                                gcol[0:1, h:h + 1], start=True, stop=True)
            pv = P.step("t", gain_bc, waits=(pv,))

            def ev_gain(v):
                return v.tensor_copy(stat[:, 0:1], ps2[:, 0:1])
            pv = P.step("v", ev_gain, waits=(pv,))

            def wb_qr(v, h=h):
                v.tensor_scalar(tmp64b[:, :], tmp64b[:, :], stat[0:64, 0:1],
                                None, OP.mult)
                v.drain()
                return v.tensor_copy(hs(qT)[64:128, :], tmp64b[:, :])
            pv = P.step("v", wb_qr, waits=(pv,))

            def gain_nope(v, h=h):
                return v.tensor_scalar(hs(qT)[0:64, :], hs(qT)[0:64, :],
                                       stat[0:64, 0:1], None, OP.mult)
            pv = P.step("v", gain_nope, waits=(pv,))

            def sq_k(a, h=h):
                return a.activation(out=Sstr[:, :], in_=hs(kfull),
                                    func=AF.Square)
            pv = P.step("a", sq_k, waits=(pv,))
            for tb in range(NTB):
                def st_k2(t, tb=tb):
                    return t.matmul(ps0[0:2, 0:PTB], g2c[:, :],
                                    Sstr[:, tb * PTB:(tb + 1) * PTB],
                                    start=True, stop=True)
                pv = P.step("t", st_k2, waits=(pv,))

                def acc_k(v, tb=tb):
                    return v.tensor_copy(trow[:, tb * PTB:(tb + 1) * PTB],
                                         ps0[0:2, 0:PTB])
                pv = P.step("v", acc_k, waits=(pv,))
            pv = P.step("v", rstd_q, waits=(pv,))
            pv = P.step("a", rsq_q, waits=(pv,))
            pv = P.step("v", rrec_q, waits=(pv,))
            for tb in range(NTB):
                def bc_k(t, tb=tb):
                    return t.matmul(psB[:, 0:PTB], sel2[:, :],
                                    trow[:, tb * PTB:(tb + 1) * PTB],
                                    start=True, stop=True)
                pv = P.step("t", bc_k, waits=(pv,))

                def ap_k(v, h=h, tb=tb):
                    return v.tensor_tensor(
                        hs(kfull)[:, tb * PTB:(tb + 1) * PTB],
                        hs(kfull)[:, tb * PTB:(tb + 1) * PTB],
                        psB[:, 0:PTB], OP.mult)
                pv = P.step("v", ap_k, waits=(pv,))

            def cp_kr(v, h=h):
                return v.tensor_copy(tmp64[:, :], hs(kfull)[64:128, :])
            pv = P.step("v", cp_kr, waits=(pv,))
            pv = P.step("v", rope_q, waits=(pv,))

            def wb_kr(v, h=h):
                return v.tensor_copy(hs(kfull)[64:128, :], tmp64b[:, :])
            pv = P.step("v", wb_kr, waits=(pv,))

        t_norms = pv

        SC = 1.0 / float(np.sqrt(128.0))
        NQ = T // 128
        psS = [ps0, ps2]
        psPT = [psN, psN2]
        psOv = [psO, psB]
        jobs = [(h, qt) for h in range(8) for qt in range(NQ)]
        NJ = len(jobs)
        t_evs_last = [t_norms, t_norms]
        t_sexp = [t_norms, t_norms]
        t_trp_last = [t_norms, t_norms]
        t_ssum = [None, None]
        t_evp = {}
        t_evtr = {}
        t_mmpv = {}
        t_evo = [t_norms, t_norms]
        t_sto = [None, None]
        t_smax = [None, None]
        svals = {}

        def emit_S_mm(j):
            h, qt = jobs[j]
            j2 = j % 2
            kend = (qt + 1) * 128
            nkb = (kend + KBW - 1) // KBW
            t_ev = None
            for kb in range(nkb):
                def mm_s(t, h=h, qt=qt, kb=kb, j2=j2):
                    return t.matmul(
                        psS[j2][:, 0:KBW],
                        qT[:, h * T + qt * 128:h * T + (qt + 1) * 128],
                        kfull[:, h * T + kb * KBW:h * T + kb * KBW + KBW],
                        start=True, stop=True)
                w_ = (t_norms, t_ev if t_ev else t_evs_last[j2],
                      t_sexp[j2])
                tok_mm = P.step("t", mm_s, waits=w_)

                last_blk = (kb == nkb - 1)

                def ev_s(v, qt=qt, kb=kb, j2=j2, last_blk=last_blk):
                    if last_blk:
                        jj = qt % (KBW // 128)
                        return v.scalar_tensor_tensor(
                            S2[j2][:, kb * KBW:kb * KBW + KBW],
                            psS[j2][:, 0:KBW], SC,
                            cmask[:, jj * KBW:(jj + 1) * KBW],
                            OP.mult, OP.add)
                    return v.tensor_scalar(
                        S2[j2][:, kb * KBW:kb * KBW + KBW],
                        psS[j2][:, 0:KBW], SC, None, OP.mult)
                t_ev = P.step("v", ev_s, waits=(tok_mm, t_sexp[j2]))
            t_evs_last[j2] = t_ev

        def emit_S_soft(j):
            h, qt = jobs[j]
            j2 = j % 2
            kend = (qt + 1) * 128
            nkb = (kend + KBW - 1) // KBW
            t_ev = t_evs_last[j2]
            kw_all = nkb * KBW

            def smax(v, kw_all=kw_all, j2=j2):
                v.reduce_max(stat[:, j2 * 8:j2 * 8 + 1], S2[j2][:, 0:kw_all],
                             axis=AX.X)
                v.drain()
                return v.tensor_scalar(stat[:, j2 * 8 + 1:j2 * 8 + 2],
                                       stat[:, j2 * 8:j2 * 8 + 1], -1.0,
                                       None, OP.mult)
            t_smax[j2] = P.step("v", smax, waits=(t_ev,))

            def sexp(a, kw_all=kw_all, j2=j2):
                return a.activation(out=P2[j2][:, 0:kw_all],
                                    in_=S2[j2][:, 0:kw_all], func=AF.Exp,
                                    bias=stat[:, j2 * 8 + 1:j2 * 8 + 2])
            t_sexp[j2] = P.step("a", sexp,
                                waits=(t_smax[j2], t_trp_last[j2]))

            def ssum(v, kw_all=kw_all, j2=j2):
                v.reduce_sum(stat[:, j2 * 8 + 2:j2 * 8 + 3],
                             P2[j2][:, 0:kw_all], axis=AX.X)
                v.drain()
                v.reciprocal(stat[:, j2 * 8 + 3:j2 * 8 + 4],
                             stat[:, j2 * 8 + 2:j2 * 8 + 3])
                v.drain()
                return v.tensor_scalar(P2[j2][:, 0:kw_all],
                                       P2[j2][:, 0:kw_all],
                                       stat[:, j2 * 8 + 3:j2 * 8 + 4],
                                       None, OP.mult)
            t_ssum[j2] = P.step("v", ssum, waits=(t_sexp[j2],))

        def emit_PV(j):
            h, qt = jobs[j]
            j2 = j % 2

            def mk_tr(kb):
                kp = kb % 2

                def tr_p(t, kb=kb, kp=kp):
                    return t.transpose(psPT[kp][:, 0:128],
                                       P2[j2][:, kb * 128:(kb + 1) * 128],
                                       idb[:, :])
                return P.step("t", tr_p, waits=(t_ssum[j2], t_evtr.get(kp)))

            t_trs = {0: mk_tr(0)}
            for kb in range(qt + 1):
                kp = kb % 2
                if kb + 1 <= qt:
                    t_trs[kb + 1] = mk_tr(kb + 1)

                def ev_p(v, kb=kb, j2=j2, kp=kp):
                    return v.tensor_copy(PTb[:, (j2 * 2 + kp) * 128:
                                              (j2 * 2 + kp + 1) * 128],
                                         psPT[kp][:, 0:128])
                t_evtr[kp] = t_evp[(j2, kp)] = P.step(
                    "v", ev_p, waits=(t_trs[kb], t_mmpv.get((j2, kp))))

                def mm_pv(t, h=h, qt=qt, kb=kb, j2=j2, kp=kp):
                    return t.matmul(psOv[j2][:, 0:128],
                                    vtok[:, h * T + kb * 128:
                                         h * T + (kb + 1) * 128],
                                    PTb[:, (j2 * 2 + kp) * 128:
                                        (j2 * 2 + kp + 1) * 128],
                                    start=(kb == 0), stop=(kb == qt))
                t_mmpv[(j2, kp)] = P.step(
                    "t", mm_pv,
                    waits=(t_evp[(j2, kp)],) +
                    ((t_evo[j2],) if kb == 0 else ()))
            t_trp_last[j2] = t_trs[qt]

            def ev_o(v, j2=j2):
                return v.tensor_copy(Obuf[:, j2 * 128:(j2 + 1) * 128],
                                     psOv[j2][:, 0:128])
            t_evo[j2] = P.step("v", ev_o,
                               waits=(t_mmpv[(j2, qt % 2)], t_sto[j2]))

            def st_o(s, h=h, qt=qt, j2=j2):
                return [s.dma_start(
                    out=og2[h * 128:(h + 1) * 128,
                            qt * 128:(qt + 1) * 128],
                    in_=Obuf[:, j2 * 128:(j2 + 1) * 128])]
            t_sto[j2] = P.step("s", st_o, dma=True, n_dma=1,
                               waits=(t_evo[j2],), sem_name="so%d" % j2)

        emit_S_mm(0)
        emit_S_soft(0)
        for j in range(1, NJ):
            emit_S_mm(j)
            emit_PV(j - 1)
            emit_S_soft(j)
        emit_PV(NJ - 1)

        P.final_wait()
    return P


def prep_l1attn_inputs(inputs, h2T_full, ckvT_full):
    consts = _consts_a3()
    wqk = np.asarray(inputs["L1_attn_w_q_krope"], dtype=np.float32)
    wkvu = np.asarray(inputs["L1_attn_w_kv_up"], dtype=np.float32)
    qg = np.asarray(inputs["L1_attn_q_gain"], dtype=np.float32)
    per_core = []
    for c in range(NCORES):
        b, r = c // 2, c % 2
        hlo = 8 * r
        wq_cols = []
        wkv_cols = []
        for h in range(8):
            hg = hlo + h
            wq_cols.append(wqk[:, hg * 128:(hg + 1) * 128])
            wq_cols.append(wqk[:, 2048 + hg * 64:2048 + (hg + 1) * 64])
            wkv_cols.append(wkvu[:, hg * 64:(hg + 1) * 64])
            wkv_cols.append(wkvu[:, 1024 + hg * 128:1024 + (hg + 1) * 128])
        d = {
            "h2Tf": h2T_full[b],
            "ckvTf": ckvT_full[b],
            "wqh": np.ascontiguousarray(np.concatenate(wq_cols, axis=1)),
            "wkvh": np.ascontiguousarray(np.concatenate(wkv_cols, axis=1)),
            "qgain": np.ascontiguousarray(qg[hlo:hlo + 8].reshape(1, 8)),
        }
        for k, v in consts.items():
            d["c_" + k] = v
        per_core.append(d)
    return per_core



def _consts_a4():
    c = _consts_a2()
    c["iota64"] = np.tile(np.arange(64, dtype=np.float32), (128, 1))
    c["dumpb"] = np.tile(8192.0 + np.arange(6, dtype=np.float32), (128, 1))
    c["iota1024"] = np.tile(np.arange(1024, dtype=np.float32), (128, 1))
    c["iota8"] = np.tile(np.arange(8, dtype=np.float32), (128, 1))
    c["riota64"] = np.tile(np.arange(63, -1, -1, dtype=np.float32), (128, 1))
    triu = np.triu(np.ones((128, 128), np.float32), 1)
    c["triu128"] = triu
    return c


def build_l1mid():
    P = Prog("l1mid")
    nc = P.nc
    TH = TPC // 2
    og2s = nc.dram_tensor("og2s", [2048, TPC], BF16, kind="ExternalInput").ap()
    h2Ts = nc.dram_tensor("h2Ts", [2048, TPC], F32, kind="ExternalInput").ap()
    wproj = nc.dram_tensor("wproj", [2048, 2048], F32, kind="ExternalInput").ap()
    wgate = nc.dram_tensor("wgate", [2048, 64], F32, kind="ExternalInput").ap()
    wshu = nc.dram_tensor("wshu", [2048, 1024], F32, kind="ExternalInput").ap()
    wshd = nc.dram_tensor("wshd", [512, 2048], F32, kind="ExternalInput").ap()
    cst = {k: nc.dram_tensor("c_" + k, list(v.shape), F32,
                             kind="ExternalInput").ap()
           for k, v in _consts_a4().items()}
    h3T_o = nc.dram_tensor("h3T", [2048, TPC], F32, kind="ExternalOutput").ap()
    h3row_o = nc.dram_tensor("h3row", [TPC, 2048], BF16,
                             kind="ExternalOutput").ap()
    shT_o = nc.dram_tensor("shT", [2048, TPC], F32, kind="ExternalOutput").ap()
    e6_o = nc.dram_tensor("e6", [TPC, 6], F32, kind="ExternalOutput").ap()
    w6_o = nc.dram_tensor("w6", [TPC, 6], F32, kind="ExternalOutput").ap()
    p6_o = nc.dram_tensor("p6", [TPC, 6], F32, kind="ExternalOutput").ap()
    cnt_o = nc.dram_tensor("cnt", [1, 64], F32, kind="ExternalOutput").ap()
    h3scr = nc.dram_tensor("h3scr", [2048, TPC], F32).ap()

    h3b = sbuf(nc, "h3b", (128, 16 * TPC), BF16)
    idf = sbuf(nc, "idf_sb", (128, 128), F32)
    ones_col = sbuf(nc, "ones_sb", (128, 1), F32)
    ones_row = sbuf(nc, "onesr_sb", (1, 128), F32)
    iota64 = sbuf(nc, "iota64_sb", (128, 64), F32)
    riota64 = sbuf(nc, "riota64_sb", (128, 64), F32)
    triu128 = sbuf(nc, "triu_sb", (128, 128), F32)
    wtile = sbuf(nc, "wtile", (128, 16 * 128), BF16)
    xck = sbuf(nc, "xck", (128, TPC), F32)
    tmp1 = sbuf(nc, "tmp1", (128, TPC), F32)
    tmp2 = sbuf(nc, "tmp2", (128, TPC), F32)
    strows = sbuf(nc, "strows", (1, 3 * TPC), F32)
    actsu = sbuf(nc, "actsu", (128, 4 * TPC), BF16)
    rowt = sbuf(nc, "rowt", (128, 16 * 128), BF16)
    s64 = sbuf(nc, "s64", (64, TPC), F32)
    stok = sbuf(nc, "stok", (128, 64), F32)
    oh6 = sbuf(nc, "oh6", (128, 6 * 64), F32)
    tokcnt = sbuf(nc, "tokcnt", (128, 64), F32)
    kpre = sbuf(nc, "kpre", (128, 64), F32)
    ptile = sbuf(nc, "ptile", (128, 64), F32)
    e6t = sbuf(nc, "e6t", (128, 6), F32)
    w6t = sbuf(nc, "w6t", (128, 6), F32)
    p6t = sbuf(nc, "p6t", (128, 6), F32)
    offrow = sbuf(nc, "offrow", (1, 64), F32)
    small = sbuf(nc, "small", (128, 16), F32)

    ps0 = psum(nc, "ps0")
    ps1 = psum(nc, "ps1")
    psS = psum(nc, "psS")
    psB = psum(nc, "psB")
    ps2 = psum(nc, "ps2")
    psPT = nc.alloc_psum_tensor("psPT", [128, 1024], BF16)
    idb = sbuf(nc, "idb_sb", (128, 128), BF16)

    NT = TPC // 128

    def ldw_chunkcol(g, srcap, dst, col0, ncol=128, nck=16):
        srcsl = srcap[:, col0:col0 + ncol].rearrange("(ck p) c -> p ck c",
                                                     p=128)
        return [g.dma_start(out=dst[:, 0:nck * ncol].rearrange(
            "p (ck c) -> p ck c", ck=nck), in_=srcsl)]

    with P.block():
        def ld_c(g):
            return [g.dma_start(out=idf[:, :], in_=cst["idf"]),
                    g.dma_start(out=idb[:, :], in_=cst["idf"]),
                    g.dma_start(out=ones_col[:, :], in_=cst["ones_col"]),
                    g.dma_start(out=ones_row[:, :], in_=cst["ones_row"]),
                    g.dma_start(out=iota64[:, :], in_=cst["iota64"]),
                    g.dma_start(out=riota64[:, :], in_=cst["riota64"]),
                    g.dma_start(out=triu128[:, :], in_=cst["triu128"])]
        P.step("g", ld_c, dma=True, n_dma=7)

        from contextlib import ExitStack
        esm = ExitStack()
        og2sb = esm.enter_context(nc.sbuf_tensor("og2sb", [128, 16 * TPC], BF16))

        def ld_og2(g):
            return [g.dma_start(out=og2sb[:, :].rearrange(
                "p (ck t) -> p ck t", ck=16),
                in_=og2s.rearrange("(ck p) t -> p ck t", p=128))]
        P.step("g", ld_og2, dma=True, n_dma=1)

        def z_stats(v):
            return v.memset(strows[:, :], 0.0)
        P.step("v", z_stats)

        for ck in range(16):
            def ld_wp(g, ck=ck):
                return ldw_chunkcol(g, wproj, wtile, ck * 128)
            P.step("g", ld_wp, dma=True, n_dma=1)

            def ld_h2(g, ck=ck):
                return [g.dma_start(out=xck[:, :],
                                    in_=h2Ts[ck * 128:(ck + 1) * 128, :])]
            P.step("g", ld_h2, dma=True, n_dma=1)

            for th in range(2):
                def mm_o(t, ck=ck, th=th):
                    last = None
                    for hk in range(16):
                        last = t.matmul(
                            ps0[:, 0:TH], wtile[:, hk * 128:(hk + 1) * 128],
                            og2sb[:, hk * TPC + th * TH:
                                  hk * TPC + (th + 1) * TH],
                            start=(hk == 0), stop=(hk == 15))
                    return last
                P.step("t", mm_o)

                def addx(v, th=th):
                    return v.tensor_tensor(tmp1[:, th * TH:(th + 1) * TH],
                                           ps0[:, 0:TH],
                                           xck[:, th * TH:(th + 1) * TH],
                                           OP.add)
                P.step("v", addx)

            def st_h3p(s, ck=ck):
                return [s.dma_start(out=h3scr[ck * 128:(ck + 1) * 128, :],
                                    in_=tmp1[:, :])]
            P.step("s", st_h3p, dma=True, n_dma=1)

            def sq_h(a):
                return a.activation(out=tmp2[:, :], in_=tmp1[:, :],
                                    func=AF.Square)
            P.step("a", sq_h)

            for th in range(2):
                def mm_stats(t, th=th):
                    t.matmul(psS[0:1, 0:TH], ones_col[:, :],
                             tmp1[:, th * TH:(th + 1) * TH],
                             start=True, stop=True)
                    return t.matmul(psB[0:1, 0:TH], ones_col[:, :],
                                    tmp2[:, th * TH:(th + 1) * TH],
                                    start=True, stop=True)
                P.step("t", mm_stats)

                def acc_stats(v, th=th):
                    v.tensor_tensor(strows[:, th * TH:(th + 1) * TH],
                                    strows[:, th * TH:(th + 1) * TH],
                                    psS[0:1, 0:TH], OP.add)
                    v.drain()
                    return v.tensor_tensor(
                        strows[:, TPC + th * TH:TPC + (th + 1) * TH],
                        strows[:, TPC + th * TH:TPC + (th + 1) * TH],
                        psB[0:1, 0:TH], OP.add)
                P.step("v", acc_stats)

        def ln_scalars(v):
            v.tensor_scalar(strows[:, 0:TPC], strows[:, 0:TPC],
                            1.0 / 2048.0, None, OP.mult)
            v.drain()
            v.tensor_scalar(strows[:, TPC:2 * TPC], strows[:, TPC:2 * TPC],
                            1.0 / 2048.0, None, OP.mult)
            v.drain()
            v.tensor_tensor(strows[:, 2 * TPC:3 * TPC], strows[:, 0:TPC],
                            strows[:, 0:TPC], OP.mult)
            v.drain()
            v.tensor_tensor(strows[:, TPC:2 * TPC], strows[:, TPC:2 * TPC],
                            strows[:, 2 * TPC:3 * TPC], OP.subtract)
            v.drain()
            return v.tensor_scalar(strows[:, TPC:2 * TPC],
                                   strows[:, TPC:2 * TPC], 1e-5, None, OP.add)
        P.step("v", ln_scalars)

        def rstd_a(a):
            return a.activation(out=strows[:, 2 * TPC:3 * TPC],
                                in_=strows[:, TPC:2 * TPC], func=AF.Sqrt)
        P.step("a", rstd_a)

        def rstd_v(v):
            return v.reciprocal(strows[:, TPC:2 * TPC],
                                strows[:, 2 * TPC:3 * TPC])
        P.step("v", rstd_v)

        for th in range(2):
            def bc_mu(t, th=th):
                t.matmul(psB[:, 0:TH], ones_row[:, :],
                         strows[:, th * TH:(th + 1) * TH],
                         start=True, stop=True)
                return t.matmul(ps2[:, 0:TH], ones_row[:, :],
                                strows[:, TPC + th * TH:TPC + (th + 1) * TH],
                                start=True, stop=True)
            P.step("t", bc_mu)

            def ev_bc(v, th=th):
                v.tensor_copy(tmp2[:, 0:TH], psB[:, 0:TH])
                v.drain()
                return v.tensor_copy(tmp2[:, TH:2 * TH], ps2[:, 0:TH])
            P.step("v", ev_bc)

            for ck in range(16):
                def ld_pre(g, ck=ck, th=th):
                    return [g.dma_start(
                        out=xck[:, 0:TH],
                        in_=h3scr[ck * 128:(ck + 1) * 128,
                                  th * TH:(th + 1) * TH])]
                P.step("g", ld_pre, dma=True, n_dma=1)

                def apply(v, ck=ck, th=th):
                    v.tensor_tensor(tmp1[:, 0:TH], xck[:, 0:TH],
                                    tmp2[:, 0:TH], OP.subtract)
                    v.drain()
                    v.tensor_tensor(tmp1[:, 0:TH], tmp1[:, 0:TH],
                                    tmp2[:, TH:2 * TH], OP.mult)
                    v.drain()
                    return v.tensor_copy(
                        h3b[:, ck * TPC + th * TH:ck * TPC + (th + 1) * TH],
                        tmp1[:, 0:TH])
                P.step("v", apply)

                def st_h3(s, ck=ck, th=th):
                    return [s.dma_start(
                        out=h3T_o[ck * 128:(ck + 1) * 128,
                                  th * TH:(th + 1) * TH], in_=tmp1[:, 0:TH])]
                P.step("s", st_h3, dma=True, n_dma=1)
        esm.close()

        for ti in range(NT):
            for ck in range(16):
                def tr_h3(t, ti=ti, ck=ck):
                    return t.transpose(psPT[:, 0:128],
                                       h3b[:, ck * TPC + ti * 128:
                                           ck * TPC + (ti + 1) * 128],
                                       idb[:, :])
                P.step("t", tr_h3)

                def ev_h3r(v, ck=ck):
                    return v.tensor_copy(rowt[:, ck * 128:(ck + 1) * 128],
                                         psPT[:, 0:128])
                P.step("v", ev_h3r)

            def st_h3r(s, ti=ti):
                return [s.dma_start(
                    out=h3row_o[ti * 128:(ti + 1) * 128, :], in_=rowt[:, :])]
            P.step("s", st_h3r, dma=True, n_dma=1)

        for uc in range(8):
            def ld_shu(g, uc=uc):
                return ldw_chunkcol(g, wshu, wtile, uc * 128)
            P.step("g", ld_shu, dma=True, n_dma=1)

            for th in range(2):
                def mm_su(t, uc=uc, th=th):
                    last = None
                    for ck in range(16):
                        last = t.matmul(
                            ps0[:, 0:TH], wtile[:, ck * 128:(ck + 1) * 128],
                            h3b[:, ck * TPC + th * TH:ck * TPC + (th + 1) * TH],
                            start=(ck == 0), stop=(ck == 15))
                    return last
                P.step("t", mm_su)

                if True:
                    def ev_su(v, uc=uc, th=th):
                        if uc < 4:
                            return v.tensor_copy(
                                actsu[:, uc * TPC + th * TH:
                                      uc * TPC + (th + 1) * TH],
                                ps0[:, 0:TH])
                        return v.tensor_copy(
                            tmp1[:, th * TH:(th + 1) * TH], ps0[:, 0:TH])
                    P.step("v", ev_su)
                if uc >= 4:
                    def silu_gs(a, uc=uc, th=th):
                        return a.activation(out=tmp2[:, th * TH:(th + 1) * TH],
                                            in_=tmp1[:, th * TH:(th + 1) * TH],
                                            func=AF.Silu)
                    P.step("a", silu_gs)

                    def mul_su(v, uc=uc, th=th):
                        return v.tensor_tensor(
                            actsu[:, (uc - 4) * TPC + th * TH:
                                  (uc - 4) * TPC + (th + 1) * TH],
                            actsu[:, (uc - 4) * TPC + th * TH:
                                  (uc - 4) * TPC + (th + 1) * TH],
                            tmp2[:, th * TH:(th + 1) * TH], OP.mult)
                    P.step("v", mul_su)

        for ck in range(16):
            def ld_shd(g, ck=ck):
                return [g.dma_start(out=wtile[:, 0:4 * 128].rearrange(
                    "p (hk c) -> p hk c", hk=4),
                    in_=wshd[:, ck * 128:(ck + 1) * 128].rearrange(
                        "(hk p) c -> p hk c", p=128))]
            P.step("g", ld_shd, dma=True, n_dma=1)

            for th in range(2):
                def mm_shd(t, ck=ck, th=th):
                    last = None
                    for hk in range(4):
                        last = t.matmul(
                            ps0[:, 0:TH], wtile[:, hk * 128:(hk + 1) * 128],
                            actsu[:, hk * TPC + th * TH:
                                  hk * TPC + (th + 1) * TH],
                            start=(hk == 0), stop=(hk == 3))
                    return last
                P.step("t", mm_shd)

                def ev_shd(v, th=th):
                    return v.tensor_copy(tmp1[:, th * TH:(th + 1) * TH],
                                         ps0[:, 0:TH])
                P.step("v", ev_shd)

            def st_sh(s, ck=ck):
                return [s.dma_start(out=shT_o[ck * 128:(ck + 1) * 128, :],
                                    in_=tmp1[:, :])]
            P.step("s", st_sh, dma=True, n_dma=1)

        def ld_wg(g):
            return ldw_chunkcol(g, wgate, wtile, 0, ncol=64)
        P.step("g", ld_wg, dma=True, n_dma=1)

        for th in range(2):
            def mm_g(t, th=th):
                last = None
                for ck in range(16):
                    last = t.matmul(ps0[0:64, 0:TH],
                                    wtile[:, ck * 64:(ck + 1) * 64],
                                    h3b[:, ck * TPC + th * TH:
                                        ck * TPC + (th + 1) * TH],
                                    start=(ck == 0), stop=(ck == 15))
                return last
            P.step("t", mm_g)

            def ev_g(a, th=th):
                return a.activation(out=s64[:, th * TH:(th + 1) * TH],
                                    in_=ps0[0:64, 0:TH], func=AF.Sigmoid)
            P.step("a", ev_g)

        def z_off(v):
            return v.memset(offrow[:, :], 0.0)
        P.step("v", z_off)

        for ti in range(NT):
            def tr_g(t, ti=ti):
                return t.transpose(ps1[0:128, 0:64],
                                   s64[:, ti * 128:(ti + 1) * 128],
                                   idf[0:64, 0:64])
            P.step("t", tr_g)

            def ev_tg(v, ti=ti):
                return v.tensor_copy(stok[:, :], ps1[0:128, 0:64])
            P.step("v", ev_tg)

            def z_cnt(v):
                v.memset(tokcnt[:, :], 0.0)
                v.drain()
                return v.memset(kpre[:, :], 0.0)
            P.step("v", z_cnt)

            for k in range(6):
                def sel_k(v, k=k):
                    v.reduce_max(w6t[:, k:k + 1], stok[:, :], axis=AX.X)
                    v.drain()
                    v.tensor_scalar(oh6[:, k * 64:(k + 1) * 64], stok[:, :],
                                    w6t[:, k:k + 1], None, OP.is_ge)
                    v.drain()
                    v.tensor_tensor(tmp1[:, 0:64],
                                    oh6[:, k * 64:(k + 1) * 64], riota64[:, :],
                                    OP.mult)
                    v.drain()
                    v.reduce_max(small[:, 0:1], tmp1[:, 0:64], axis=AX.X)
                    v.drain()
                    v.tensor_scalar(e6t[:, k:k + 1], small[:, 0:1], -1.0,
                                    63.0, OP.mult, OP.add)
                    v.drain()
                    v.tensor_scalar(oh6[:, k * 64:(k + 1) * 64], iota64[:, :],
                                    e6t[:, k:k + 1], None, OP.is_equal)
                    v.drain()
                    v.scalar_tensor_tensor(stok[:, :],
                                           oh6[:, k * 64:(k + 1) * 64], -2.0,
                                           stok[:, :], OP.mult, OP.add)
                    v.drain()
                    return v.tensor_tensor(tokcnt[:, :], tokcnt[:, :],
                                           oh6[:, k * 64:(k + 1) * 64], OP.add)
                P.step("v", sel_k)

            def norm_w(v):
                v.reduce_sum(small[:, 1:2], w6t[:, :], axis=AX.X)
                v.drain()
                v.reciprocal(small[:, 2:3], small[:, 1:2])
                v.drain()
                return v.tensor_scalar(w6t[:, :], w6t[:, :], small[:, 2:3],
                                       None, OP.mult)
            P.step("v", norm_w)

            def mm_pre(t, ti=ti):
                t.matmul(ps1[:, 0:64], triu128[:, :], tokcnt[:, :],
                         start=True, stop=True)
                return t.matmul(ps2[:, 0:64], ones_row[:, :], offrow[:, :],
                                start=True, stop=True)
            P.step("t", mm_pre)

            def mk_ptile(v):
                v.tensor_copy(ptile[:, :], ps1[:, 0:64])
                v.drain()
                return v.tensor_tensor(ptile[:, :], ptile[:, :], ps2[:, 0:64],
                                       OP.add)
            P.step("v", mk_ptile)

            for k in range(6):
                def pos_k(v, k=k):
                    v.tensor_tensor(tmp1[:, 0:64], ptile[:, :], kpre[:, :],
                                    OP.add)
                    v.drain()
                    v.tensor_tensor(tmp1[:, 0:64], tmp1[:, 0:64],
                                    oh6[:, k * 64:(k + 1) * 64], OP.mult)
                    v.drain()
                    v.reduce_sum(p6t[:, k:k + 1], tmp1[:, 0:64], axis=AX.X)
                    v.drain()
                    return v.tensor_tensor(kpre[:, :], kpre[:, :],
                                           oh6[:, k * 64:(k + 1) * 64], OP.add)
                P.step("v", pos_k)

            def mm_tot(t, ti=ti):
                return t.matmul(ps1[0:1, 0:64], ones_col[:, :], tokcnt[:, :],
                                start=True, stop=True)
            P.step("t", mm_tot)

            def upd_off(v):
                return v.tensor_tensor(offrow[:, :], offrow[:, :],
                                       ps1[0:1, 0:64], OP.add)
            P.step("v", upd_off)

            def st_routing(s, ti=ti):
                return [s.dma_start(out=e6_o[ti * 128:(ti + 1) * 128, :],
                                    in_=e6t[:, :]),
                        s.dma_start(out=w6_o[ti * 128:(ti + 1) * 128, :],
                                    in_=w6t[:, :]),
                        s.dma_start(out=p6_o[ti * 128:(ti + 1) * 128, :],
                                    in_=p6t[:, :])]
            P.step("s", st_routing, dma=True, n_dma=3)

        def st_cnt(s):
            return [s.dma_start(out=cnt_o, in_=offrow[:, :])]
        P.step("s", st_cnt, dma=True, n_dma=1)

        P.final_wait()
    return P


def prep_l1mid_inputs(inputs, og2_full, h2T_full):
    consts = _consts_a4()
    per_core = []
    for c in range(NCORES):
        b, r = c // 2, c % 2
        d = {
            "og2s": np.ascontiguousarray(
                og2_full[b][:, r * TPC:(r + 1) * TPC]),
            "h2Ts": np.ascontiguousarray(
                h2T_full[b][:, r * TPC:(r + 1) * TPC]),
            "wproj": np.asarray(inputs["L1_attn_w_proj"], dtype=np.float32),
            "wgate": np.asarray(inputs["L1_ffn_w_gate"], dtype=np.float32),
            "wshu": np.asarray(inputs["L1_ffn_w_shared_up"], dtype=np.float32),
            "wshd": np.asarray(inputs["L1_ffn_w_shared_down"],
                               dtype=np.float32),
        }
        for k, v in consts.items():
            d["c_" + k] = v
        per_core.append(d)
    return per_core



def build_l1route():
    P = Prog("l1route")
    nc = P.nc
    NTOK = NCORES * TPC
    NTI = TPC // 128
    h3rf = nc.dram_tensor("h3rf", [NTOK, 2048], BF16, kind="ExternalInput").ap()
    e6f = nc.dram_tensor("e6f", [NTOK, 6], F32, kind="ExternalInput").ap()
    p6f = nc.dram_tensor("p6f", [NTOK, 6], F32, kind="ExternalInput").ap()
    cntf = nc.dram_tensor("cntf", [8, 64], F32, kind="ExternalInput").ap()
    wup = nc.dram_tensor("wup", [8 * 2048, 512], F32, kind="ExternalInput").ap()
    wdn = nc.dram_tensor("wdn", [8 * 256, 2048], F32, kind="ExternalInput").ap()
    expoff = nc.dram_tensor("expoff", [128, 64], F32, kind="ExternalInput").ap()
    tria = nc.dram_tensor("tria", [8, 8], F32, kind="ExternalInput").ap()
    tokid = nc.dram_tensor("tokid", [128, 1], F32, kind="ExternalInput").ap()
    cid8 = nc.dram_tensor("cid8", [128, 1], F32, kind="ExternalInput").ap()
    cst = {k: nc.dram_tensor("c_" + k, list(v.shape), F32,
                             kind="ExternalInput").ap()
           for k, v in _consts_a4().items()}
    outr_o = nc.dram_tensor("outr", [8 * 1024, 2048], BF16,
                            kind="ExternalOutput").ap()
    tbl = nc.dram_tensor("tbl", [8576, 1], F32).ap()
    offscr = nc.dram_tensor("offscr", [1024, 1], F32).ap()

    iota64 = sbuf(nc, "iota64_sb", (128, 64), F32)
    ones_row = sbuf(nc, "onesr_sb", (1, 128), F32)
    tri8 = sbuf(nc, "tri8_sb", (8, 8), F32)
    cnts = sbuf(nc, "cnts", (8, 64), F32)
    offs = sbuf(nc, "offs", (8, 64), F32)
    offs2 = sbuf(nc, "offs2", (8, 64), F32)
    expob = sbuf(nc, "expob", (128, 64), F32)
    e6all = sbuf(nc, "e6all", (128, 8 * NTI * 6), F32)
    e6loc = sbuf(nc, "e6loc", (128, 8 * NTI * 6), F32)
    p6all = sbuf(nc, "p6all", (128, 8 * NTI * 6), F32)
    offbL = sbuf(nc, "offbL", (128, 64), F32)
    offrow2 = sbuf(nc, "offrow2", (1, 128), F32)
    oh = sbuf(nc, "oh", (128, 64), F32)
    iota1024 = sbuf(nc, "iota1024", (128, 1024), F32)
    iota8 = sbuf(nc, "iota8", (128, 8), F32)
    cids = sbuf(nc, "cids", (128, 1), F32)
    dloc6 = [sbuf(nc, "dloc6%d" % i, (128, 6), F32) for i in range(2)]
    lhs8 = [sbuf(nc, "lhs8%d" % i, (128, 8), F32) for i in range(2)]
    rhs1k = [sbuf(nc, "rhs1k%d" % i, (128, 1024), F32) for i in range(2)]
    tmp = sbuf(nc, "tmp", (128, 64), F32)
    tblsb = sbuf(nc, "tblsb", (8, 1024), F32)
    tokv6 = [sbuf(nc, "tokv6%d" % i, (128, 1), F32) for i in range(2)]
    tokc = sbuf(nc, "tokc", (128, 1), F32)
    zrow = sbuf(nc, "zrow", (128, 67), F32)
    idb = sbuf(nc, "idb_sb", (128, 128), BF16)
    idxi = [sbuf(nc, "idxi0", (128, 8), I32), sbuf(nc, "idxi1", (128, 8), I32)]
    idxf = [sbuf(nc, "idxf0", (128, 8), F32), sbuf(nc, "idxf1", (128, 8), F32)]
    xg = [sbuf(nc, "xg0", (128, 2048), BF16),
          sbuf(nc, "xg1", (128, 2048), BF16)]
    xdT = [sbuf(nc, "xdT0", (128, 16 * 1024), BF16),
           sbuf(nc, "xdT1", (128, 16 * 1024), BF16)]
    wub = [sbuf(nc, "wub0", (128, 16 * 512), BF16),
           sbuf(nc, "wub1", (128, 16 * 512), BF16)]
    wdb = [sbuf(nc, "wdb0", (128, 2 * 2048), BF16),
           sbuf(nc, "wdb1", (128, 2 * 2048), BF16)]
    wsu = [sbuf(nc, "wsu0", (128, 4 * 512), F32),
           sbuf(nc, "wsu1", (128, 4 * 512), F32)]
    gbuf = sbuf(nc, "gbuf", (128, 2 * 1024), F32)
    hbuf = sbuf(nc, "hbuf", (128, 2 * 1024), BF16)
    orow = sbuf(nc, "orow", (128, 2048), BF16)

    ps0 = psum(nc, "ps0")
    ps1 = psum(nc, "ps1")
    ps2 = psum(nc, "ps2")
    ps3 = psum(nc, "ps3")
    psT0 = psum(nc, "psT0", BF16, cols=128)
    psT1 = psum(nc, "psT1", BF16, cols=128)
    psT = [psT0, psT1]

    with P.block():
        def ld_c(g):
            return [g.dma_start(out=iota64[:, :], in_=cst["iota64"]),
                    g.dma_start(out=ones_row[:, :], in_=cst["ones_row"]),
                    g.dma_start(out=tri8[:, :], in_=tria),
                    g.dma_start(out=cnts[:, :], in_=cntf),
                    g.dma_start(out=expob[:, :], in_=expoff),
                    g.dma_start(out=tokc[:, :], in_=tokid),
                    g.dma_start(out=iota1024[:, :], in_=cst["iota1024"]),
                    g.dma_start(out=iota8[:, :], in_=cst["iota8"]),
                    g.dma_start(out=cids[:, :], in_=cid8),
                    g.dma_start(out=idb[:, :], in_=cst["idf"])]
        t_ldc = P.step("g", ld_c, dma=True, n_dma=10, sem_name="ldc")

        def mm_off(t):
            return t.matmul(ps0[0:8, 0:64], tri8[:, :], cnts[:, :],
                            start=True, stop=True)
        t_mo = P.step("t", mm_off, waits=(t_ldc,))

        def ev_off(v):
            return v.tensor_copy(offs[:, :], ps0[0:8, 0:64])
        t_eo = P.step("v", ev_off, waits=(t_mo,))

        def st_off(s):
            return [s.dma_start(out=offscr[0:512, :].rearrange(
                "(a b) one -> a (b one)", a=8), in_=offs[:, :])]
        t_so = P.step("s", st_off, dma=True, n_dma=1, waits=(t_eo,),
                      sem_name="sof")

        def ld_rt_all(g):
            return [g.dma_start(out=e6all[:, :].rearrange(
                        "p (t k) -> p t k", k=6),
                    in_=e6f.rearrange("(t p) k -> p t k", p=128)),
                    g.dma_start(out=p6all[:, :].rearrange(
                        "p (t k) -> p t k", k=6),
                    in_=p6f.rearrange("(t p) k -> p t k", p=128))]
        t_rt = P.step("g", ld_rt_all, dma=True, n_dma=2, waits=(),
                      sem_name="rta")

        def mk_eloc(v):
            return v.tensor_scalar(e6loc[:, :], e6all[:, :], cids[:, 0:1],
                                   None, OP.subtract)
        t_el = P.step("v", mk_eloc, waits=(t_rt, t_ldc))

        t_mm = [None, None]
        t_dve = [None, None]
        t_bc_prev = None
        first_mm = [True]
        gti = [0]
        for cb in range(8):
            def ld_offr(g, cb=cb):
                return [g.dma_start(out=offrow2[:, 0:64],
                                    in_=offscr[cb * 64:(cb + 1) * 64, :])]
            t_lo = P.step("g", ld_offr, dma=True, n_dma=1,
                          waits=(t_so, t_bc_prev, t_dve[0], t_dve[1]),
                          sem_name="lof")

            def bc_off(t):
                return t.matmul(ps0[:, 0:64], ones_row[:, :],
                                offrow2[:, 0:64], start=True, stop=True)
            t_bc = P.step("t", bc_off, waits=(t_lo, t_bc_prev))

            def ev_offb(v):
                return v.tensor_copy(offbL[:, :], ps0[:, 0:64])
            t_bc_prev = P.step("v", ev_offb, waits=(t_bc,))

            for ti in range(NTI):
                tix = gti[0]
                gti[0] += 1
                for k in range(6):
                    kp = k % 2

                    def bld(v, cb=cb, ti=ti, k=k, kp=kp, tix=tix):
                        col = tix * 6 + k
                        v.tensor_scalar(oh[:, :], iota64[:, :],
                                        e6all[:, col:col + 1], None,
                                        OP.is_equal)
                        v.drain()
                        v.tensor_tensor(tmp[:, :], oh[:, :], offbL[:, :],
                                        OP.mult)
                        v.drain()
                        v.reduce_sum(dloc6[kp][:, 0:1], tmp[:, :], axis=AX.X)
                        v.drain()
                        v.tensor_tensor(dloc6[kp][:, 0:1], dloc6[kp][:, 0:1],
                                        p6all[:, col:col + 1], OP.add)
                        v.drain()
                        v.tensor_scalar(rhs1k[kp][:, :], iota1024[:, :],
                                        dloc6[kp][:, 0:1], None, OP.is_equal)
                        v.drain()
                        v.tensor_scalar(oh[:, 0:8], iota8[:, :],
                                        e6loc[:, col:col + 1], None,
                                        OP.is_equal)
                        v.drain()
                        v.tensor_scalar(tokv6[kp][:, 0:1], tokc[:, 0:1], 1.0,
                                        float(cb * TPC + ti * 128),
                                        OP.mult, OP.add)
                        v.drain()
                        return v.tensor_scalar(lhs8[kp][:, :], oh[:, 0:8],
                                               tokv6[kp][:, 0:1], None,
                                               OP.mult)
                    t_dve[kp] = P.step("v", bld,
                                       waits=(t_el, t_bc_prev, t_mm[kp]))

                    last = (tix == 8 * NTI - 1 and k == 5)

                    def mm_tbl(t, kp=kp, last=last):
                        st = first_mm[0]
                        first_mm[0] = False
                        t.matmul(ps1[0:8, 0:512], lhs8[kp][:, :],
                                 rhs1k[kp][:, 0:512], start=st, stop=last)
                        return t.matmul(ps2[0:8, 0:512], lhs8[kp][:, :],
                                        rhs1k[kp][:, 512:1024],
                                        start=st, stop=last)
                    t_mm[kp] = P.step("t", mm_tbl, waits=(t_dve[kp],))

        def ev_tbl(v):
            v.tensor_copy(tblsb[:, 0:512], ps1[0:8, 0:512])
            v.drain()
            return v.tensor_copy(tblsb[:, 512:1024], ps2[0:8, 0:512])
        t_evt = P.step("v", ev_tbl, waits=(t_mm[0], t_mm[1]))

        def st_tbl(s):
            return [s.dma_start(out=tbl[0:8192, :].rearrange(
                "(a b) one -> a (b one)", a=8), in_=tblsb[:, :])]
        t_stbl = P.step("s", st_tbl, dma=True, n_dma=1, waits=(t_evt,),
                        sem_name="stb")
        t_scat = (t_stbl,)

        tok_cvt = [None, None]
        tok_xrd = [None, None]
        tok_wld = [None, None]
        tok_wcast = [None, None]
        tok_mmup_done = [None, None]
        tok_dn_done = [None, None]
        t_store_prev = None
        prev_tr = {0: None, 1: None}
        t_evT = [None, None]
        t_cast_q = [None, None]

        for e in range(8):
            par = e % 2

            def ld_tbl(g, e=e, par=par):
                return [g.dma_start(
                    out=idxf[par][:, rt:rt + 1],
                    in_=tbl[e * 1024 + rt * 128:e * 1024 + (rt + 1) * 128, :])
                    for rt in range(8)]
            t_ldt = P.step("g", ld_tbl, dma=True, n_dma=8,
                           waits=t_scat + (tok_cvt[par],),
                           sem_name="ix%d" % par)

            def cvt_idx(v, par=par):
                return v.tensor_copy(idxi[par][:, :], idxf[par][:, :])
            tok_cvt[par] = P.step("v", cvt_idx, waits=(t_ldt,))

            for qd in range(4):
                qp = qd % 2

                def ld_wq(s, e=e, qd=qd, qp=qp):
                    return [s.dma_start(
                        out=wsu[qp][:, :].rearrange("p (ck c) -> p ck c",
                                                    ck=4),
                        in_=wup[e * 2048 + qd * 512:
                                e * 2048 + (qd + 1) * 512, :].rearrange(
                            "(ck p) c -> p ck c", p=128))]
                t_lq = P.step("s", ld_wq, dma=True, n_dma=1,
                              waits=(t_cast_q[qp],), sem_name="wq%d" % qp)

                def cast_wq(v, par=par, qd=qd, qp=qp):
                    return v.tensor_copy(
                        wub[par][:, qd * 2048:(qd + 1) * 2048],
                        wsu[qp][:, 0:2048])
                t_cast_q[qp] = P.step(
                    "v", cast_wq,
                    waits=(t_lq,) +
                    ((tok_mmup_done[par],) if qd == 0 else ()))
            tok_wcast[par] = t_cast_q[(4 - 1) % 2]
            tok_wcast_all = (t_cast_q[0], t_cast_q[1])

            t_ev_last = None
            for rt in range(8):
                xp = rt % 2

                def gath(g, rt=rt, par=par, xp=xp):
                    return [g.indirect_dma_start(
                        out=xg[xp][:, :], out_offset=None, in_=h3rf,
                        in_offset=IOA(idxi[par][:, rt:rt + 1], 0))]
                t_g = P.step("g", gath, dma=True, n_dma=1,
                             waits=(tok_cvt[par], prev_tr[xp]),
                             sem_name="xg%d" % xp)

                for fb in range(16):
                    tp = fb % 2

                    def tr_x(t, rt=rt, fb=fb, xp=xp, tp=tp):
                        return t.transpose(psT[tp][:, 0:128],
                                           xg[xp][:, fb * 128:(fb + 1) * 128],
                                           idb[:, :])
                    t_tr = P.step("t", tr_x, waits=(t_g, t_evT[tp]))

                    def ev_x(v, rt=rt, fb=fb, par=par, tp=tp):
                        return v.tensor_copy(
                            xdT[par][:, fb * 1024 + rt * 128:
                                     fb * 1024 + (rt + 1) * 128],
                            psT[tp][:, 0:128])
                    t_evT[tp] = P.step(
                        "v", ev_x,
                        waits=(t_tr,) +
                        ((tok_mmup_done[par],) if rt == 0 and fb < 2 else ()))
                prev_tr[xp] = t_tr
            tok_xrd[par] = (t_evT[0], t_evT[1])

            def ld_wd(g, e=e, par=par):
                return [g.dma_start(
                    out=wdb[par][:, :].rearrange("p (hk c) -> p hk c", hk=2),
                    in_=wdn[e * 256:(e + 1) * 256, :].rearrange(
                        "(hk p) c -> p hk c", p=128))]
            tok_wld[par] = P.step("g", ld_wd, dma=True, n_dma=1,
                                  waits=(tok_dn_done[par],),
                                  sem_name="wl%d" % par)

            ev_toks = []
            mm_toks = []
            for hc in range(4):
                for th in range(2):
                    i = hc * 2 + th
                    bank = (ps0, ps1)[i % 2]

                    def mm_up(t, hc=hc, th=th, par=par, bank=bank):
                        last = None
                        for ck in range(16):
                            last = t.matmul(
                                bank[:, 0:512],
                                wub[par][:, ck * 512 + hc * 128:
                                         ck * 512 + (hc + 1) * 128],
                                xdT[par][:, ck * 1024 + th * 512:
                                         ck * 1024 + (th + 1) * 512],
                                start=(ck == 0), stop=(ck == 15))
                        return last
                    w_ = [tok_xrd[par][0], tok_xrd[par][1],
                          tok_wcast_all[0], tok_wcast_all[1]]
                    if len(ev_toks) >= 2:
                        w_.append(ev_toks[-2])
                    tok_mm = P.step("t", mm_up, waits=tuple(w_))
                    mm_toks.append(tok_mm)

                    if hc < 2:
                        def ev_up(a, hc=hc, th=th, bank=bank):
                            return a.activation(
                                out=gbuf[:, hc * 1024 + th * 512:
                                         hc * 1024 + (th + 1) * 512],
                                in_=bank[:, 0:512], func=AF.Silu)
                        ev_toks.append(P.step("a", ev_up, waits=(tok_mm,)))
                    else:
                        def ev_up2(v, hc=hc, th=th, bank=bank):
                            return v.tensor_tensor(
                                hbuf[:, (hc - 2) * 1024 + th * 512:
                                     (hc - 2) * 1024 + (th + 1) * 512],
                                gbuf[:, (hc - 2) * 1024 + th * 512:
                                     (hc - 2) * 1024 + (th + 1) * 512],
                                bank[:, 0:512], OP.mult)
                        ev_toks.append(P.step("v", ev_up2,
                                              waits=(tok_mm,
                                                     ev_toks[i - 4])))
            tok_mmup_done[par] = mm_toks[-1]

            dn_ev = []
            for tc in range(8):
                for dm4 in range(4):
                    i = tc * 4 + dm4
                    bank = (ps2, ps3)[i % 2]

                    def mm_dn(t, tc=tc, dm4=dm4, par=par, bank=bank):
                        t.matmul(bank[:, 0:512],
                                 hbuf[:, tc * 128:(tc + 1) * 128],
                                 wdb[par][:, dm4 * 512:(dm4 + 1) * 512],
                                 start=True, stop=False)
                        return t.matmul(
                            bank[:, 0:512],
                            hbuf[:, 1024 + tc * 128:1024 + (tc + 1) * 128],
                            wdb[par][:, 2048 + dm4 * 512:
                                     2048 + (dm4 + 1) * 512],
                            start=False, stop=True)
                    w_ = [ev_toks[-1], ev_toks[-2], tok_wld[par]]
                    if len(dn_ev) >= 2:
                        w_.append(dn_ev[-2])
                    tok_md = P.step("t", mm_dn, waits=tuple(w_))

                    def ev_dn(v, tc=tc, dm4=dm4, bank=bank):
                        return v.tensor_copy(
                            orow[:, dm4 * 512:(dm4 + 1) * 512],
                            bank[:, 0:512])
                    dn_ev.append(P.step("v", ev_dn,
                                        waits=(tok_md, t_store_prev)))

                def st_or(a, e=e, tc=tc):
                    return [a.dma_start(
                        out=outr_o[e * 1024 + tc * 128:
                                   e * 1024 + (tc + 1) * 128, :],
                        in_=orow[:, :])]
                t_store_prev = P.step("a", st_or, dma=True, n_dma=1,
                                      waits=tuple(dn_ev[-4:]),
                                      sem_name="sor")
            tok_dn_done[par] = dn_ev[-1]

        P.final_wait()
    return P


def prep_l1route_inputs(inputs, h3row_full, e6_full, p6_full, cnt_full):
    consts = _consts_a4()
    wu = np.asarray(inputs["L1_ffn_w_up"], dtype=np.float32)
    wd = np.asarray(inputs["L1_ffn_w_down"], dtype=np.float32)
    tria = np.triu(np.ones((8, 8), np.float32), 1)
    per_core = []
    for c in range(NCORES):
        eo = np.full((128, 64), 1e9, np.float32)
        for e in range(8 * c, 8 * c + 8):
            eo[:, e] = (e - 8 * c) * 1024.0
        d = {
            "h3rf": h3row_full,
            "e6f": e6_full, "p6f": p6_full, "cntf": cnt_full,
            "wup": np.ascontiguousarray(
                wu[8 * c:8 * c + 8].reshape(8 * 2048, 512)),
            "wdn": np.ascontiguousarray(
                wd[8 * c:8 * c + 8].reshape(8 * 256, 2048)),
            "expoff": eo,
            "tria": tria,
            "tokid": np.arange(128, dtype=np.float32).reshape(128, 1),
            "cid8": np.full((128, 1), 8.0 * c, dtype=np.float32),
        }
        for k, v in consts.items():
            d["c_" + k] = v
        per_core.append(d)
    return per_core



def build_final():
    P = Prog("final")
    nc = P.nc
    outrf = nc.dram_tensor("outrf", [64 * 1024, 2048], BF16,
                           kind="ExternalInput").ap()
    h3Ts = nc.dram_tensor("h3Ts", [2048, TPC], F32, kind="ExternalInput").ap()
    shTs = nc.dram_tensor("shTs", [2048, TPC], F32, kind="ExternalInput").ap()
    e6s = nc.dram_tensor("e6s", [TPC, 6], F32, kind="ExternalInput").ap()
    p6s = nc.dram_tensor("p6s", [TPC, 6], F32, kind="ExternalInput").ap()
    w6s = nc.dram_tensor("w6s", [TPC, 6], F32, kind="ExternalInput").ap()
    cntf = nc.dram_tensor("cntf", [8, 64], F32, kind="ExternalInput").ap()
    selc = nc.dram_tensor("selc", [8, 1], F32, kind="ExternalInput").ap()
    cst = {k: nc.dram_tensor("c_" + k, list(v.shape), F32,
                             kind="ExternalInput").ap()
           for k, v in _consts_a4().items()}
    out_o = nc.dram_tensor("out", [TPC, 2048], F32, kind="ExternalOutput").ap()

    iota64 = sbuf(nc, "iota64_sb", (128, 64), F32)
    ones_row = sbuf(nc, "onesr_sb", (1, 128), F32)
    idf = sbuf(nc, "idf_sb", (128, 128), F32)
    tri8 = sbuf(nc, "tri8_sb", (8, 8), F32)
    selcs = sbuf(nc, "selcs", (8, 1), F32)
    cnts = sbuf(nc, "cnts", (8, 64), F32)
    offs = sbuf(nc, "offs", (8, 64), F32)
    offb = sbuf(nc, "offb", (128, 64), F32)
    e6t = sbuf(nc, "e6t", (128, 6), F32)
    p6t = sbuf(nc, "p6t", (128, 6), F32)
    w6t = sbuf(nc, "w6t", (128, 6), F32)
    offrow1 = sbuf(nc, "offrow1", (1, 64), F32)
    tmp = sbuf(nc, "tmp", (128, 64), F32)
    oh = sbuf(nc, "oh", (128, 64), F32)
    dcol = sbuf(nc, "dcol", (128, 1), F32)
    dcoli = sbuf(nc, "dcoli", (128, 1), I32)
    vcol = sbuf(nc, "vcol", (128, 1), F32)
    racc = sbuf(nc, "racc", (128, 2048), F32)
    gbuf = sbuf(nc, "gbuf", (128, 2048), BF16)
    fsl = sbuf(nc, "fsl", (128, 128), F32)
    stat = sbuf(nc, "stat", (128, 8), F32)
    sq = sbuf(nc, "sq", (128, 2048), F32)
    tria_in = nc.dram_tensor("tria", [8, 8], F32, kind="ExternalInput").ap()

    ps0 = psum(nc, "ps0")
    ps1 = psum(nc, "ps1")

    NT = TPC // 128

    with P.block():
        def ld_c(g):
            return [g.dma_start(out=iota64[:, :], in_=cst["iota64"]),
                    g.dma_start(out=ones_row[:, :], in_=cst["ones_row"]),
                    g.dma_start(out=idf[:, :], in_=cst["idf"]),
                    g.dma_start(out=tri8[:, :], in_=tria_in),
                    g.dma_start(out=cnts[:, :], in_=cntf),
                    g.dma_start(out=selcs[:, :], in_=selc)]
        P.step("g", ld_c, dma=True, n_dma=6)

        def mm_off(t):
            return t.matmul(ps0[0:8, 0:64], tri8[:, :], cnts[:, :],
                            start=True, stop=True)
        P.step("t", mm_off)

        def ev_off(v):
            return v.tensor_copy(offs[:, :], ps0[0:8, 0:64])
        P.step("v", ev_off)

        def mm_my(t):
            return t.matmul(ps1[0:1, 0:64], selcs[:, :], offs[:, :],
                            start=True, stop=True)
        P.step("t", mm_my)

        def ev_my(v):
            return v.tensor_copy(offrow1[:, :], ps1[0:1, 0:64])
        P.step("v", ev_my)

        for ti in range(NT):
            def ld_rt(g, ti=ti):
                return [g.dma_start(out=e6t[:, :],
                                    in_=e6s[ti * 128:(ti + 1) * 128, :]),
                        g.dma_start(out=p6t[:, :],
                                    in_=p6s[ti * 128:(ti + 1) * 128, :]),
                        g.dma_start(out=w6t[:, :],
                                    in_=w6s[ti * 128:(ti + 1) * 128, :])]
            P.step("g", ld_rt, dma=True, n_dma=3)

            if ti == 0:
                def bc_off(t):
                    return t.matmul(ps1[:, 0:64], ones_row[:, :],
                                    offrow1[:, :], start=True, stop=True)
                P.step("t", bc_off)

                def ev_offb(v):
                    return v.tensor_copy(offb[:, :], ps1[:, 0:64])
                P.step("v", ev_offb)

            def z_racc(v):
                return v.memset(racc[:, :], 0.0)
            P.step("v", z_racc)

            for k in range(6):
                def mkidx(v, k=k):
                    v.tensor_scalar(oh[:, :], iota64[:, :], e6t[:, k:k + 1],
                                    None, OP.is_equal)
                    v.drain()
                    v.tensor_tensor(tmp[:, :], oh[:, :], offb[:, :], OP.mult)
                    v.drain()
                    v.reduce_sum(dcol[:, 0:1], tmp[:, :], axis=AX.X)
                    v.drain()
                    v.tensor_tensor(dcol[:, 0:1], dcol[:, 0:1],
                                    p6t[:, k:k + 1], OP.add)
                    v.drain()
                    v.tensor_scalar(vcol[:, 0:1], dcol[:, 0:1], 1024.0, None,
                                    OP.is_lt)
                    v.drain()
                    v.tensor_tensor(vcol[:, 0:1], vcol[:, 0:1],
                                    w6t[:, k:k + 1], OP.mult)
                    v.drain()
                    v.tensor_scalar(dcol[:, 0:1], dcol[:, 0:1], 1023.0, None,
                                    OP.min)
                    v.drain()
                    v.tensor_scalar(stat[:, 0:1], e6t[:, k:k + 1], 1024.0,
                                    None, OP.mult)
                    v.drain()
                    v.tensor_tensor(dcol[:, 0:1], dcol[:, 0:1], stat[:, 0:1],
                                    OP.add)
                    v.drain()
                    return v.tensor_copy(dcoli[:, :], dcol[:, :])
                P.step("v", mkidx)

                def gath(g):
                    return [g.indirect_dma_start(
                        out=gbuf[:, :], out_offset=None, in_=outrf,
                        in_offset=IOA(dcoli[:, 0:1], 0))]
                P.step("g", gath, dma=True, n_dma=1)

                def acc_r(v, k=k):
                    return v.scalar_tensor_tensor(racc[:, :], gbuf[:, :],
                                                  vcol[:, 0:1], racc[:, :],
                                                  OP.mult, OP.add)
                P.step("v", acc_r)

            for ck in range(16):
                for which, srct in (("sh", shTs), ("h3", h3Ts)):
                    def ld_f(g, ck=ck, ti=ti, srct=srct):
                        return [g.dma_start(
                            out=fsl[:, :],
                            in_=srct[ck * 128:(ck + 1) * 128,
                                     ti * 128:(ti + 1) * 128])]
                    P.step("g", ld_f, dma=True, n_dma=1)

                    def tr_f(t):
                        return t.transpose(ps0[:, 0:128], fsl[:, :],
                                           idf[:, :])
                    P.step("t", tr_f)

                    def add_f(v, ck=ck):
                        return v.tensor_tensor(
                            racc[:, ck * 128:(ck + 1) * 128],
                            racc[:, ck * 128:(ck + 1) * 128],
                            ps0[:, 0:128], OP.add)
                    P.step("v", add_f)

            def ln_f(v):
                v.reduce_sum(stat[:, 1:2], racc[:, :], axis=AX.X)
                v.drain()
                return v.tensor_scalar(stat[:, 1:2], stat[:, 1:2],
                                       1.0 / 2048.0, None, OP.mult)
            P.step("v", ln_f)

            def sq_f(a):
                return a.activation(out=sq[:, :], in_=racc[:, :],
                                    func=AF.Square)
            P.step("a", sq_f)

            def var_f(v):
                v.reduce_sum(stat[:, 2:3], sq[:, :], axis=AX.X)
                v.drain()
                v.tensor_scalar(stat[:, 2:3], stat[:, 2:3], 1.0 / 2048.0,
                                None, OP.mult)
                v.drain()
                v.tensor_tensor(stat[:, 3:4], stat[:, 1:2], stat[:, 1:2],
                                OP.mult)
                v.drain()
                v.tensor_tensor(stat[:, 2:3], stat[:, 2:3], stat[:, 3:4],
                                OP.subtract)
                v.drain()
                return v.tensor_scalar(stat[:, 2:3], stat[:, 2:3], 1e-5,
                                       None, OP.add)
            P.step("v", var_f)

            def sqrt_f(a):
                return a.activation(out=stat[:, 4:5], in_=stat[:, 2:3],
                                    func=AF.Sqrt)
            P.step("a", sqrt_f)

            def fin_f(v):
                v.reciprocal(stat[:, 5:6], stat[:, 4:5])
                v.drain()
                v.tensor_scalar(stat[:, 6:7], stat[:, 1:2], -1.0, None,
                                OP.mult)
                v.drain()
                return v.tensor_scalar(racc[:, :], racc[:, :], stat[:, 6:7],
                                       stat[:, 5:6], OP.add, OP.mult)
            P.step("v", fin_f)

            def st_out(s, ti=ti):
                return [s.dma_start(out=out_o[ti * 128:(ti + 1) * 128, :],
                                    in_=racc[:, :])]
            P.step("s", st_out, dma=True, n_dma=1)

        P.final_wait()
    return P


def prep_final_inputs(inputs, outr_full, h3T_slices, shT_slices, e6_slices,
                      p6_slices, w6_slices, cnt_full):
    consts = _consts_a4()
    tria = np.triu(np.ones((8, 8), np.float32), 1)
    per_core = []
    for c in range(NCORES):
        d = {
            "outrf": outr_full,
            "h3Ts": h3T_slices[c], "shTs": shT_slices[c],
            "e6s": e6_slices[c], "p6s": p6_slices[c], "w6s": w6_slices[c],
            "cntf": cnt_full,
            "selc": np.eye(8, dtype=np.float32)[:, c:c + 1].copy(),
            "tria": tria,
        }
        for k, v in consts.items():
            d["c_" + k] = v
        per_core.append(d)
    return per_core


_BUILD_CACHE = {}


def _run(builder, per_core):
    if builder not in _BUILD_CACHE:
        _BUILD_CACHE[builder] = builder()
    P = _BUILD_CACHE[builder]
    res = run_bass_kernel_spmd(P.nc, per_core, core_ids=list(range(NCORES)))
    return res.results


def kernel(**inputs):
    r1 = _run(build_l0attn, prep_l0attn_inputs(inputs))
    og_full = [np.concatenate([np.asarray(r1[2 * b]["og"]),
                               np.asarray(r1[2 * b + 1]["og"])], axis=0)
               for b in range(4)]
    r2 = _run(build_l0rest, prep_l0rest_inputs(inputs, og_full))
    h2T_full = [np.concatenate([np.asarray(r2[2 * b]["h2T"]),
                                np.asarray(r2[2 * b + 1]["h2T"])], axis=1)
                for b in range(4)]
    ckvT_full = [np.concatenate([np.asarray(r2[2 * b]["ckvT"]),
                                 np.asarray(r2[2 * b + 1]["ckvT"])], axis=1)
                 for b in range(4)]
    r3 = _run(build_l1attn, prep_l1attn_inputs(inputs, h2T_full, ckvT_full))
    og2_full = [np.concatenate([np.asarray(r3[2 * b]["og2"]),
                                np.asarray(r3[2 * b + 1]["og2"])], axis=0)
                for b in range(4)]
    r4 = _run(build_l1mid, prep_l1mid_inputs(inputs, og2_full, h2T_full))
    h3row_full = np.concatenate([np.asarray(r4[c]["h3row"])
                                 for c in range(NCORES)], axis=0)
    e6_full = np.concatenate([np.asarray(r4[c]["e6"]) for c in range(NCORES)],
                             axis=0)
    p6_full = np.concatenate([np.asarray(r4[c]["p6"]) for c in range(NCORES)],
                             axis=0)
    cnt_full = np.concatenate([np.asarray(r4[c]["cnt"])
                               for c in range(NCORES)], axis=0)
    r5 = _run(build_l1route,
              prep_l1route_inputs(inputs, h3row_full, e6_full, p6_full,
                                  cnt_full))
    outr_full = np.concatenate([np.asarray(r5[c]["outr"])
                                for c in range(NCORES)], axis=0)
    h3T_slices = [np.asarray(r4[c]["h3T"]) for c in range(NCORES)]
    shT_slices = [np.asarray(r4[c]["shT"]) for c in range(NCORES)]
    e6_slices = [np.asarray(r4[c]["e6"]) for c in range(NCORES)]
    p6_slices = [np.asarray(r4[c]["p6"]) for c in range(NCORES)]
    w6_slices = [np.asarray(r4[c]["w6"]) for c in range(NCORES)]
    r6 = _run(build_final,
              prep_final_inputs(inputs, outr_full, h3T_slices, shT_slices,
                                e6_slices, p6_slices, w6_slices, cnt_full))
    out = np.zeros((4, T, DM), np.float32)
    for c in range(NCORES):
        b, r = c // 2, c % 2
        out[b, r * TPC:(r + 1) * TPC, :] = np.asarray(r6[c]["out"],
                                                      np.float32)
    return out
